# version 08
# baseline (speedup 1.0000x reference)
import numpy as np
from contextlib import ExitStack

import concourse.bass as bass
import concourse.mybir as mybir
from concourse.bass_utils import run_bass_kernel_spmd

F32 = mybir.dt.float32
F16 = mybir.dt.float16
U8 = mybir.dt.uint8
I8 = mybir.dt.int8
I32 = mybir.dt.int32
ALU = mybir.AluOpType
ACT = mybir.ActivationFunctionType
AX = mybir.AxisListType.X

NCORES = 8
S = 4096
D = 1024
FF = 4096
T = 2 * S
NST = T // 512
SPB = S // 512
EPS_LN = 1e-5
EPS_Q = 1e-8
NEG = -60000.0


def _sgroups():
    out = []
    sgidx = 0
    for st in range(NST):
        sl = st % SPB
        rows = []
        for tt in range(4):
            qt = sl * 4 + tt
            gf = (qt * 128 + 511) // 512
            for g in range(gf):
                ng = min(512, qt * 128 - g * 512)
                rows.append((tt, g, ng, False, 4 + (sgidx % 4), sgidx))
                sgidx += 1
            rows.append((tt, gf, 128, True, 4 + (sgidx % 4), sgidx))
            sgidx += 1
        out.append(rows)
    return out


SGROUPS = _sgroups()
BUILD_PHASES = 6
B_STAGES = 4
B_HMAX = True
B_SIMPLE_EVAC = False
SKIP_A = False
SKIP_AR1 = False
SKIP_P = False
DEBUG_TAPS = False


def build_nc():
    nc = bass.Bass("TRN2", target_bir_lowering=False,
                   detect_race_conditions=False)

    x_h = nc.dram_tensor("x", [T, D], F32, kind="ExternalInput")
    wq_h = nc.dram_tensor("wq", [D, D], F32, kind="ExternalInput")
    wk_h = nc.dram_tensor("wk", [D, D], F32, kind="ExternalInput")
    wv_h = nc.dram_tensor("wv", [D, D], F32, kind="ExternalInput")
    bq_h = nc.dram_tensor("bqr", [128, 8], F32, kind="ExternalInput")
    bk_h = nc.dram_tensor("bkr", [128, 8], F32, kind="ExternalInput")
    bv_h = nc.dram_tensor("bvr", [1, D], F32, kind="ExternalInput")
    w1_h = nc.dram_tensor("w1", [FF, D], F32, kind="ExternalInput")
    b1_h = nc.dram_tensor("b1r", [128, 32], F32, kind="ExternalInput")
    w2_h = nc.dram_tensor("w2", [D, FF], F32, kind="ExternalInput")
    b2_h = nc.dram_tensor("b2r", [1, D], F32, kind="ExternalInput")
    mask_h = nc.dram_tensor("mask", [128, 128], F16, kind="ExternalInput")
    cnt_h = nc.dram_tensor("cnt", [128, 32], F32, kind="ExternalInput")
    ones_h = nc.dram_tensor("onesf", [1, 128], F32, kind="ExternalInput")
    out_h = nc.dram_tensor("out", [T, D], F32, kind="ExternalOutput")

    KI = "ExternalOutput" if DEBUG_TAPS else "Internal"
    dbg_h = nc.dram_tensor("dbg", [128, 64], F32, kind=KI)
    y16_h = nc.dram_tensor("y16", [T, D], F16, kind=KI)
    o16_h = nc.dram_tensor("o16", [T, D], F16, kind=KI)
    w1q_h = nc.dram_tensor("w1qs", [FF, D], F16, kind=KI)
    w2q_h = nc.dram_tensor("w2qs", [D, FF], F16, kind=KI)
    h16_h = nc.dram_tensor("h16", [32, 16, 128, 512], F16, kind=KI)
    red1_h = nc.dram_tensor("red1", [1, 128], F32, kind="Internal")
    red2_h = nc.dram_tensor("red2", [1, 128], F32, kind="Internal")
    sw2d_h = nc.dram_tensor("sw2d", [1, D], F32, kind=KI)
    cc1i_h = nc.dram_tensor("cc1i", [128, 4], F32, kind="Internal")
    cc1o_h = nc.dram_tensor("cc1o", [128, 4], F32, kind="Internal")
    cc2i_h = nc.dram_tensor("cc2i", [128, 4], F32, kind="Internal")
    cc2o_h = nc.dram_tensor("cc2o", [128, 4], F32, kind="Internal")

    with ExitStack() as outer:
        def osb(*a):
            return outer.enter_context(nc.sbuf_tensor(*a))

        def sem(name):
            return outer.enter_context(nc.semaphore(name))

        b1_sb = osb("b1_sb", [128, 32], F32)
        sw1_sb = osb("sw1_sb", [128, 32], F32)
        scale1_sb = osb("scale1_sb", [128, 32], F32)
        b1q_sb = osb("b1q_sb", [128, 32], F32)
        sw2_sb = osb("sw2_sb", [128, 8], F32)
        ones_sb = osb("ones_sb", [1, 128], F32)
        ymax_sb = osb("ymax_sb", [128, 2], F32)
        hmax_sb = osb("hmax_sb", [128, 2], F32)
        s1b_sb = osb("s1b_sb", [128, 1], F32)
        r1b_sb = osb("r1b_sb", [128, 1], F32)
        s2b_sb = osb("s2b_sb", [128, 1], F32)
        r2b_sb = osb("r2b_sb", [128, 1], F32)
        g1_sb = osb("g1_sb", [1, 1], F32)
        g2_sb = osb("g2_sb", [1, 1], F32)
        s1v_sb = osb("s1v_sb", [1, 1], F32)
        s2v_sb = osb("s2v_sb", [1, 1], F32)
        redt_sb = osb("redt_sb", [1, 128], F32)
        dbg_sb = osb("dbg_sb", [128, 64], F32)
        cc_sb = osb("cc_sb", [128, 4], F32)

        pLD = sem("pLD"); pW = sem("pW"); pWF = sem("pWF"); pQS = sem("pQS")
        pST = sem("pST"); pSLF = sem("pSLF"); pMM = sem("pMM")
        pLDg = sem("pLDg"); r1T = sem("r1T")
        pW0 = sem("pW0"); pW1 = sem("pW1")
        bHS0 = sem("bHS0"); bHS1 = sem("bHS1"); bHS2 = sem("bHS2"); bHS3 = sem("bHS3")
        aXL0 = sem("aXL0"); aXL1 = sem("aXL1"); aX20 = sem("aX20")
        aX21 = sem("aX21"); aLN = sem("aLN"); aTR0 = sem("aTR0")
        aTR1 = sem("aTR1"); aPJ = sem("aPJ"); aQK = sem("aQK")
        aVE = sem("aVE"); aSG = sem("aSG"); aMN = sem("aMN")
        aES = sem("aES"); aEM = sem("aEM"); aO1 = sem("aO1")
        aOS0 = sem("aOS0"); aOS1 = sem("aOS1"); aY = sem("aY")
        aYS0 = sem("aYS0"); aYS1 = sem("aYS1"); aSQ1 = sem("aSQ1")
        aSQ2 = sem("aSQ2"); aRSTD1 = sem("aRSTD1"); aRSTD2 = sem("aRSTD2")
        aVAR1 = sem("aVAR1"); aVAR2 = sem("aVAR2"); aXLF = sem("aXLF")
        aX2F = sem("aX2F"); aSLF = sem("aSLF")
        r1S = sem("r1S"); r1C = sem("r1C"); r1D = sem("r1D"); r1P = sem("r1P")
        r1Cc = sem("r1Cc"); r2Cc = sem("r2Cc")
        bYL = sem("bYL"); bYQ = sem("bYQ"); bXQ = sem("bXQ"); bXF = sem("bXF")
        bHM = sem("bHM"); bHE = sem("bHE"); bHS = sem("bHS"); bHR = sem("bHR")
        r2S = sem("r2S"); r2C = sem("r2C"); r2D = sem("r2D"); r2P = sem("r2P")
        cHL = sem("cHL"); cHF = sem("cHF"); cOL = sem("cOL"); cQ1 = sem("cQ1")
        cQ2 = sem("cQ2"); cMM = sem("cMM"); cOE = sem("cOE"); cFO = sem("cFO")
        cOS = sem("cOS")
        vBAR = sem("vBAR"); sBAR = sem("sBAR")

        _vb = [0]
        _sb_ = [0]

        def vbar(ins=None):
            if ins is None:
                ins = nc.vector.engine_nop()
            _vb[0] += 1
            ins.then_inc(vBAR, 1)
            nc.vector.wait_ge(vBAR, _vb[0])

        def sbar(ins):
            _sb_[0] += 1
            ins.then_inc(sBAR, 1)
            nc.scalar.wait_ge(sBAR, _sb_[0])

        pa = outer.enter_context(ExitStack())

        def pasb(*a):
            return pa.enter_context(nc.sbuf_tensor(*a))
        wq_sb = pasb("wq_sb", [128, 8 * D], F16)
        wk_sb = pasb("wk_sb", [128, 8 * D], F16)
        wv_sb = pasb("wv_sb", [128, 8 * D], F16)
        bq_sb = pasb("bq_sb", [128, 8], F32)
        bk_sb = pasb("bk_sb", [128, 8], F32)
        mask_sb = pasb("mask_sb", [128, 128], F16)
        cnt_sb = pasb("cnt_sb", [128, 32], F32)
        eps_sb = pasb("eps_sb", [128, 1], F32)
        bv_bc = pasb("bv_bc", [128, D], F16)

        with ExitStack() as pstack:
            def psb(*a):
                return pstack.enter_context(nc.sbuf_tensor(*a))
            wbuf = [psb(f"wbuf{i}", [128, D], F32) for i in range(2)]
            w2buf = [psb(f"w2buf{i}", [128, FF], F32) for i in range(2)]
            i8b = psb("i8b", [128, D], I8)
            i8b2 = psb("i8b2", [128, FF], I8)
            w1qf = [psb(f"w1qf{i}", [128, D], F16) for i in range(2)]
            w2qf = [psb(f"w2qf{i}", [128, FF], F16) for i in range(2)]
            rw_sb = psb("rw_sb", [128, 1], F32)
            rw_in = psb("rw_in", [128, 1], F32)
            bvrow_sb = psb("bvrow_sb", [1, D], F32)
            psB = [pstack.enter_context(nc.psum_tensor(f"psB{i}", [128, 512], F32))
                   for i in range(2)]

            pblk = pstack.enter_context(nc.Block())

            @pblk.sync
            def _(sync):
                if SKIP_P:
                    return
                sync.dma_start(out=bq_sb[:, :], in_=bq_h[:, :]).then_inc(pLD, 16)
                sync.dma_start(out=bk_sb[:, :], in_=bk_h[:, :]).then_inc(pLD, 16)
                sync.dma_start(out=b1_sb[:, :], in_=b1_h[:, :]).then_inc(pLD, 16)
                sync.dma_start(out=mask_sb[:, :], in_=mask_h[:, :]).then_inc(pLD, 16)
                sync.dma_start(out=cnt_sb[:, :], in_=cnt_h[:, :]).then_inc(pLD, 16)
                sync.dma_start(out=ones_sb[:, :], in_=ones_h[:, :]).then_inc(pLD, 16)
                sync.dma_start(out=bvrow_sb[:, :], in_=bv_h[:, :]).then_inc(pLD, 16)
                for it in range(42):
                    if it < 40:
                        if it >= 2:
                            sync.wait_ge(pWF, it - 1)
                        if it < 32:
                            sync.dma_start(out=wbuf[it % 2][:, :],
                                           in_=w1_h[it * 128:(it + 1) * 128, :]
                                           ).then_inc(pW1 if it % 2 else pW0, 16)
                        else:
                            dc = it - 32
                            sync.dma_start(out=w2buf[dc % 2][:, :],
                                           in_=w2_h[dc * 128:(dc + 1) * 128, :]
                                           ).then_inc(pW1 if it % 2 else pW0, 16)
                    j = it - 2
                    if j >= 0:
                        sync.wait_ge(pQS, j + 1)
                        if j < 32:
                            sync.dma_start(out=w1q_h[j * 128:(j + 1) * 128, :],
                                           in_=w1qf[j % 2][:, :]).then_inc(pST, 16)
                        else:
                            dc = j - 32
                            sync.dma_start(out=w2q_h[dc * 128:(dc + 1) * 128, :],
                                           in_=w2qf[dc % 2][:, :]).then_inc(pST, 16)
                            sync.dma_start(out=sw2d_h[0:1, dc * 128:(dc + 1) * 128],
                                           in_=sw2_sb[:, dc:dc + 1]).then_inc(pST, 16)

            @pblk.gpsimd
            def _(gpsimd):
                if SKIP_P:
                    return
                for wh, wsb in ((wq_h, wq_sb), (wk_h, wk_sb), (wv_h, wv_sb)):
                    for c in range(8):
                        gpsimd.dma_start(
                            out=wsb[:, c * D:(c + 1) * D],
                            in_=wh[c * 128:(c + 1) * 128, :]).then_inc(pLDg, 16)
                gpsimd.wait_ge(pLDg, 16 * 24)

            @pblk.tensor
            def _(tensor):
                if SKIP_P:
                    return
                tensor.wait_ge(pLD, 16 * 7)
                tensor.matmul(psB[0][:, :], ones_sb[:, :], bvrow_sb[0:1, 0:512],
                              start=True, stop=True).then_inc(pMM, 1)
                tensor.matmul(psB[1][:, :], ones_sb[:, :], bvrow_sb[0:1, 512:1024],
                              start=True, stop=True).then_inc(pMM, 1)

            @pblk.vector
            def _(vector):
                if SKIP_P:
                    return
                vector.memset(eps_sb[:, :], EPS_LN)
                vector.memset(ymax_sb[:, :], 0.0)
                vector.memset(hmax_sb[:, :], 0.0)
                vector.wait_ge(pMM, 2)
                vector.tensor_copy(bv_bc[:, 0:512], psB[0][:, :])
                vector.tensor_copy(bv_bc[:, 512:1024], psB[1][:, :])
                selfc = 0
                for i in range(40):
                    is1 = i < 32
                    buf = wbuf[i % 2] if is1 else w2buf[(i - 32) % 2]
                    i8t = i8b if is1 else i8b2
                    qf = w1qf[i % 2] if is1 else w2qf[(i - 32) % 2]
                    scol = (sw1_sb[:, i:i + 1] if is1
                            else sw2_sb[:, i - 32:i - 31])
                    vector.wait_ge(pW1 if i % 2 else pW0, 16 * (i // 2 + 1))
                    vector.reduce_max(scol, buf[:, :], AX, apply_absolute_value=True)
                    vbar()
                    selfc += 1
                    vector.reciprocal(rw_sb[:, :], scol).then_inc(pSLF, 1)
                    nc.vector.wait_ge(pSLF, selfc)
                    vector.tensor_scalar(i8t[:, :], buf[:, :], rw_sb[:, :],
                                         127.0, ALU.mult,
                                         ALU.mult).then_inc(pWF, 1)
                    nc.vector.wait_ge(pWF, i + 1)
                    if DEBUG_TAPS:
                        vector.tensor_copy(dbg_sb[:, i:i + 1], rw_sb[:, :])
                        if i < 8:
                            vbar(vector.tensor_copy(dbg_sb[:, 40 + i:41 + i],
                                                    scol))
                            vbar(vector.tensor_copy(dbg_sb[:, 48 + i:49 + i],
                                                    buf[:, 0:1]))
                    if i >= 2:
                        j = i - 1
                        pst_need = 16 * (j + 1) if j < 32 else 512 + 32 * (j - 31)
                        vector.wait_ge(pST, pst_need)
                    vector.tensor_copy(qf[:, :], i8t[:, :]).then_inc(pQS, 1)

        if BUILD_PHASES < 2:
            nc.finalize()
            return nc
        with ExitStack() as astack:
            if SKIP_A:
                pass
            def asb(*a):
                return astack.enter_context(nc.sbuf_tensor(*a))
            K_sb = asb("K_sb", [128, 8 * S], F16)
            xa = [asb(f"xa{i}", [128, D], F32) for i in range(2)]
            x2b = [asb(f"x2b{i}", [128, D], F32) for i in range(2)]
            xn_tm = asb("xn_tm", [128, 4 * D], F16)
            xn_fm = [asb(f"xn_fm{i}", [128, 8 * 512], F16) for i in range(2)]
            q_fm = asb("q_fm", [128, 8 * 512], F16)
            v_sb = asb("v_sb", [128, 4 * D], F16)
            out1b = [asb(f"out1b{i}", [128, D], F16) for i in range(2)]
            ybuf = [asb(f"ybuf{i}", [128, D], F16) for i in range(2)]
            e_dump = asb("e_dump", [128, D], F16)
            masked_sb = asb("masked_sb", [128, 128], F32)
            msum_sb = asb("msum_sb", [128, 2], F32)
            m1_sb = asb("m1_sb", [128, 2], F32)
            mm_sb = asb("mm_sb", [128, 2], F32)
            v1_sb = asb("v1_sb", [128, 2], F32)
            sqacc_sb = asb("sqacc_sb", [128, 2], F32)
            lnt1_sb = asb("lnt1_sb", [128, 2], F32)
            rstd_sb = asb("rstd_sb", [128, 2], F32)
            msum2_sb = asb("msum2_sb", [128, 2], F32)
            m2_sb = asb("m2_sb", [128, 2], F32)
            mm2_sb = asb("mm2_sb", [128, 2], F32)
            v2_sb = asb("v2_sb", [128, 2], F32)
            sqacc2_sb = asb("sqacc2_sb", [128, 2], F32)
            lnt2_sb = asb("lnt2_sb", [128, 2], F32)
            rstd2_sb = asb("rstd2_sb", [128, 2], F32)
            mrun_sb = [asb(f"mrun{i}", [128, 1], F32) for i in range(2)]
            mneg_sb = [asb(f"mneg{i}", [128, 1], F32) for i in range(2)]
            delta_sb = [asb(f"delta{i}", [128, 1], F32) for i in range(2)]
            srun_sb = [asb(f"srun{i}", [128, 1], F32) for i in range(2)]
            esum_sb = [asb(f"esum{i}", [128, 1], F32) for i in range(2)]
            resc_sb = [asb(f"resc{i}", [128, 1], F32) for i in range(2)]
            gm_sb = asb("gm_sb", [128, 1], F32)
            den_sb = asb("den_sb", [128, 1], F32)
            rden_sb = asb("rden_sb", [128, 1], F32)
            rs_sb = asb("rs_sb", [128, 1], F32)
            ymt_sb = asb("ymt_sb", [128, 1], F32)
            ps = [astack.enter_context(
                nc.psum_tensor(f"psA{i}", [128, 512], F32)) for i in range(8)]

            ablk = astack.enter_context(nc.Block())

            NSUB = 4 * NST

            def xl_sem(k):
                return aXL0 if k % 2 == 0 else aXL1

            def x2_sem(k):
                return aX20 if k % 2 == 0 else aX21

            def os_sem(k):
                return aOS0 if k % 2 == 0 else aOS1

            def ys_sem(k):
                return aYS0 if k % 2 == 0 else aYS1

            def tr_sem(stx):
                return aTR0 if stx % 2 == 0 else aTR1

            @ablk.sync
            def _(sync):
                if SKIP_A:
                    return
                for j in range(2):
                    sync.dma_start(out=xa[j][:, :],
                                   in_=x_h[j * 128:(j + 1) * 128, :]
                                   ).then_inc(xl_sem(j), 16)
                    sync.dma_start(out=x2b[j][:, :],
                                   in_=x_h[j * 128:(j + 1) * 128, :]
                                   ).then_inc(x2_sem(j), 16)
                for j in range(2, 6):
                    sync.wait_ge(aXLF, j - 1)
                    sync.wait_ge(aSQ1, j - 1)
                    sync.dma_start(out=xa[j % 2][:, :],
                                   in_=x_h[j * 128:(j + 1) * 128, :]
                                   ).then_inc(xl_sem(j), 16)
                for p in range(NSUB):
                    j2 = p + 2
                    if j2 < NSUB:
                        sync.wait_ge(aX2F, p + 1)
                        sync.dma_start(out=x2b[j2 % 2][:, :],
                                       in_=x_h[j2 * 128:(j2 + 1) * 128, :]
                                       ).then_inc(x2_sem(j2), 16)
                    ja = p + 6
                    if ja < NSUB:
                        sync.wait_ge(aXLF, ja - 1)
                        sync.wait_ge(aSQ1, ja - 1)
                        sync.dma_start(out=xa[ja % 2][:, :],
                                       in_=x_h[ja * 128:(ja + 1) * 128, :]
                                       ).then_inc(xl_sem(ja), 16)

            @ablk.gpsimd
            def _(gpsimd):
                if SKIP_A:
                    return

            @ablk.scalar
            def _(scalar):
                if SKIP_A:
                    return

                def fe_sc(kN):
                    p = kN % 2
                    stN = kN // 4
                    ttn = kN % 4
                    scalar.wait_ge(xl_sem(kN), 16 * (kN // 2 + 1))
                    scalar.activation(e_dump[:, 0:D], xa[p][:, :], ACT.Square,
                                      accum_out=sqacc_sb[:, p:p + 1]
                                      ).then_inc(aSQ1, 1)
                    scalar.wait_ge(aVAR1, kN + 1)
                    sbar(scalar.activation(lnt1_sb[:, p:p + 1],
                                           v1_sb[:, p:p + 1], ACT.Ln,
                                           bias=eps_sb[:, :], scale=1.0))
                    scalar.activation(rstd_sb[:, p:p + 1], lnt1_sb[:, p:p + 1],
                                      ACT.Exp, scale=-0.5).then_inc(aRSTD1, 1)
                    scalar.wait_ge(aLN, kN + 1)
                    if stN >= 2:
                        scalar.wait_ge(aPJ, 24 * (stN - 1))
                    buf = xn_fm[stN % 2]
                    for c in range(8):
                        scalar.dma_start_transpose(
                            buf[:, c * 512 + ttn * 128: c * 512 + (ttn + 1) * 128],
                            xn_tm[:, ttn * D + c * 128: ttn * D + (c + 1) * 128]
                        ).then_inc(tr_sem(stN), 16)

                for j in range(4):
                    fe_sc(j)
                for st in range(NST):
                    sl = st % SPB
                    for u in range(16):
                        scalar.wait_ge(aPJ, 24 * st + u + 1)
                        bank = ps[u % 4]
                        if u < 8:
                            dest = q_fm[:, u * 512:(u + 1) * 512]
                            bias = bq_sb[:, u:u + 1]
                        else:
                            c = u - 8
                            dest = K_sb[:, c * S + sl * 512: c * S + (sl + 1) * 512]
                            bias = bk_sb[:, c:c + 1]
                        scalar.activation(dest, bank[:, :], ACT.Identity,
                                          bias=bias, scale=1.0).then_inc(aQK, 1)
                    grp_by_tt = {}
                    for rec in SGROUPS[st]:
                        grp_by_tt.setdefault(rec[0], []).append(rec)
                    for tt in range(4):
                        k = 4 * st + tt
                        kp = k % 2
                        for (_tt, g, ng, is_diag, bank, sgidx) in grp_by_tt[tt]:
                            scalar.wait_ge(aMN, sgidx + 1)
                            par = g % 2
                            scalar.activation(resc_sb[par][:, :],
                                              delta_sb[par][:, :],
                                              ACT.Exp, scale=1.0 / 32.0)
                            src = (masked_sb[:, 0:128] if is_diag
                                   else ps[bank][:, 0:ng])
                            scalar.activation(e_dump[:, 0:ng], src, ACT.Exp,
                                              scale=1.0 / 32.0,
                                              bias=mneg_sb[par][:, :],
                                              accum_out=esum_sb[par][:, :]
                                              ).then_inc(aES, 1)
                            if is_diag:
                                scalar.activation(resc_sb[(g + 1) % 2][:, :],
                                                  mneg_sb[par][:, :], ACT.Exp,
                                                  scale=1.0).then_inc(aEM, 1)
                        scalar.wait_ge(aO1, k + 1)
                        scalar.activation(e_dump[:, 0:D], out1b[kp][:, :],
                                          ACT.Square,
                                          accum_out=sqacc2_sb[:, kp:kp + 1]
                                          ).then_inc(aSQ2, 1)
                        scalar.dma_start(out=o16_h[k * 128:(k + 1) * 128, :],
                                         in_=out1b[kp][:, :]
                                         ).then_inc(os_sem(k), 16)
                        scalar.wait_ge(aVAR2, k + 1)
                        sbar(scalar.activation(lnt2_sb[:, kp:kp + 1],
                                               v2_sb[:, kp:kp + 1], ACT.Ln,
                                               bias=eps_sb[:, :], scale=1.0))
                        scalar.activation(rstd2_sb[:, kp:kp + 1],
                                          lnt2_sb[:, kp:kp + 1], ACT.Exp,
                                          scale=-0.5).then_inc(aRSTD2, 1)
                        if k >= 1:
                            scalar.wait_ge(aY, k)
                            scalar.dma_start(
                                out=y16_h[(k - 1) * 128:k * 128, :],
                                in_=ybuf[(k - 1) % 2][:, :]
                            ).then_inc(ys_sem(k - 1), 16)
                        if k + 4 < NSUB:
                            fe_sc(k + 4)
                scalar.wait_ge(aY, NSUB)
                scalar.dma_start(out=y16_h[(NSUB - 1) * 128:NSUB * 128, :],
                                 in_=ybuf[(NSUB - 1) % 2][:, :]
                                 ).then_inc(ys_sem(NSUB - 1), 16)
                scalar.wait_ge(aOS0, 16 * (NSUB // 2))
                scalar.wait_ge(aOS1, 16 * (NSUB // 2))
                scalar.wait_ge(aYS0, 16 * (NSUB // 2))
                scalar.wait_ge(aYS1, 16 * (NSUB // 2))

            @ablk.tensor
            def _(tensor):
                if SKIP_A:
                    return
                for st in range(NST):
                    sl = st % SPB
                    buf = xn_fm[st % 2]
                    tensor.wait_ge(tr_sem(st), 512 * (st // 2 + 1))
                    for u in range(24):
                        pu = 24 * st + u
                        if pu >= 4:
                            prev = pu - 4
                            pm = prev % 24
                            if pm < 16:
                                tensor.wait_ge(aQK, 16 * (prev // 24) + pm + 1)
                            else:
                                tensor.wait_ge(aVE, 8 * (prev // 24) + pm - 16 + 1)
                        bank = ps[u % 4]
                        if u < 8:
                            for c in range(8):
                                ins = tensor.matmul(
                                    bank[:, :],
                                    wq_sb[:, c * D + u * 128: c * D + (u + 1) * 128],
                                    buf[:, c * 512:(c + 1) * 512],
                                    start=(c == 0), stop=(c == 7))
                        elif u < 16:
                            co = u - 8
                            for c in range(8):
                                ins = tensor.matmul(
                                    bank[:, :],
                                    wk_sb[:, c * D + co * 128: c * D + (co + 1) * 128],
                                    buf[:, c * 512:(c + 1) * 512],
                                    start=(c == 0), stop=(c == 7))
                        else:
                            tt = (u - 16) // 2
                            half = (u - 16) % 2
                            for c in range(8):
                                ins = tensor.matmul(
                                    bank[:, :],
                                    buf[:, c * 512 + tt * 128: c * 512 + (tt + 1) * 128],
                                    wv_sb[:, c * D + half * 512: c * D + half * 512 + 512],
                                    start=(c == 0), stop=(c == 7))
                        ins.then_inc(aPJ, 1)
                    tensor.wait_ge(aQK, 16 * (st + 1))
                    for (tt, g, ng, is_diag, bank, sgidx) in SGROUPS[st]:
                        qt = sl * 4 + tt
                        if sgidx >= 4:
                            tensor.wait_ge(aES, sgidx - 3)
                        k0 = qt * 128 if is_diag else g * 512
                        for c in range(8):
                            ins = tensor.matmul(
                                ps[bank][:, 0:ng],
                                q_fm[:, c * 512 + tt * 128: c * 512 + (tt + 1) * 128],
                                K_sb[:, c * S + k0: c * S + k0 + ng],
                                start=(c == 0), stop=(c == 7))
                        ins.then_inc(aSG, 1)

            @ablk.vector
            def _(vector):
                if SKIP_A:
                    return
                slf = [0]

                def self_wait(ins):
                    slf[0] += 1
                    ins.then_inc(aSLF, 1)
                    nc.vector.wait_ge(aSLF, slf[0])

                def fe_dve(kN):
                    p = kN % 2
                    vector.wait_ge(xl_sem(kN), 16 * (kN // 2 + 1))
                    vector.reduce_sum(msum_sb[:, p:p + 1], xa[p][:, :], AX)
                    vbar()
                    vbar(vector.tensor_scalar(m1_sb[:, p:p + 1],
                                              msum_sb[:, p:p + 1],
                                              1.0 / D, None, ALU.mult))
                    vector.wait_ge(aSQ1, kN + 1)
                    vbar(vector.tensor_mul(mm_sb[:, p:p + 1], m1_sb[:, p:p + 1],
                                           m1_sb[:, p:p + 1]))
                    vector.scalar_tensor_tensor(
                        v1_sb[:, p:p + 1], sqacc_sb[:, p:p + 1], 1.0 / D,
                        mm_sb[:, p:p + 1], ALU.mult,
                        ALU.subtract).then_inc(aVAR1, 1)
                    nc.vector.wait_ge(aVAR1, kN + 1)
                    vector.wait_ge(aRSTD1, kN + 1)
                    if kN >= 4:
                        stp = (kN - 4) // 4
                        vector.wait_ge(tr_sem(stp), 512 * (stp // 2 + 1))
                    ttn = kN % 4
                    vector.tensor_scalar(xn_tm[:, ttn * D:(ttn + 1) * D],
                                         xa[p][:, :], m1_sb[:, p:p + 1],
                                         rstd_sb[:, p:p + 1],
                                         ALU.subtract, ALU.mult)
                    nc.vector.engine_nop().then_inc(aLN, 1)
                    nc.vector.engine_nop().then_inc(aXLF, 1)

                for j in range(4):
                    fe_dve(j)
                for st in range(NST):
                    sl = st % SPB
                    for u in range(8):
                        vector.wait_ge(aPJ, 24 * st + 16 + u + 1)
                        bank = ps[(16 + u) % 4]
                        tt = u // 2
                        half = u % 2
                        dst = v_sb[:, tt * D + half * 512: tt * D + half * 512 + 512]
                        vector.tensor_add(dst, bank[:, :],
                                          bv_bc[:, half * 512: half * 512 + 512]
                                          ).then_inc(aVE, 1)
                    grp_by_tt = {}
                    for rec in SGROUPS[st]:
                        grp_by_tt.setdefault(rec[0], []).append(rec)
                    for tt in range(4):
                        k = 4 * st + tt
                        kp = k % 2
                        qt = sl * 4 + tt
                        recs = grp_by_tt[tt]
                        vector.memset(mrun_sb[0][:, :], 0.0)
                        vector.memset(srun_sb[0][:, :], 0.0)
                        for (_tt, g, ng, is_diag, bank, sgidx) in recs:
                            par = g % 2
                            nxt = (g + 1) % 2
                            vector.wait_ge(aSG, sgidx + 1)
                            if is_diag:
                                vbar(vector.tensor_add(masked_sb[:, :],
                                                       ps[bank][:, 0:128],
                                                       mask_sb[:, :]))
                                src = masked_sb[:, :]
                            else:
                                src = ps[bank][:, 0:ng]
                            vector.reduce_max(gm_sb[:, :], src, AX)
                            vbar()
                            vbar(vector.tensor_max(mrun_sb[nxt][:, :],
                                                   mrun_sb[par][:, :],
                                                   gm_sb[:, :]))
                            vector.tensor_sub(delta_sb[par][:, :],
                                              mrun_sb[par][:, :], mrun_sb[nxt][:, :])
                            vector.tensor_scalar(mneg_sb[par][:, :],
                                                 mrun_sb[nxt][:, :], -1.0 / 32.0,
                                                 None, ALU.mult).then_inc(aMN, 1)
                            vector.wait_ge(aES, sgidx + 1)
                            vector.scalar_tensor_tensor(
                                srun_sb[nxt][:, :], srun_sb[par][:, :],
                                resc_sb[par][:, :], esum_sb[par][:, :],
                                ALU.mult, ALU.add)
                        lastpar = len(recs) % 2
                        vector.wait_ge(aEM, 4 * st + tt + 1)
                        vbar(vector.scalar_tensor_tensor(
                            den_sb[:, :], resc_sb[lastpar][:, :],
                            cnt_sb[:, qt:qt + 1], srun_sb[lastpar][:, :],
                            ALU.mult, ALU.add))
                        vbar(vector.reciprocal(rden_sb[:, :], den_sb[:, :]))
                        self_wait(vector.tensor_mul(rs_sb[:, :], den_sb[:, :],
                                                    rden_sb[:, :]))
                        vector.wait_ge(x2_sem(k), 16 * (k // 2 + 1))
                        if k >= 2:
                            vector.wait_ge(os_sem(k), 16 * (k // 2))
                        vector.scalar_tensor_tensor(
                            out1b[kp][:, :],
                            v_sb[:, tt * D:(tt + 1) * D], rs_sb[:, :],
                            x2b[kp][:, :], ALU.mult, ALU.add)
                        nc.vector.engine_nop().then_inc(aO1, 1)
                        nc.vector.engine_nop().then_inc(aX2F, 1)
                        vbar()
                        vector.reduce_sum(msum2_sb[:, kp:kp + 1],
                                          out1b[kp][:, :], AX)
                        vbar()
                        vbar(vector.tensor_scalar(m2_sb[:, kp:kp + 1],
                                                  msum2_sb[:, kp:kp + 1],
                                                  1.0 / D, None, ALU.mult))
                        vector.wait_ge(aSQ2, k + 1)
                        vbar(vector.tensor_mul(mm2_sb[:, kp:kp + 1],
                                               m2_sb[:, kp:kp + 1],
                                               m2_sb[:, kp:kp + 1]))
                        vector.scalar_tensor_tensor(
                            v2_sb[:, kp:kp + 1], sqacc2_sb[:, kp:kp + 1],
                            1.0 / D, mm2_sb[:, kp:kp + 1], ALU.mult,
                            ALU.subtract).then_inc(aVAR2, 1)
                        nc.vector.wait_ge(aVAR2, k + 1)
                        if k + 4 < NSUB:
                            fe_dve(k + 4)
                        vector.wait_ge(aRSTD2, k + 1)
                        if k >= 2:
                            vector.wait_ge(ys_sem(k), 16 * (k // 2))
                        vector.tensor_scalar(ybuf[kp][:, :], out1b[kp][:, :],
                                             m2_sb[:, kp:kp + 1],
                                             rstd2_sb[:, kp:kp + 1],
                                             ALU.subtract, ALU.mult)
                        nc.vector.engine_nop().then_inc(aY, 1)
                        vbar()
                        vector.reduce_max(ymt_sb[:, :], ybuf[kp][:, :], AX)
                        vbar()
                        vbar(vector.tensor_max(
                            ymax_sb[:, (k + 1) % 2:(k + 1) % 2 + 1],
                            ymax_sb[:, k % 2:k % 2 + 1], ymt_sb[:, :]))

        pa.close()
        if BUILD_PHASES < 3:
            nc.finalize()
            return nc

        with ExitStack() as ccom:
            def csb(*a):
                return ccom.enter_context(nc.sbuf_tensor(*a))
            w2qT_sb = csb("w2qT_sb", [128, 32 * D], F16)
            psR = [ccom.enter_context(nc.psum_tensor(f"psR{i}", [128, 512], F32))
                   for i in range(2)]

            with ExitStack() as bstack:
                def bsb(*a):
                    return bstack.enter_context(nc.sbuf_tensor(*a))
                w1qT_sb = bsb("w1qT_sb", [128, 8 * FF], F16)
                y_fm = [bsb(f"y_fm{i}", [128, 8 * 512], F16) for i in range(2)]
                xq_sb = [bsb(f"xq{i}", [128, 8 * 512], F16) for i in range(2)]
                yq_u8 = bsb("yq_u8", [128, 512], U8)
                h_tile = [bsb(f"h_tile{i}", [128, 512], F16) for i in range(4)]
                hmc_sb = bsb("hmc_sb", [128, 32], F32)
                hmt_sb = bsb("hmt_sb", [128, 1], F32)
                b1i_sb = bsb("b1i_sb", [128, 32], I32)
                b1f_sb = bsb("b1f_sb", [128, 32], F32)
                rcs1_sb = bsb("rcs1_sb", [128, 32], F32)
                ps_b = [bstack.enter_context(
                    nc.psum_tensor(f"psHB{i}", [128, 512], F32)) for i in range(4)]

                ar1blk = nc.Block()
                ar1blk.__enter__()

                @ar1blk.sync
                def _(sync):
                    if SKIP_AR1:
                        return
                    ymfin = ymax_sb[:, NST % 2:NST % 2 + 1]
                    sync.dma_start(out=red1_h[:, :], in_=ymfin).then_inc(r1S, 16)
                    sync.wait_ge(r1S, 16)
                    sync.dma_start(out=redt_sb[:, :],
                                   in_=red1_h[:, :]).then_inc(r1S, 16)
                    sync.wait_ge(r1D, 1)
                    sync.dma_start(out=cc1i_h[:, :],
                                   in_=cc_sb[:, :]).then_inc(r1S, 16)
                    sync.wait_ge(r1Cc, 1)
                    sync.dma_start(out=g1_sb[:, :],
                                   in_=cc1o_h[0:1, 0:1]).then_inc(r1C, 16)

                @ar1blk.vector
                def _(vector):
                    if SKIP_AR1:
                        return
                    vector.wait_ge(r1S, 32)
                    vector.reduce_max(g1_sb[:, :], redt_sb[:, :], AX)
                    vbar()
                    vbar(vector.memset(cc_sb[:, :], 0.0))
                    vector.tensor_copy(cc_sb[0:1, 0:1], g1_sb[:, :]).then_inc(r1D, 1)
                    vector.wait_ge(r1C, 16)
                    vector.tensor_scalar(s1v_sb[:, :], g1_sb[:, :], EPS_Q,
                                         1.0 / 255.0, ALU.max,
                                         ALU.mult).then_inc(r1D, 1)
                    vector.wait_ge(r1P, 1)
                    vector.tensor_copy(s1b_sb[:, :],
                                       psR[0][:, 0:1]).then_inc(r1D, 1)
                    nc.vector.wait_ge(r1D, 3)
                    vector.reciprocal(r1b_sb[:, :], s1b_sb[:, :])
                    vector.tensor_scalar(scale1_sb[:, :], sw1_sb[:, :],
                                         s1b_sb[:, :], 1.0 / 127.0, ALU.mult,
                                         ALU.mult).then_inc(r1D, 1)
                    nc.vector.wait_ge(r1D, 4)
                    vbar(vector.reciprocal(rcs1_sb[:, :], scale1_sb[:, :]))
                    vbar(vector.tensor_mul(b1f_sb[:, :], b1_sb[:, :], rcs1_sb[:, :]))
                    vbar(vector.tensor_copy(b1i_sb[:, :], b1f_sb[:, :]))
                    vbar(vector.tensor_copy(b1f_sb[:, :], b1i_sb[:, :]))
                    vbar(vector.tensor_mul(b1q_sb[:, :], b1f_sb[:, :],
                                           scale1_sb[:, :]))

                @ar1blk.gpsimd
                def _(gpsimd):
                    if SKIP_AR1:
                        return
                    gpsimd.wait_ge(r1S, 48)
                    gpsimd.collective_compute(
                        "AllReduce", ALU.max,
                        replica_groups=[list(range(NCORES))],
                        ins=[cc1i_h.ap().opt()],
                        outs=[cc1o_h.ap().opt()]).then_inc(r1Cc, 1)

                @ar1blk.tensor
                def _(tensor):
                    if SKIP_AR1:
                        return
                    tensor.wait_ge(r1D, 2)
                    tensor.matmul(psR[0][:, 0:1], ones_sb[:, :], s1v_sb[:, :],
                                  start=True, stop=True).then_inc(r1P, 1)

                @ar1blk.scalar
                def _(scalar):
                    if SKIP_AR1:
                        return
                    for dc in range(8):
                        scalar.dma_start_transpose(
                            w1qT_sb[:, dc * FF:(dc + 1) * FF],
                            w1q_h[0:FF, dc * 128:(dc + 1) * 128]).then_inc(r1T, 16)
                    for fc in range(32):
                        scalar.dma_start_transpose(
                            w2qT_sb[:, fc * D:(fc + 1) * D],
                            w2q_h[0:D, fc * 128:(fc + 1) * 128]).then_inc(r1T, 16)
                    scalar.wait_ge(r1T, 16 * 40)

                ar1blk.__exit__(None, None, None)
                if BUILD_PHASES < 4:
                    nc.finalize()
                    return nc
                bblk = bstack.enter_context(nc.Block())

                @bblk.vector
                def _(vector):
                    def hmax_group(g):
                        for fc in range(32):
                            u = 32 * g + fc
                            nc.vector.wait_ge(bHE, u + 1)
                            nc.vector.reduce_max(hmc_sb[:, fc:fc + 1],
                                                 h_tile[fc % 4][:, :], AX)
                            nc.vector.engine_nop().then_inc(bHR, 1)
                        nc.vector.wait_ge(bHR, 32 * (g + 1))
                        nc.vector.reduce_max(hmt_sb[:, :], hmc_sb[:, :], AX)
                        vbar()
                        nc.vector.tensor_max(
                            hmax_sb[:, (g + 1) % 2:(g + 1) % 2 + 1],
                            hmax_sb[:, g % 2:g % 2 + 1], hmt_sb[:, :])

                    for g in range(16):
                        vector.wait_ge(bYL, 128 * (g + 1))
                        if g >= 2 and B_STAGES >= 2:
                            vector.wait_ge(bHM, 32 * (g - 1))
                        for c in range(8):
                            sl_ = slice(c * 512, (c + 1) * 512)
                            vbar(vector.tensor_scalar(yq_u8[:, :],
                                                      y_fm[g % 2][:, sl_],
                                                      r1b_sb[:, :], None,
                                                      ALU.mult))
                            vector.tensor_copy(xq_sb[g % 2][:, sl_], yq_u8[:, :])
                        nc.vector.engine_nop().then_inc(bYQ, 1)
                        nc.vector.engine_nop().then_inc(bXQ, 1)
                        if g >= 1 and B_STAGES >= 3 and B_HMAX:
                            hmax_group(g - 1)
                    if B_STAGES >= 3 and B_HMAX:
                        hmax_group(15)

                @bblk.tensor
                def _(tensor):
                    if B_STAGES < 2:
                        return
                    for g in range(16):
                        tensor.wait_ge(bXQ, g + 1)
                        for fc in range(32):
                            u = 32 * g + fc
                            if u >= 4 and B_STAGES >= 3:
                                tensor.wait_ge(bHE, u - 3)
                            for dc in range(8):
                                ins = tensor.matmul(
                                    ps_b[fc % 4][:, :],
                                    w1qT_sb[:, dc * FF + fc * 128:
                                            dc * FF + (fc + 1) * 128],
                                    xq_sb[g % 2][:, dc * 512:(dc + 1) * 512],
                                    start=(dc == 0), stop=(dc == 7))
                            ins.then_inc(bHM, 1)

                @bblk.scalar
                def _(scalar):
                    def yfm_loads(g):
                        if g >= 2:
                            scalar.wait_ge(bYQ, g - 1)
                        for c in range(8):
                            scalar.dma_start_transpose(
                                y_fm[g % 2][:, c * 512:(c + 1) * 512],
                                y16_h[g * 512:(g + 1) * 512,
                                      c * 128:(c + 1) * 128]).then_inc(bYL, 16)

                    def evac_group(g):
                        for fc in range(32):
                            u = 32 * g + fc
                            scalar.wait_ge(bHM, u + 1)
                            if u >= 4:
                                if B_STAGES >= 4:
                                    scalar.wait_ge([bHS0, bHS1, bHS2, bHS3][u % 4],
                                                   16 * (u // 4))
                                if B_HMAX:
                                    scalar.wait_ge(bHR, u - 3)
                            if B_SIMPLE_EVAC:
                                scalar.copy(h_tile[fc % 4][:, :],
                                            ps_b[fc % 4][:, :]).then_inc(bHE, 1)
                            else:
                                scalar.activation(h_tile[fc % 4][:, :],
                                                  ps_b[fc % 4][:, :], ACT.Relu,
                                                  bias=b1q_sb[:, fc:fc + 1],
                                                  scale=scale1_sb[:, fc:fc + 1]
                                                  ).then_inc(bHE, 1)
                            if B_STAGES >= 4:
                                scalar.wait_ge(bHE, u + 1)
                                scalar.dma_start(out=h16_h[fc, g],
                                                 in_=h_tile[fc % 4][:, :]
                                                 ).then_inc([bHS0, bHS1, bHS2, bHS3][u % 4], 16)

                    for g in range(16):
                        yfm_loads(g)
                        if B_STAGES >= 3 and g >= 1:
                            evac_group(g - 1)
                    if B_STAGES >= 3:
                        evac_group(15)
                    if B_STAGES >= 4:
                        for bb in range(4):
                            scalar.wait_ge([bHS0, bHS1, bHS2, bHS3][bb], 16 * 8 * 16)

            if BUILD_PHASES < 5:
                nc.finalize()
                return nc
            with ExitStack() as cstack:
                def c2sb(*a):
                    return cstack.enter_context(nc.sbuf_tensor(*a))
                SCALE2B = c2sb("SCALE2B", [128, D], F32)
                B2QB = c2sb("B2QB", [128, D], F32)
                ps_c = [cstack.enter_context(
                    nc.psum_tensor(f"psOC{i}", [128, 512], F32)) for i in range(4)]

                arstack = cstack.enter_context(ExitStack())

                def arsb(*a):
                    return arstack.enter_context(nc.sbuf_tensor(*a))
                b2row_sb = arsb("b2row_sb", [1, D], F32)
                s2wrow_sb = arsb("s2wrow_sb", [1, D], F32)
                scale2row_sb = arsb("scale2row_sb", [1, D], F32)
                b2qrow_sb = arsb("b2qrow_sb", [1, D], F32)
                rc2row_sb = arsb("rc2row_sb", [1, D], F32)
                tmp1r_sb = arsb("tmp1r_sb", [1, D], F32)
                tmpi1r_sb = arsb("tmpi1r_sb", [1, D], I32)

                ar2blk = nc.Block()
                ar2blk.__enter__()

                @ar2blk.sync
                def _(sync):
                    hmfin = hmax_sb[:, 16 % 2:16 % 2 + 1]
                    sync.dma_start(out=red2_h[:, :], in_=hmfin).then_inc(r2S, 16)
                    sync.wait_ge(r2S, 16)
                    sync.dma_start(out=redt_sb[:, :],
                                   in_=red2_h[:, :]).then_inc(r2S, 16)
                    sync.dma_start(out=s2wrow_sb[:, :],
                                   in_=sw2d_h[:, :]).then_inc(r2S, 16)
                    sync.dma_start(out=b2row_sb[:, :],
                                   in_=b2_h[:, :]).then_inc(r2S, 16)
                    sync.wait_ge(r2D, 1)
                    sync.dma_start(out=cc2i_h[:, :],
                                   in_=cc_sb[:, :]).then_inc(r2S, 16)
                    sync.wait_ge(r2Cc, 1)
                    sync.dma_start(out=g2_sb[:, :],
                                   in_=cc2o_h[0:1, 0:1]).then_inc(r2C, 16)

                @ar2blk.vector
                def _(vector):
                    vector.wait_ge(r2S, 32)
                    vector.reduce_max(g2_sb[:, :], redt_sb[:, :], AX)
                    vbar()
                    vbar(vector.memset(cc_sb[:, :], 0.0))
                    vector.tensor_copy(cc_sb[0:1, 0:1], g2_sb[:, :]).then_inc(r2D, 1)
                    vector.wait_ge(r2C, 16)
                    vector.tensor_scalar(s2v_sb[:, :], g2_sb[:, :], EPS_Q,
                                         1.0 / 255.0, ALU.max,
                                         ALU.mult).then_inc(r2D, 1)
                    nc.vector.wait_ge(r2D, 2)
                    vector.wait_ge(r2S, 64)
                    vbar(vector.tensor_scalar(scale2row_sb[:, :], s2wrow_sb[:, :],
                                              s2v_sb[:, :], 1.0 / 127.0, ALU.mult,
                                              ALU.mult))
                    vbar(vector.reciprocal(rc2row_sb[:, :], scale2row_sb[:, :]))
                    vbar(vector.tensor_mul(tmp1r_sb[:, :], b2row_sb[:, :],
                                           rc2row_sb[:, :]))
                    vbar(vector.tensor_copy(tmpi1r_sb[:, :], tmp1r_sb[:, :]))
                    vbar(vector.tensor_copy(tmp1r_sb[:, :], tmpi1r_sb[:, :]))
                    vector.tensor_mul(b2qrow_sb[:, :], tmp1r_sb[:, :],
                                      scale2row_sb[:, :]).then_inc(r2D, 1)
                    vector.wait_ge(r2P, 2)
                    vector.tensor_copy(SCALE2B[:, 0:512], psR[0][:, :])
                    vector.tensor_copy(SCALE2B[:, 512:1024],
                                       psR[1][:, :]).then_inc(r2D, 1)
                    vector.wait_ge(r2P, 4)
                    vector.tensor_copy(B2QB[:, 0:512], psR[0][:, :])
                    vector.tensor_copy(B2QB[:, 512:1024],
                                       psR[1][:, :]).then_inc(r2D, 1)
                    vector.wait_ge(r2P, 5)
                    vector.tensor_copy(s2b_sb[:, :],
                                       psR[0][:, 0:1]).then_inc(r2D, 1)
                    nc.vector.wait_ge(r2D, 6)
                    vector.reciprocal(r2b_sb[:, :], s2b_sb[:, :])

                @ar2blk.gpsimd
                def _(gpsimd):
                    gpsimd.wait_ge(r2S, 80)
                    gpsimd.collective_compute(
                        "AllReduce", ALU.max,
                        replica_groups=[list(range(NCORES))],
                        ins=[cc2i_h.ap().opt()],
                        outs=[cc2o_h.ap().opt()]).then_inc(r2Cc, 1)

                @ar2blk.tensor
                def _(tensor):
                    tensor.wait_ge(r2D, 3)
                    tensor.matmul(psR[0][:, :], ones_sb[:, :],
                                  scale2row_sb[0:1, 0:512], start=True,
                                  stop=True).then_inc(r2P, 1)
                    tensor.matmul(psR[1][:, :], ones_sb[:, :],
                                  scale2row_sb[0:1, 512:1024], start=True,
                                  stop=True).then_inc(r2P, 1)
                    tensor.wait_ge(r2D, 4)
                    tensor.matmul(psR[0][:, :], ones_sb[:, :],
                                  b2qrow_sb[0:1, 0:512], start=True,
                                  stop=True).then_inc(r2P, 1)
                    tensor.matmul(psR[1][:, :], ones_sb[:, :],
                                  b2qrow_sb[0:1, 512:1024], start=True,
                                  stop=True).then_inc(r2P, 1)
                    tensor.wait_ge(r2D, 5)
                    tensor.matmul(psR[0][:, 0:1], ones_sb[:, :], s2v_sb[:, :],
                                  start=True, stop=True).then_inc(r2P, 1)

                ar2blk.__exit__(None, None, None)
                arstack.close()
                if BUILD_PHASES < 6:
                    nc.finalize()
                    return nc
                h_ld = [c2sb(f"h_ld{i}", [128, 32 * 512], F16) for i in range(2)]
                o1_ld = [c2sb(f"o1_ld{i}", [128, 4 * D], F16) for i in range(2)]
                fout_sb = c2sb("fout_sb", [128, 4 * D], F32)
                hu8_sb = [c2sb(f"hu8{i}", [128, 512], U8) for i in range(2)]
                eb1_sb = c2sb("eb1_sb", [128, 512], F32)
                eb2_sb = c2sb("eb2_sb", [128, 512], F32)

                cblk = cstack.enter_context(nc.Block())

                @cblk.sync
                def _(sync):
                    for g in range(16):
                        if g >= 2:
                            sync.wait_ge(cMM, 4 * (g - 1))
                            sync.wait_ge(cFO, g - 1)
                        for fc in range(32):
                            sync.dma_start(
                                out=h_ld[g % 2][:, fc * 512:(fc + 1) * 512],
                                in_=h16_h[fc, g]).then_inc(cHL, 16)
                        for tt in range(4):
                            r0 = g * 512 + tt * 128
                            sync.dma_start(out=o1_ld[g % 2][:, tt * D:(tt + 1) * D],
                                           in_=o16_h[r0:r0 + 128, :]
                                           ).then_inc(cOL, 16)
                        if g >= 1:
                            sync.wait_ge(cFO, g)
                            for tt in range(4):
                                r0 = (g - 1) * 512 + tt * 128
                                sync.dma_start(out=out_h[r0:r0 + 128, :],
                                               in_=fout_sb[:, tt * D:(tt + 1) * D]
                                               ).then_inc(cOS, 16)
                    sync.wait_ge(cFO, 16)
                    for tt in range(4):
                        r0 = 15 * 512 + tt * 128
                        sync.dma_start(out=out_h[r0:r0 + 128, :],
                                       in_=fout_sb[:, tt * D:(tt + 1) * D]
                                       ).then_inc(cOS, 16)
                    sync.wait_ge(cOS, 16 * 64)
                    if DEBUG_TAPS:
                        sync.dma_start(out=dbg_h[:, :], in_=dbg_sb[:, :]).then_inc(cOS, 16)
                        sync.wait_ge(cOS, 16 * 65)

                @cblk.vector
                def _(vector):
                    def epilogue_group(g):
                        if g >= 1:
                            nc.vector.wait_ge(cOS, 64 * g)
                        nc.vector.wait_ge(cOL, 64 * min(g + 2, 16))
                        for tt in range(4):
                            nc.vector.wait_ge(cMM, 4 * g + tt + 1)
                            for half in range(2):
                                bank = ps_c[(tt % 2) * 2 + half]
                                sl_ = slice(half * 512, half * 512 + 512)
                                osl = slice(tt * D + half * 512,
                                            tt * D + half * 512 + 512)
                                vbar(nc.vector.tensor_mul(eb1_sb[:, :],
                                                          bank[:, :],
                                                          SCALE2B[:, sl_]))
                                vbar(nc.vector.tensor_add(eb2_sb[:, :],
                                                          eb1_sb[:, :],
                                                          B2QB[:, sl_]))
                                nc.vector.tensor_add(fout_sb[:, osl],
                                                     eb2_sb[:, :],
                                                     o1_ld[g % 2][:, osl])
                            nc.vector.engine_nop().then_inc(cOE, 1)
                        nc.vector.engine_nop().then_inc(cFO, 1)

                    for g in range(16):
                        vector.wait_ge(cHL, 512 * (g + 1))
                        for fc in range(32):
                            u = 32 * g + fc
                            if u >= 2:
                                vector.wait_ge(cQ2, u - 1)
                            vector.tensor_scalar(
                                hu8_sb[fc % 2][:, :],
                                h_ld[g % 2][:, fc * 512:(fc + 1) * 512],
                                r2b_sb[:, :], None, ALU.mult).then_inc(cQ1, 1)
                        if g >= 1:
                            epilogue_group(g - 1)
                    epilogue_group(15)

                @cblk.scalar
                def _(scalar):
                    for g in range(16):
                        for fc in range(32):
                            u = 32 * g + fc
                            scalar.wait_ge(cQ1, u + 1)
                            scalar.copy(h_ld[g % 2][:, fc * 512:(fc + 1) * 512],
                                        hu8_sb[fc % 2][:, :]).then_inc(cQ2, 1)

                @cblk.tensor
                def _(tensor):
                    for g in range(16):
                        for tt in range(4):
                            tu = 4 * g + tt
                            if tu >= 2:
                                tensor.wait_ge(cOE, tu - 1)
                            b0 = ps_c[(tt % 2) * 2]
                            b1 = ps_c[(tt % 2) * 2 + 1]
                            for fc in range(32):
                                if tt == 0:
                                    tensor.wait_ge(cQ2, 32 * g + fc + 1)
                                lhs = h_ld[g % 2][:, fc * 512 + tt * 128:
                                                  fc * 512 + (tt + 1) * 128]
                                tensor.matmul(b0[:, :], lhs,
                                              w2qT_sb[:, fc * D: fc * D + 512],
                                              start=(fc == 0), stop=(fc == 31))
                                ins = tensor.matmul(b1[:, :], lhs,
                                                    w2qT_sb[:, fc * D + 512:
                                                            fc * D + 1024],
                                                    start=(fc == 0),
                                                    stop=(fc == 31))
                            ins.then_inc(cMM, 1)

    nc.finalize()
    return nc


def kernel(L0_attn_X, L0_attn_Wq, L0_attn_bq, L0_attn_Wk, L0_attn_bk,
           L0_attn_Wv, L0_attn_bv, L0_ffn_w1, L0_ffn_b1, L0_ffn_w2, L0_ffn_b2):
    X = np.ascontiguousarray(np.asarray(L0_attn_X, dtype=np.float32))
    wq = np.ascontiguousarray(np.asarray(L0_attn_Wq, dtype=np.float32))
    wk = np.ascontiguousarray(np.asarray(L0_attn_Wk, dtype=np.float32))
    wv = np.ascontiguousarray(np.asarray(L0_attn_Wv, dtype=np.float32))
    bq = np.asarray(L0_attn_bq, dtype=np.float32)
    bk = np.asarray(L0_attn_bk, dtype=np.float32)
    bv = np.asarray(L0_attn_bv, dtype=np.float32)
    w1 = np.ascontiguousarray(np.asarray(L0_ffn_w1, dtype=np.float32))
    b1 = np.asarray(L0_ffn_b1, dtype=np.float32)
    w2 = np.ascontiguousarray(np.asarray(L0_ffn_w2, dtype=np.float32))
    b2 = np.asarray(L0_ffn_b2, dtype=np.float32)

    bqr = np.ascontiguousarray(bq.reshape(8, 128).T)
    bkr = np.ascontiguousarray(bk.reshape(8, 128).T)
    bvr = np.ascontiguousarray(bv.reshape(1, D))
    b1r = np.ascontiguousarray(b1.reshape(32, 128).T)
    b2r = np.ascontiguousarray(b2.reshape(1, D))
    ii = np.arange(128)
    mask = np.where(ii[None, :] <= ii[:, None], 0.0, NEG).astype(np.float16)
    qts = np.arange(32)
    cnt = (S - 1 - (qts[None, :] * 128 + ii[:, None])).astype(np.float32)
    onesf = np.ones((1, 128), np.float32)

    nc = build_nc()

    in_maps = []
    for c in range(NCORES):
        xs = np.ascontiguousarray(X[2 * c:2 * c + 2].reshape(T, D))
        in_maps.append(dict(x=xs, wq=wq, wk=wk, wv=wv, bqr=bqr, bkr=bkr,
                            bvr=bvr, w1=w1, b1r=b1r, w2=w2, b2r=b2r,
                            mask=mask, cnt=cnt, onesf=onesf))
    res = run_bass_kernel_spmd(nc, in_maps, core_ids=list(range(NCORES)))
    out = np.concatenate([np.asarray(res.results[c]["out"], np.float32)
                          for c in range(NCORES)], axis=0)
    return out.reshape(16, S, D)


# version 09
# speedup vs baseline: 1.0674x; 1.0674x over previous
import numpy as np
from contextlib import ExitStack

import concourse.bass as bass
import concourse.mybir as mybir
from concourse.bass_utils import run_bass_kernel_spmd

F32 = mybir.dt.float32
F16 = mybir.dt.float16
U8 = mybir.dt.uint8
I8 = mybir.dt.int8
I32 = mybir.dt.int32
ALU = mybir.AluOpType
ACT = mybir.ActivationFunctionType
AX = mybir.AxisListType.X

NCORES = 8
S = 4096
D = 1024
FF = 4096
T = 2 * S
NST = T // 512
SPB = S // 512
EPS_LN = 1e-5
EPS_Q = 1e-8
NEG = -60000.0


def _sgroups():
    out = []
    sgidx = 0
    for st in range(NST):
        sl = st % SPB
        rows = []
        for tt in range(4):
            qt = sl * 4 + tt
            gf = (qt * 128 + 511) // 512
            for g in range(gf):
                ng = min(512, qt * 128 - g * 512)
                rows.append((tt, g, ng, False, sgidx % 8, sgidx))
                sgidx += 1
            rows.append((tt, gf, 128, True, sgidx % 8, sgidx))
            sgidx += 1
        out.append(rows)
    return out


SGROUPS = _sgroups()
LASTUSE = [{b: max(r[5] for r in rows if r[4] == b)
            for b in set(r[4] for r in rows)} for rows in SGROUPS]
BUILD_PHASES = 6
B_STAGES = 4
B_HMAX = True
B_SIMPLE_EVAC = False
SKIP_A = False
SKIP_AR1 = False
SKIP_P = False
DEBUG_TAPS = False


def build_nc():
    nc = bass.Bass("TRN2", target_bir_lowering=False,
                   detect_race_conditions=False)

    x_h = nc.dram_tensor("x", [T, D], F32, kind="ExternalInput")
    wq_h = nc.dram_tensor("wq", [D, D], F32, kind="ExternalInput")
    wk_h = nc.dram_tensor("wk", [D, D], F32, kind="ExternalInput")
    wv_h = nc.dram_tensor("wv", [D, D], F32, kind="ExternalInput")
    bq_h = nc.dram_tensor("bqr", [128, 8], F32, kind="ExternalInput")
    bk_h = nc.dram_tensor("bkr", [128, 8], F32, kind="ExternalInput")
    bv_h = nc.dram_tensor("bvr", [1, D], F32, kind="ExternalInput")
    w1_h = nc.dram_tensor("w1", [FF, D], F32, kind="ExternalInput")
    b1_h = nc.dram_tensor("b1r", [128, 32], F32, kind="ExternalInput")
    w2_h = nc.dram_tensor("w2", [D, FF], F32, kind="ExternalInput")
    b2_h = nc.dram_tensor("b2r", [1, D], F32, kind="ExternalInput")
    mask_h = nc.dram_tensor("mask", [128, 128], F16, kind="ExternalInput")
    cnt_h = nc.dram_tensor("cnt", [128, 32], F32, kind="ExternalInput")
    ones_h = nc.dram_tensor("onesf", [1, 128], F32, kind="ExternalInput")
    out_h = nc.dram_tensor("out", [T, D], F32, kind="ExternalOutput")

    KI = "ExternalOutput" if DEBUG_TAPS else "Internal"
    dbg_h = nc.dram_tensor("dbg", [128, 64], F32, kind=KI)
    y16_h = nc.dram_tensor("y16", [T, D], F16, kind=KI)
    o16_h = nc.dram_tensor("o16", [T, D], F16, kind=KI)
    w1q_h = nc.dram_tensor("w1qs", [FF, D], F16, kind=KI)
    w2q_h = nc.dram_tensor("w2qs", [D, FF], F16, kind=KI)
    h16_h = nc.dram_tensor("h16", [32, 16, 128, 512], F16, kind=KI)
    red1_h = nc.dram_tensor("red1", [1, 128], F32, kind="Internal")
    red2_h = nc.dram_tensor("red2", [1, 128], F32, kind="Internal")
    sw2d_h = nc.dram_tensor("sw2d", [1, D], F32, kind=KI)
    cc1i_h = nc.dram_tensor("cc1i", [128, 4], F32, kind="Internal")
    cc1o_h = nc.dram_tensor("cc1o", [128, 4], F32, kind="Internal")
    cc2i_h = nc.dram_tensor("cc2i", [128, 4], F32, kind="Internal")
    cc2o_h = nc.dram_tensor("cc2o", [128, 4], F32, kind="Internal")

    with ExitStack() as outer:
        def osb(*a):
            return outer.enter_context(nc.sbuf_tensor(*a))

        def sem(name):
            return outer.enter_context(nc.semaphore(name))

        b1_sb = osb("b1_sb", [128, 32], F32)
        sw1_sb = osb("sw1_sb", [128, 32], F32)
        scale1_sb = osb("scale1_sb", [128, 32], F32)
        b1q_sb = osb("b1q_sb", [128, 32], F32)
        sw2_sb = osb("sw2_sb", [128, 8], F32)
        ones_sb = osb("ones_sb", [1, 128], F32)
        ymax_sb = osb("ymax_sb", [128, 2], F32)
        hmax_sb = osb("hmax_sb", [128, 2], F32)
        s1b_sb = osb("s1b_sb", [128, 1], F32)
        r1b_sb = osb("r1b_sb", [128, 1], F32)
        s2b_sb = osb("s2b_sb", [128, 1], F32)
        r2b_sb = osb("r2b_sb", [128, 1], F32)
        g1_sb = osb("g1_sb", [1, 1], F32)
        g2_sb = osb("g2_sb", [1, 1], F32)
        s1v_sb = osb("s1v_sb", [1, 1], F32)
        s2v_sb = osb("s2v_sb", [1, 1], F32)
        redt_sb = osb("redt_sb", [1, 128], F32)
        dbg_sb = osb("dbg_sb", [128, 64], F32)
        cc_sb = osb("cc_sb", [128, 4], F32)

        pLD = sem("pLD"); pW = sem("pW"); pWF = sem("pWF"); pQS = sem("pQS")
        pST = sem("pST"); pSLF = sem("pSLF"); pMM = sem("pMM")
        pLDg = sem("pLDg"); r1T = sem("r1T")
        pW0 = sem("pW0"); pW1 = sem("pW1")
        bHS0 = sem("bHS0"); bHS1 = sem("bHS1"); bHS2 = sem("bHS2"); bHS3 = sem("bHS3")
        aXL0 = sem("aXL0"); aXL1 = sem("aXL1"); aX20 = sem("aX20")
        aX21 = sem("aX21"); aLN = sem("aLN"); aTR0 = sem("aTR0")
        aTR1 = sem("aTR1"); aPJ = sem("aPJ"); aQK = sem("aQK")
        aVE = sem("aVE"); aSG = sem("aSG"); aMN = sem("aMN")
        aES = sem("aES"); aEM = sem("aEM"); aO1 = sem("aO1")
        aOS0 = sem("aOS0"); aOS1 = sem("aOS1"); aY = sem("aY")
        aYS0 = sem("aYS0"); aYS1 = sem("aYS1"); aSQ1 = sem("aSQ1")
        aSQ2 = sem("aSQ2"); aRSTD1 = sem("aRSTD1"); aRSTD2 = sem("aRSTD2")
        aVAR1 = sem("aVAR1"); aVAR2 = sem("aVAR2"); aXLF = sem("aXLF")
        aX2F = sem("aX2F"); aSLF = sem("aSLF")
        r1S = sem("r1S"); r1C = sem("r1C"); r1D = sem("r1D"); r1P = sem("r1P")
        r1Cc = sem("r1Cc"); r2Cc = sem("r2Cc")
        bYL = sem("bYL"); bYQ = sem("bYQ"); bXQ = sem("bXQ"); bXF = sem("bXF")
        bHM = sem("bHM"); bHE = sem("bHE"); bHS = sem("bHS"); bHR = sem("bHR")
        r2S = sem("r2S"); r2C = sem("r2C"); r2D = sem("r2D"); r2P = sem("r2P")
        cHL = sem("cHL"); cHF = sem("cHF"); cOL = sem("cOL"); cQ1 = sem("cQ1")
        cQ2 = sem("cQ2"); cMM = sem("cMM"); cOE = sem("cOE"); cFO = sem("cFO")
        cOS = sem("cOS")
        vBAR = sem("vBAR"); sBAR = sem("sBAR")

        _vb = [0]
        _sb_ = [0]

        def vbar(ins=None):
            if ins is None:
                ins = nc.vector.engine_nop()
            _vb[0] += 1
            ins.then_inc(vBAR, 1)
            nc.vector.wait_ge(vBAR, _vb[0])

        def sbar(ins):
            _sb_[0] += 1
            ins.then_inc(sBAR, 1)
            nc.scalar.wait_ge(sBAR, _sb_[0])

        pa = outer.enter_context(ExitStack())

        def pasb(*a):
            return pa.enter_context(nc.sbuf_tensor(*a))
        wq_sb = pasb("wq_sb", [128, 8 * D], F16)
        wk_sb = pasb("wk_sb", [128, 8 * D], F16)
        wv_sb = pasb("wv_sb", [128, 8 * D], F16)
        bq_sb = pasb("bq_sb", [128, 8], F32)
        bk_sb = pasb("bk_sb", [128, 8], F32)
        mask_sb = pasb("mask_sb", [128, 128], F16)
        cnt_sb = pasb("cnt_sb", [128, 32], F32)
        eps_sb = pasb("eps_sb", [128, 1], F32)
        bv_bc = pasb("bv_bc", [128, D], F16)

        with ExitStack() as pstack:
            def psb(*a):
                return pstack.enter_context(nc.sbuf_tensor(*a))
            wbuf = [psb(f"wbuf{i}", [128, D], F32) for i in range(2)]
            w2buf = [psb(f"w2buf{i}", [128, FF], F32) for i in range(2)]
            i8b = psb("i8b", [128, D], I8)
            i8b2 = psb("i8b2", [128, FF], I8)
            w1qf = [psb(f"w1qf{i}", [128, D], F16) for i in range(2)]
            w2qf = [psb(f"w2qf{i}", [128, FF], F16) for i in range(2)]
            rw_sb = psb("rw_sb", [128, 1], F32)
            rw_in = psb("rw_in", [128, 1], F32)
            bvrow_sb = psb("bvrow_sb", [1, D], F32)
            psB = [pstack.enter_context(nc.psum_tensor(f"psB{i}", [128, 512], F32))
                   for i in range(2)]

            pblk = pstack.enter_context(nc.Block())

            @pblk.sync
            def _(sync):
                if SKIP_P:
                    return
                sync.dma_start(out=bq_sb[:, :], in_=bq_h[:, :]).then_inc(pLD, 16)
                sync.dma_start(out=bk_sb[:, :], in_=bk_h[:, :]).then_inc(pLD, 16)
                sync.dma_start(out=b1_sb[:, :], in_=b1_h[:, :]).then_inc(pLD, 16)
                sync.dma_start(out=mask_sb[:, :], in_=mask_h[:, :]).then_inc(pLD, 16)
                sync.dma_start(out=cnt_sb[:, :], in_=cnt_h[:, :]).then_inc(pLD, 16)
                sync.dma_start(out=ones_sb[:, :], in_=ones_h[:, :]).then_inc(pLD, 16)
                sync.dma_start(out=bvrow_sb[:, :], in_=bv_h[:, :]).then_inc(pLD, 16)
                for it in range(42):
                    if it < 40:
                        if it >= 2:
                            sync.wait_ge(pWF, it - 1)
                        if it < 32:
                            sync.dma_start(out=wbuf[it % 2][:, :],
                                           in_=w1_h[it * 128:(it + 1) * 128, :]
                                           ).then_inc(pW1 if it % 2 else pW0, 16)
                        else:
                            dc = it - 32
                            sync.dma_start(out=w2buf[dc % 2][:, :],
                                           in_=w2_h[dc * 128:(dc + 1) * 128, :]
                                           ).then_inc(pW1 if it % 2 else pW0, 16)
                    j = it - 2
                    if j >= 0:
                        sync.wait_ge(pQS, j + 1)
                        if j < 32:
                            sync.dma_start(out=w1q_h[j * 128:(j + 1) * 128, :],
                                           in_=w1qf[j % 2][:, :]).then_inc(pST, 16)
                        else:
                            dc = j - 32
                            sync.dma_start(out=w2q_h[dc * 128:(dc + 1) * 128, :],
                                           in_=w2qf[dc % 2][:, :]).then_inc(pST, 16)
                            sync.dma_start(out=sw2d_h[0:1, dc * 128:(dc + 1) * 128],
                                           in_=sw2_sb[:, dc:dc + 1]).then_inc(pST, 16)

            @pblk.gpsimd
            def _(gpsimd):
                if SKIP_P:
                    return
                for wh, wsb in ((wq_h, wq_sb), (wk_h, wk_sb), (wv_h, wv_sb)):
                    for c in range(8):
                        gpsimd.dma_start(
                            out=wsb[:, c * D:(c + 1) * D],
                            in_=wh[c * 128:(c + 1) * 128, :]).then_inc(pLDg, 16)
                gpsimd.wait_ge(pLDg, 16 * 24)

            @pblk.tensor
            def _(tensor):
                if SKIP_P:
                    return
                tensor.wait_ge(pLD, 16 * 7)
                tensor.matmul(psB[0][:, :], ones_sb[:, :], bvrow_sb[0:1, 0:512],
                              start=True, stop=True).then_inc(pMM, 1)
                tensor.matmul(psB[1][:, :], ones_sb[:, :], bvrow_sb[0:1, 512:1024],
                              start=True, stop=True).then_inc(pMM, 1)

            @pblk.vector
            def _(vector):
                if SKIP_P:
                    return
                vector.memset(eps_sb[:, :], EPS_LN)
                vector.memset(ymax_sb[:, :], 0.0)
                vector.memset(hmax_sb[:, :], 0.0)
                vector.wait_ge(pMM, 2)
                vector.tensor_copy(bv_bc[:, 0:512], psB[0][:, :])
                vector.tensor_copy(bv_bc[:, 512:1024], psB[1][:, :])
                selfc = 0
                for i in range(40):
                    is1 = i < 32
                    buf = wbuf[i % 2] if is1 else w2buf[(i - 32) % 2]
                    i8t = i8b if is1 else i8b2
                    qf = w1qf[i % 2] if is1 else w2qf[(i - 32) % 2]
                    scol = (sw1_sb[:, i:i + 1] if is1
                            else sw2_sb[:, i - 32:i - 31])
                    vector.wait_ge(pW1 if i % 2 else pW0, 16 * (i // 2 + 1))
                    vector.reduce_max(scol, buf[:, :], AX, apply_absolute_value=True)
                    vbar()
                    selfc += 1
                    vector.reciprocal(rw_sb[:, :], scol).then_inc(pSLF, 1)
                    nc.vector.wait_ge(pSLF, selfc)
                    vector.tensor_scalar(i8t[:, :], buf[:, :], rw_sb[:, :],
                                         127.0, ALU.mult,
                                         ALU.mult).then_inc(pWF, 1)
                    nc.vector.wait_ge(pWF, i + 1)
                    if DEBUG_TAPS:
                        vector.tensor_copy(dbg_sb[:, i:i + 1], rw_sb[:, :])
                        if i < 8:
                            vbar(vector.tensor_copy(dbg_sb[:, 40 + i:41 + i],
                                                    scol))
                            vbar(vector.tensor_copy(dbg_sb[:, 48 + i:49 + i],
                                                    buf[:, 0:1]))
                    if i >= 2:
                        j = i - 1
                        pst_need = 16 * (j + 1) if j < 32 else 512 + 32 * (j - 31)
                        vector.wait_ge(pST, pst_need)
                    vector.tensor_copy(qf[:, :], i8t[:, :]).then_inc(pQS, 1)

        if BUILD_PHASES < 2:
            nc.finalize()
            return nc
        with ExitStack() as astack:
            if SKIP_A:
                pass
            def asb(*a):
                return astack.enter_context(nc.sbuf_tensor(*a))
            K_sb = asb("K_sb", [128, 8 * S], F16)
            xa = [asb(f"xa{i}", [128, D], F32) for i in range(2)]
            x2b = [asb(f"x2b{i}", [128, D], F32) for i in range(2)]
            xn_tm = asb("xn_tm", [128, 4 * D], F16)
            xn_fm = [asb(f"xn_fm{i}", [128, 8 * 512], F16) for i in range(2)]
            q_fm = asb("q_fm", [128, 8 * 512], F16)
            v_sb = asb("v_sb", [128, 4 * D], F16)
            out1b = [asb(f"out1b{i}", [128, D], F16) for i in range(2)]
            ybuf = [asb(f"ybuf{i}", [128, D], F16) for i in range(2)]
            e_dump = asb("e_dump", [128, D], F16)
            masked_sb = asb("masked_sb", [128, 128], F32)
            msum_sb = asb("msum_sb", [128, 2], F32)
            m1_sb = asb("m1_sb", [128, 2], F32)
            mm_sb = asb("mm_sb", [128, 2], F32)
            v1_sb = asb("v1_sb", [128, 2], F32)
            sqacc_sb = asb("sqacc_sb", [128, 2], F32)
            lnt1_sb = asb("lnt1_sb", [128, 2], F32)
            rstd_sb = asb("rstd_sb", [128, 2], F32)
            msum2_sb = asb("msum2_sb", [128, 2], F32)
            m2_sb = asb("m2_sb", [128, 2], F32)
            mm2_sb = asb("mm2_sb", [128, 2], F32)
            v2_sb = asb("v2_sb", [128, 2], F32)
            sqacc2_sb = asb("sqacc2_sb", [128, 2], F32)
            lnt2_sb = asb("lnt2_sb", [128, 2], F32)
            rstd2_sb = asb("rstd2_sb", [128, 2], F32)
            mrun_sb = [asb(f"mrun{i}", [128, 1], F32) for i in range(2)]
            mneg_sb = [asb(f"mneg{i}", [128, 1], F32) for i in range(2)]
            delta_sb = [asb(f"delta{i}", [128, 1], F32) for i in range(2)]
            srun_sb = [asb(f"srun{i}", [128, 1], F32) for i in range(2)]
            esum_sb = [asb(f"esum{i}", [128, 1], F32) for i in range(2)]
            resc_sb = [asb(f"resc{i}", [128, 1], F32) for i in range(2)]
            gm_sb = asb("gm_sb", [128, 1], F32)
            den_sb = asb("den_sb", [128, 1], F32)
            rden_sb = asb("rden_sb", [128, 1], F32)
            rs_sb = asb("rs_sb", [128, 1], F32)
            ymt_sb = asb("ymt_sb", [128, 1], F32)
            ps = [astack.enter_context(
                nc.psum_tensor(f"psA{i}", [128, 512], F32)) for i in range(8)]

            ablk = astack.enter_context(nc.Block())

            NSUB = 4 * NST

            def xl_sem(k):
                return aXL0 if k % 2 == 0 else aXL1

            def x2_sem(k):
                return aX20 if k % 2 == 0 else aX21

            def os_sem(k):
                return aOS0 if k % 2 == 0 else aOS1

            def ys_sem(k):
                return aYS0 if k % 2 == 0 else aYS1

            def tr_sem(stx):
                return aTR0 if stx % 2 == 0 else aTR1

            @ablk.sync
            def _(sync):
                if SKIP_A:
                    return
                for j in range(2):
                    sync.dma_start(out=xa[j][:, :],
                                   in_=x_h[j * 128:(j + 1) * 128, :]
                                   ).then_inc(xl_sem(j), 16)
                    sync.dma_start(out=x2b[j][:, :],
                                   in_=x_h[j * 128:(j + 1) * 128, :]
                                   ).then_inc(x2_sem(j), 16)
                for j in range(2, 6):
                    sync.wait_ge(aXLF, j - 1)
                    sync.wait_ge(aSQ1, j - 1)
                    sync.dma_start(out=xa[j % 2][:, :],
                                   in_=x_h[j * 128:(j + 1) * 128, :]
                                   ).then_inc(xl_sem(j), 16)
                for p in range(NSUB):
                    j2 = p + 2
                    if j2 < NSUB:
                        sync.wait_ge(aX2F, p + 1)
                        sync.dma_start(out=x2b[j2 % 2][:, :],
                                       in_=x_h[j2 * 128:(j2 + 1) * 128, :]
                                       ).then_inc(x2_sem(j2), 16)
                    ja = p + 6
                    if ja < NSUB:
                        sync.wait_ge(aXLF, ja - 1)
                        sync.wait_ge(aSQ1, ja - 1)
                        sync.dma_start(out=xa[ja % 2][:, :],
                                       in_=x_h[ja * 128:(ja + 1) * 128, :]
                                       ).then_inc(xl_sem(ja), 16)

            @ablk.gpsimd
            def _(gpsimd):
                if SKIP_A:
                    return

            @ablk.scalar
            def _(scalar):
                if SKIP_A:
                    return

                def fe_sc(kN):
                    p = kN % 2
                    stN = kN // 4
                    ttn = kN % 4
                    scalar.wait_ge(xl_sem(kN), 16 * (kN // 2 + 1))
                    scalar.activation(e_dump[:, 0:D], xa[p][:, :], ACT.Square,
                                      accum_out=sqacc_sb[:, p:p + 1]
                                      ).then_inc(aSQ1, 1)
                    scalar.wait_ge(aVAR1, kN + 1)
                    sbar(scalar.activation(lnt1_sb[:, p:p + 1],
                                           v1_sb[:, p:p + 1], ACT.Ln,
                                           bias=eps_sb[:, :], scale=1.0))
                    scalar.activation(rstd_sb[:, p:p + 1], lnt1_sb[:, p:p + 1],
                                      ACT.Exp, scale=-0.5).then_inc(aRSTD1, 1)
                    scalar.wait_ge(aLN, kN + 1)
                    if stN >= 2:
                        scalar.wait_ge(aPJ, 24 * (stN - 1))
                    buf = xn_fm[stN % 2]
                    for c in range(8):
                        scalar.dma_start_transpose(
                            buf[:, c * 512 + ttn * 128: c * 512 + (ttn + 1) * 128],
                            xn_tm[:, ttn * D + c * 128: ttn * D + (c + 1) * 128]
                        ).then_inc(tr_sem(stN), 16)

                for j in range(4):
                    fe_sc(j)
                for st in range(NST):
                    sl = st % SPB
                    for u in range(16):
                        scalar.wait_ge(aPJ, 24 * st + u + 1)
                        bank = ps[u % 4]
                        if u < 8:
                            dest = q_fm[:, u * 512:(u + 1) * 512]
                            bias = bq_sb[:, u:u + 1]
                        else:
                            c = u - 8
                            dest = K_sb[:, c * S + sl * 512: c * S + (sl + 1) * 512]
                            bias = bk_sb[:, c:c + 1]
                        scalar.activation(dest, bank[:, :], ACT.Identity,
                                          bias=bias, scale=1.0).then_inc(aQK, 1)
                    grp_by_tt = {}
                    for rec in SGROUPS[st]:
                        grp_by_tt.setdefault(rec[0], []).append(rec)
                    for tt in range(4):
                        k = 4 * st + tt
                        kp = k % 2
                        for (_tt, g, ng, is_diag, bank, sgidx) in grp_by_tt[tt]:
                            scalar.wait_ge(aMN, sgidx + 1)
                            par = g % 2
                            scalar.activation(resc_sb[par][:, :],
                                              delta_sb[par][:, :],
                                              ACT.Exp, scale=1.0 / 32.0)
                            src = (masked_sb[:, 0:128] if is_diag
                                   else ps[bank][:, 0:ng])
                            scalar.activation(e_dump[:, 0:ng], src, ACT.Exp,
                                              scale=1.0 / 32.0,
                                              bias=mneg_sb[par][:, :],
                                              accum_out=esum_sb[par][:, :]
                                              ).then_inc(aES, 1)
                            if is_diag:
                                scalar.activation(resc_sb[(g + 1) % 2][:, :],
                                                  mneg_sb[par][:, :], ACT.Exp,
                                                  scale=1.0).then_inc(aEM, 1)
                        scalar.wait_ge(aO1, k + 1)
                        scalar.activation(e_dump[:, 0:D], out1b[kp][:, :],
                                          ACT.Square,
                                          accum_out=sqacc2_sb[:, kp:kp + 1]
                                          ).then_inc(aSQ2, 1)
                        scalar.dma_start(out=o16_h[k * 128:(k + 1) * 128, :],
                                         in_=out1b[kp][:, :]
                                         ).then_inc(os_sem(k), 16)
                        scalar.wait_ge(aVAR2, k + 1)
                        sbar(scalar.activation(lnt2_sb[:, kp:kp + 1],
                                               v2_sb[:, kp:kp + 1], ACT.Ln,
                                               bias=eps_sb[:, :], scale=1.0))
                        scalar.activation(rstd2_sb[:, kp:kp + 1],
                                          lnt2_sb[:, kp:kp + 1], ACT.Exp,
                                          scale=-0.5).then_inc(aRSTD2, 1)
                        if k >= 1:
                            scalar.wait_ge(aY, k)
                            scalar.dma_start(
                                out=y16_h[(k - 1) * 128:k * 128, :],
                                in_=ybuf[(k - 1) % 2][:, :]
                            ).then_inc(ys_sem(k - 1), 16)
                        if k + 4 < NSUB:
                            fe_sc(k + 4)
                scalar.wait_ge(aY, NSUB)
                scalar.dma_start(out=y16_h[(NSUB - 1) * 128:NSUB * 128, :],
                                 in_=ybuf[(NSUB - 1) % 2][:, :]
                                 ).then_inc(ys_sem(NSUB - 1), 16)
                scalar.wait_ge(aOS0, 16 * (NSUB // 2))
                scalar.wait_ge(aOS1, 16 * (NSUB // 2))
                scalar.wait_ge(aYS0, 16 * (NSUB // 2))
                scalar.wait_ge(aYS1, 16 * (NSUB // 2))

            @ablk.tensor
            def _(tensor):
                if SKIP_A:
                    return
                for st in range(NST):
                    sl = st % SPB
                    buf = xn_fm[st % 2]
                    tensor.wait_ge(tr_sem(st), 512 * (st // 2 + 1))
                    for u in range(24):
                        pu = 24 * st + u
                        if u < 4:
                            if st >= 1 and u in LASTUSE[st - 1]:
                                tensor.wait_ge(aES, LASTUSE[st - 1][u] + 1)
                        elif pu >= 4:
                            prev = pu - 4
                            pm = prev % 24
                            if pm < 16:
                                tensor.wait_ge(aQK, 16 * (prev // 24) + pm + 1)
                            else:
                                tensor.wait_ge(aVE, 8 * (prev // 24) + pm - 16 + 1)
                        bank = ps[u % 4]
                        if u < 8:
                            for c in range(8):
                                ins = tensor.matmul(
                                    bank[:, :],
                                    wq_sb[:, c * D + u * 128: c * D + (u + 1) * 128],
                                    buf[:, c * 512:(c + 1) * 512],
                                    start=(c == 0), stop=(c == 7))
                        elif u < 16:
                            co = u - 8
                            for c in range(8):
                                ins = tensor.matmul(
                                    bank[:, :],
                                    wk_sb[:, c * D + co * 128: c * D + (co + 1) * 128],
                                    buf[:, c * 512:(c + 1) * 512],
                                    start=(c == 0), stop=(c == 7))
                        else:
                            tt = (u - 16) // 2
                            half = (u - 16) % 2
                            for c in range(8):
                                ins = tensor.matmul(
                                    bank[:, :],
                                    buf[:, c * 512 + tt * 128: c * 512 + (tt + 1) * 128],
                                    wv_sb[:, c * D + half * 512: c * D + half * 512 + 512],
                                    start=(c == 0), stop=(c == 7))
                        ins.then_inc(aPJ, 1)
                    tensor.wait_ge(aQK, 16 * (st + 1))
                    seen = set()
                    for (tt, g, ng, is_diag, bank, sgidx) in SGROUPS[st]:
                        qt = sl * 4 + tt
                        if bank < 4 and bank not in seen:
                            seen.add(bank)
                            tensor.wait_ge(aVE, 8 * st + bank + 5)
                        if sgidx >= 8:
                            tensor.wait_ge(aES, sgidx - 7)
                        k0 = qt * 128 if is_diag else g * 512
                        for c in range(8):
                            ins = tensor.matmul(
                                ps[bank][:, 0:ng],
                                q_fm[:, c * 512 + tt * 128: c * 512 + (tt + 1) * 128],
                                K_sb[:, c * S + k0: c * S + k0 + ng],
                                start=(c == 0), stop=(c == 7))
                        ins.then_inc(aSG, 1)

            @ablk.vector
            def _(vector):
                if SKIP_A:
                    return
                slf = [0]

                def self_wait(ins):
                    slf[0] += 1
                    ins.then_inc(aSLF, 1)
                    nc.vector.wait_ge(aSLF, slf[0])

                def fe_dve(kN):
                    p = kN % 2
                    vector.wait_ge(xl_sem(kN), 16 * (kN // 2 + 1))
                    vector.reduce_sum(msum_sb[:, p:p + 1], xa[p][:, :], AX)
                    vbar()
                    vbar(vector.tensor_scalar(m1_sb[:, p:p + 1],
                                              msum_sb[:, p:p + 1],
                                              1.0 / D, None, ALU.mult))
                    vector.wait_ge(aSQ1, kN + 1)
                    vbar(vector.tensor_mul(mm_sb[:, p:p + 1], m1_sb[:, p:p + 1],
                                           m1_sb[:, p:p + 1]))
                    vector.scalar_tensor_tensor(
                        v1_sb[:, p:p + 1], sqacc_sb[:, p:p + 1], 1.0 / D,
                        mm_sb[:, p:p + 1], ALU.mult,
                        ALU.subtract).then_inc(aVAR1, 1)
                    nc.vector.wait_ge(aVAR1, kN + 1)
                    vector.wait_ge(aRSTD1, kN + 1)
                    if kN >= 4:
                        stp = (kN - 4) // 4
                        vector.wait_ge(tr_sem(stp), 512 * (stp // 2 + 1))
                    ttn = kN % 4
                    vector.tensor_scalar(xn_tm[:, ttn * D:(ttn + 1) * D],
                                         xa[p][:, :], m1_sb[:, p:p + 1],
                                         rstd_sb[:, p:p + 1],
                                         ALU.subtract, ALU.mult)
                    nc.vector.engine_nop().then_inc(aLN, 1)
                    nc.vector.engine_nop().then_inc(aXLF, 1)

                for j in range(4):
                    fe_dve(j)
                for st in range(NST):
                    sl = st % SPB
                    for u in range(8):
                        vector.wait_ge(aPJ, 24 * st + 16 + u + 1)
                        bank = ps[(16 + u) % 4]
                        tt = u // 2
                        half = u % 2
                        dst = v_sb[:, tt * D + half * 512: tt * D + half * 512 + 512]
                        vector.tensor_add(dst, bank[:, :],
                                          bv_bc[:, half * 512: half * 512 + 512]
                                          ).then_inc(aVE, 1)
                    grp_by_tt = {}
                    for rec in SGROUPS[st]:
                        grp_by_tt.setdefault(rec[0], []).append(rec)
                    for tt in range(4):
                        k = 4 * st + tt
                        kp = k % 2
                        qt = sl * 4 + tt
                        recs = grp_by_tt[tt]
                        vector.memset(mrun_sb[0][:, :], 0.0)
                        vector.memset(srun_sb[0][:, :], 0.0)
                        for (_tt, g, ng, is_diag, bank, sgidx) in recs:
                            par = g % 2
                            nxt = (g + 1) % 2
                            vector.wait_ge(aSG, sgidx + 1)
                            if is_diag:
                                vbar(vector.tensor_add(masked_sb[:, :],
                                                       ps[bank][:, 0:128],
                                                       mask_sb[:, :]))
                                src = masked_sb[:, :]
                            else:
                                src = ps[bank][:, 0:ng]
                            vector.reduce_max(gm_sb[:, :], src, AX)
                            vbar()
                            vbar(vector.tensor_max(mrun_sb[nxt][:, :],
                                                   mrun_sb[par][:, :],
                                                   gm_sb[:, :]))
                            vector.tensor_sub(delta_sb[par][:, :],
                                              mrun_sb[par][:, :], mrun_sb[nxt][:, :])
                            vector.tensor_scalar(mneg_sb[par][:, :],
                                                 mrun_sb[nxt][:, :], -1.0 / 32.0,
                                                 None, ALU.mult).then_inc(aMN, 1)
                            vector.wait_ge(aES, sgidx + 1)
                            vector.scalar_tensor_tensor(
                                srun_sb[nxt][:, :], srun_sb[par][:, :],
                                resc_sb[par][:, :], esum_sb[par][:, :],
                                ALU.mult, ALU.add)
                        lastpar = len(recs) % 2
                        vector.wait_ge(aEM, 4 * st + tt + 1)
                        vbar(vector.scalar_tensor_tensor(
                            den_sb[:, :], resc_sb[lastpar][:, :],
                            cnt_sb[:, qt:qt + 1], srun_sb[lastpar][:, :],
                            ALU.mult, ALU.add))
                        vbar(vector.reciprocal(rden_sb[:, :], den_sb[:, :]))
                        self_wait(vector.tensor_mul(rs_sb[:, :], den_sb[:, :],
                                                    rden_sb[:, :]))
                        vector.wait_ge(x2_sem(k), 16 * (k // 2 + 1))
                        if k >= 2:
                            vector.wait_ge(os_sem(k), 16 * (k // 2))
                        vector.scalar_tensor_tensor(
                            out1b[kp][:, :],
                            v_sb[:, tt * D:(tt + 1) * D], rs_sb[:, :],
                            x2b[kp][:, :], ALU.mult, ALU.add)
                        nc.vector.engine_nop().then_inc(aO1, 1)
                        nc.vector.engine_nop().then_inc(aX2F, 1)
                        vbar()
                        vector.reduce_sum(msum2_sb[:, kp:kp + 1],
                                          out1b[kp][:, :], AX)
                        vbar()
                        vbar(vector.tensor_scalar(m2_sb[:, kp:kp + 1],
                                                  msum2_sb[:, kp:kp + 1],
                                                  1.0 / D, None, ALU.mult))
                        vector.wait_ge(aSQ2, k + 1)
                        vbar(vector.tensor_mul(mm2_sb[:, kp:kp + 1],
                                               m2_sb[:, kp:kp + 1],
                                               m2_sb[:, kp:kp + 1]))
                        vector.scalar_tensor_tensor(
                            v2_sb[:, kp:kp + 1], sqacc2_sb[:, kp:kp + 1],
                            1.0 / D, mm2_sb[:, kp:kp + 1], ALU.mult,
                            ALU.subtract).then_inc(aVAR2, 1)
                        nc.vector.wait_ge(aVAR2, k + 1)
                        if k + 4 < NSUB:
                            fe_dve(k + 4)
                        vector.wait_ge(aRSTD2, k + 1)
                        if k >= 2:
                            vector.wait_ge(ys_sem(k), 16 * (k // 2))
                        vector.tensor_scalar(ybuf[kp][:, :], out1b[kp][:, :],
                                             m2_sb[:, kp:kp + 1],
                                             rstd2_sb[:, kp:kp + 1],
                                             ALU.subtract, ALU.mult)
                        nc.vector.engine_nop().then_inc(aY, 1)
                        vbar()
                        vector.reduce_max(ymt_sb[:, :], ybuf[kp][:, :], AX)
                        vbar()
                        vbar(vector.tensor_max(
                            ymax_sb[:, (k + 1) % 2:(k + 1) % 2 + 1],
                            ymax_sb[:, k % 2:k % 2 + 1], ymt_sb[:, :]))

        pa.close()
        if BUILD_PHASES < 3:
            nc.finalize()
            return nc

        with ExitStack() as ccom:
            def csb(*a):
                return ccom.enter_context(nc.sbuf_tensor(*a))
            w2qT_sb = csb("w2qT_sb", [128, 32 * D], F16)
            prstack = ExitStack()
            psR = [prstack.enter_context(nc.psum_tensor(f"psR{i}", [128, 512], F32))
                   for i in range(2)]

            with ExitStack() as bstack:
                def bsb(*a):
                    return bstack.enter_context(nc.sbuf_tensor(*a))
                w1qT_sb = bsb("w1qT_sb", [128, 8 * FF], F16)
                y_fm = [bsb(f"y_fm{i}", [128, 8 * 512], F16) for i in range(2)]
                xq_sb = [bsb(f"xq{i}", [128, 8 * 512], F16) for i in range(2)]
                yq_u8 = bsb("yq_u8", [128, 512], U8)
                h_tile = [bsb(f"h_tile{i}", [128, 512], F16) for i in range(4)]
                hmc_sb = bsb("hmc_sb", [128, 32], F32)
                hmt_sb = bsb("hmt_sb", [128, 1], F32)
                b1i_sb = bsb("b1i_sb", [128, 32], I32)
                b1f_sb = bsb("b1f_sb", [128, 32], F32)
                rcs1_sb = bsb("rcs1_sb", [128, 32], F32)
                ps_b = [bstack.enter_context(
                    nc.psum_tensor(f"psHB{i}", [128, 512], F32)) for i in range(4)]

                ar1blk = nc.Block()
                ar1blk.__enter__()

                @ar1blk.sync
                def _(sync):
                    if SKIP_AR1:
                        return
                    ymfin = ymax_sb[:, NST % 2:NST % 2 + 1]
                    sync.dma_start(out=red1_h[:, :], in_=ymfin).then_inc(r1S, 16)
                    sync.wait_ge(r1S, 16)
                    sync.dma_start(out=redt_sb[:, :],
                                   in_=red1_h[:, :]).then_inc(r1S, 16)
                    sync.wait_ge(r1D, 1)
                    sync.dma_start(out=cc1i_h[:, :],
                                   in_=cc_sb[:, :]).then_inc(r1S, 16)
                    sync.wait_ge(r1Cc, 1)
                    sync.dma_start(out=g1_sb[:, :],
                                   in_=cc1o_h[0:1, 0:1]).then_inc(r1C, 16)

                @ar1blk.vector
                def _(vector):
                    if SKIP_AR1:
                        return
                    vector.wait_ge(r1S, 32)
                    vector.reduce_max(g1_sb[:, :], redt_sb[:, :], AX)
                    vbar()
                    vbar(vector.memset(cc_sb[:, :], 0.0))
                    vector.tensor_copy(cc_sb[0:1, 0:1], g1_sb[:, :]).then_inc(r1D, 1)
                    vector.wait_ge(r1C, 16)
                    vector.tensor_scalar(s1v_sb[:, :], g1_sb[:, :], EPS_Q,
                                         1.0 / 255.0, ALU.max,
                                         ALU.mult).then_inc(r1D, 1)
                    vector.wait_ge(r1P, 1)
                    vector.tensor_copy(s1b_sb[:, :],
                                       psR[0][:, 0:1]).then_inc(r1D, 1)
                    nc.vector.wait_ge(r1D, 3)
                    vector.reciprocal(r1b_sb[:, :], s1b_sb[:, :])
                    vector.tensor_scalar(scale1_sb[:, :], sw1_sb[:, :],
                                         s1b_sb[:, :], 1.0 / 127.0, ALU.mult,
                                         ALU.mult).then_inc(r1D, 1)
                    nc.vector.wait_ge(r1D, 4)
                    vbar(vector.reciprocal(rcs1_sb[:, :], scale1_sb[:, :]))
                    vbar(vector.tensor_mul(b1f_sb[:, :], b1_sb[:, :], rcs1_sb[:, :]))
                    vbar(vector.tensor_copy(b1i_sb[:, :], b1f_sb[:, :]))
                    vbar(vector.tensor_copy(b1f_sb[:, :], b1i_sb[:, :]))
                    vbar(vector.tensor_mul(b1q_sb[:, :], b1f_sb[:, :],
                                           scale1_sb[:, :]))

                @ar1blk.gpsimd
                def _(gpsimd):
                    if SKIP_AR1:
                        return
                    gpsimd.wait_ge(r1S, 48)
                    gpsimd.collective_compute(
                        "AllReduce", ALU.max,
                        replica_groups=[list(range(NCORES))],
                        ins=[cc1i_h.ap().opt()],
                        outs=[cc1o_h.ap().opt()]).then_inc(r1Cc, 1)

                @ar1blk.tensor
                def _(tensor):
                    if SKIP_AR1:
                        return
                    tensor.wait_ge(r1D, 2)
                    tensor.matmul(psR[0][:, 0:1], ones_sb[:, :], s1v_sb[:, :],
                                  start=True, stop=True).then_inc(r1P, 1)

                @ar1blk.scalar
                def _(scalar):
                    if SKIP_AR1:
                        return
                    for dc in range(8):
                        scalar.dma_start_transpose(
                            w1qT_sb[:, dc * FF:(dc + 1) * FF],
                            w1q_h[0:FF, dc * 128:(dc + 1) * 128]).then_inc(r1T, 16)
                    for fc in range(32):
                        scalar.dma_start_transpose(
                            w2qT_sb[:, fc * D:(fc + 1) * D],
                            w2q_h[0:D, fc * 128:(fc + 1) * 128]).then_inc(r1T, 16)
                    scalar.wait_ge(r1T, 16 * 40)

                ar1blk.__exit__(None, None, None)
                if BUILD_PHASES < 4:
                    nc.finalize()
                    return nc
                bblk = bstack.enter_context(nc.Block())

                @bblk.vector
                def _(vector):
                    def hmax_group(g):
                        for fc in range(32):
                            u = 32 * g + fc
                            nc.vector.wait_ge(bHE, u + 1)
                            nc.vector.reduce_max(hmc_sb[:, fc:fc + 1],
                                                 h_tile[fc % 4][:, :], AX)
                            nc.vector.engine_nop().then_inc(bHR, 1)
                        nc.vector.wait_ge(bHR, 32 * (g + 1))
                        nc.vector.reduce_max(hmt_sb[:, :], hmc_sb[:, :], AX)
                        vbar()
                        nc.vector.tensor_max(
                            hmax_sb[:, (g + 1) % 2:(g + 1) % 2 + 1],
                            hmax_sb[:, g % 2:g % 2 + 1], hmt_sb[:, :])

                    for g in range(16):
                        vector.wait_ge(bYL, 128 * (g + 1))
                        if g >= 2 and B_STAGES >= 2:
                            vector.wait_ge(bHM, 32 * (g - 1))
                        for c in range(8):
                            sl_ = slice(c * 512, (c + 1) * 512)
                            vbar(vector.tensor_scalar(yq_u8[:, :],
                                                      y_fm[g % 2][:, sl_],
                                                      r1b_sb[:, :], None,
                                                      ALU.mult))
                            vector.tensor_copy(xq_sb[g % 2][:, sl_], yq_u8[:, :])
                        nc.vector.engine_nop().then_inc(bYQ, 1)
                        nc.vector.engine_nop().then_inc(bXQ, 1)
                        if g >= 1 and B_STAGES >= 3 and B_HMAX:
                            hmax_group(g - 1)
                    if B_STAGES >= 3 and B_HMAX:
                        hmax_group(15)

                @bblk.tensor
                def _(tensor):
                    if B_STAGES < 2:
                        return
                    for g in range(16):
                        tensor.wait_ge(bXQ, g + 1)
                        for fc in range(32):
                            u = 32 * g + fc
                            if u >= 4 and B_STAGES >= 3:
                                tensor.wait_ge(bHE, u - 3)
                            for dc in range(8):
                                ins = tensor.matmul(
                                    ps_b[fc % 4][:, :],
                                    w1qT_sb[:, dc * FF + fc * 128:
                                            dc * FF + (fc + 1) * 128],
                                    xq_sb[g % 2][:, dc * 512:(dc + 1) * 512],
                                    start=(dc == 0), stop=(dc == 7))
                            ins.then_inc(bHM, 1)

                @bblk.scalar
                def _(scalar):
                    def yfm_loads(g):
                        if g >= 2:
                            scalar.wait_ge(bYQ, g - 1)
                        for c in range(8):
                            scalar.dma_start_transpose(
                                y_fm[g % 2][:, c * 512:(c + 1) * 512],
                                y16_h[g * 512:(g + 1) * 512,
                                      c * 128:(c + 1) * 128]).then_inc(bYL, 16)

                    def evac_group(g):
                        for fc in range(32):
                            u = 32 * g + fc
                            scalar.wait_ge(bHM, u + 1)
                            if u >= 4:
                                if B_STAGES >= 4:
                                    scalar.wait_ge([bHS0, bHS1, bHS2, bHS3][u % 4],
                                                   16 * (u // 4))
                                if B_HMAX:
                                    scalar.wait_ge(bHR, u - 3)
                            if B_SIMPLE_EVAC:
                                scalar.copy(h_tile[fc % 4][:, :],
                                            ps_b[fc % 4][:, :]).then_inc(bHE, 1)
                            else:
                                scalar.activation(h_tile[fc % 4][:, :],
                                                  ps_b[fc % 4][:, :], ACT.Relu,
                                                  bias=b1q_sb[:, fc:fc + 1],
                                                  scale=scale1_sb[:, fc:fc + 1]
                                                  ).then_inc(bHE, 1)
                            if B_STAGES >= 4:
                                scalar.wait_ge(bHE, u + 1)
                                scalar.dma_start(out=h16_h[fc, g],
                                                 in_=h_tile[fc % 4][:, :]
                                                 ).then_inc([bHS0, bHS1, bHS2, bHS3][u % 4], 16)

                    for g in range(16):
                        yfm_loads(g)
                        if B_STAGES >= 3 and g >= 1:
                            evac_group(g - 1)
                    if B_STAGES >= 3:
                        evac_group(15)
                    if B_STAGES >= 4:
                        for bb in range(4):
                            scalar.wait_ge([bHS0, bHS1, bHS2, bHS3][bb], 16 * 8 * 16)

            if BUILD_PHASES < 5:
                nc.finalize()
                return nc
            with ExitStack() as cstack:
                def c2sb(*a):
                    return cstack.enter_context(nc.sbuf_tensor(*a))
                SCALE2B = c2sb("SCALE2B", [128, D], F32)
                B2QB = c2sb("B2QB", [128, D], F32)

                arstack = cstack.enter_context(ExitStack())

                def arsb(*a):
                    return arstack.enter_context(nc.sbuf_tensor(*a))
                b2row_sb = arsb("b2row_sb", [1, D], F32)
                s2wrow_sb = arsb("s2wrow_sb", [1, D], F32)
                scale2row_sb = arsb("scale2row_sb", [1, D], F32)
                b2qrow_sb = arsb("b2qrow_sb", [1, D], F32)
                rc2row_sb = arsb("rc2row_sb", [1, D], F32)
                tmp1r_sb = arsb("tmp1r_sb", [1, D], F32)
                tmpi1r_sb = arsb("tmpi1r_sb", [1, D], I32)

                ar2blk = nc.Block()
                ar2blk.__enter__()

                @ar2blk.sync
                def _(sync):
                    hmfin = hmax_sb[:, 16 % 2:16 % 2 + 1]
                    sync.dma_start(out=red2_h[:, :], in_=hmfin).then_inc(r2S, 16)
                    sync.wait_ge(r2S, 16)
                    sync.dma_start(out=redt_sb[:, :],
                                   in_=red2_h[:, :]).then_inc(r2S, 16)
                    sync.dma_start(out=s2wrow_sb[:, :],
                                   in_=sw2d_h[:, :]).then_inc(r2S, 16)
                    sync.dma_start(out=b2row_sb[:, :],
                                   in_=b2_h[:, :]).then_inc(r2S, 16)
                    sync.wait_ge(r2D, 1)
                    sync.dma_start(out=cc2i_h[:, :],
                                   in_=cc_sb[:, :]).then_inc(r2S, 16)
                    sync.wait_ge(r2Cc, 1)
                    sync.dma_start(out=g2_sb[:, :],
                                   in_=cc2o_h[0:1, 0:1]).then_inc(r2C, 16)

                @ar2blk.vector
                def _(vector):
                    vector.wait_ge(r2S, 32)
                    vector.reduce_max(g2_sb[:, :], redt_sb[:, :], AX)
                    vbar()
                    vbar(vector.memset(cc_sb[:, :], 0.0))
                    vector.tensor_copy(cc_sb[0:1, 0:1], g2_sb[:, :]).then_inc(r2D, 1)
                    vector.wait_ge(r2C, 16)
                    vector.tensor_scalar(s2v_sb[:, :], g2_sb[:, :], EPS_Q,
                                         1.0 / 255.0, ALU.max,
                                         ALU.mult).then_inc(r2D, 1)
                    nc.vector.wait_ge(r2D, 2)
                    vector.wait_ge(r2S, 64)
                    vbar(vector.tensor_scalar(scale2row_sb[:, :], s2wrow_sb[:, :],
                                              s2v_sb[:, :], 1.0 / 127.0, ALU.mult,
                                              ALU.mult))
                    vbar(vector.reciprocal(rc2row_sb[:, :], scale2row_sb[:, :]))
                    vbar(vector.tensor_mul(tmp1r_sb[:, :], b2row_sb[:, :],
                                           rc2row_sb[:, :]))
                    vbar(vector.tensor_copy(tmpi1r_sb[:, :], tmp1r_sb[:, :]))
                    vbar(vector.tensor_copy(tmp1r_sb[:, :], tmpi1r_sb[:, :]))
                    vector.tensor_mul(b2qrow_sb[:, :], tmp1r_sb[:, :],
                                      scale2row_sb[:, :]).then_inc(r2D, 1)
                    vector.wait_ge(r2P, 2)
                    vector.tensor_copy(SCALE2B[:, 0:512], psR[0][:, :])
                    vector.tensor_copy(SCALE2B[:, 512:1024],
                                       psR[1][:, :]).then_inc(r2D, 1)
                    vector.wait_ge(r2P, 4)
                    vector.tensor_copy(B2QB[:, 0:512], psR[0][:, :])
                    vector.tensor_copy(B2QB[:, 512:1024],
                                       psR[1][:, :]).then_inc(r2D, 1)
                    vector.wait_ge(r2P, 5)
                    vector.tensor_copy(s2b_sb[:, :],
                                       psR[0][:, 0:1]).then_inc(r2D, 1)
                    nc.vector.wait_ge(r2D, 6)
                    vector.reciprocal(r2b_sb[:, :], s2b_sb[:, :])

                @ar2blk.gpsimd
                def _(gpsimd):
                    gpsimd.wait_ge(r2S, 80)
                    gpsimd.collective_compute(
                        "AllReduce", ALU.max,
                        replica_groups=[list(range(NCORES))],
                        ins=[cc2i_h.ap().opt()],
                        outs=[cc2o_h.ap().opt()]).then_inc(r2Cc, 1)

                @ar2blk.tensor
                def _(tensor):
                    tensor.wait_ge(r2D, 3)
                    tensor.matmul(psR[0][:, :], ones_sb[:, :],
                                  scale2row_sb[0:1, 0:512], start=True,
                                  stop=True).then_inc(r2P, 1)
                    tensor.matmul(psR[1][:, :], ones_sb[:, :],
                                  scale2row_sb[0:1, 512:1024], start=True,
                                  stop=True).then_inc(r2P, 1)
                    tensor.wait_ge(r2D, 4)
                    tensor.matmul(psR[0][:, :], ones_sb[:, :],
                                  b2qrow_sb[0:1, 0:512], start=True,
                                  stop=True).then_inc(r2P, 1)
                    tensor.matmul(psR[1][:, :], ones_sb[:, :],
                                  b2qrow_sb[0:1, 512:1024], start=True,
                                  stop=True).then_inc(r2P, 1)
                    tensor.wait_ge(r2D, 5)
                    tensor.matmul(psR[0][:, 0:1], ones_sb[:, :], s2v_sb[:, :],
                                  start=True, stop=True).then_inc(r2P, 1)

                ar2blk.__exit__(None, None, None)
                arstack.close()
                prstack.close()
                if BUILD_PHASES < 6:
                    nc.finalize()
                    return nc
                ps_c = [cstack.enter_context(
                    nc.psum_tensor(f"psOC{i}", [128, 512], F32)) for i in range(8)]
                h_ld = [c2sb(f"h_ld{i}", [128, 32 * 512], F16) for i in range(2)]
                o1_ld = [c2sb(f"o1_ld{i}", [128, 4 * D], F16) for i in range(2)]
                fout_sb = c2sb("fout_sb", [128, 4 * D], F32)
                hu8_sb = [c2sb(f"hu8{i}", [128, 512], U8) for i in range(2)]
                eb1_sb = c2sb("eb1_sb", [128, 512], F32)
                eb2_sb = c2sb("eb2_sb", [128, 512], F32)

                cblk = cstack.enter_context(nc.Block())

                @cblk.sync
                def _(sync):
                    for g in range(16):
                        if g >= 2:
                            sync.wait_ge(cMM, 4 * (g - 1))
                            sync.wait_ge(cFO, g - 1)
                        for fc in range(32):
                            sync.dma_start(
                                out=h_ld[g % 2][:, fc * 512:(fc + 1) * 512],
                                in_=h16_h[fc, g]).then_inc(cHL, 16)
                        for tt in range(4):
                            r0 = g * 512 + tt * 128
                            sync.dma_start(out=o1_ld[g % 2][:, tt * D:(tt + 1) * D],
                                           in_=o16_h[r0:r0 + 128, :]
                                           ).then_inc(cOL, 16)
                        if g >= 1:
                            sync.wait_ge(cFO, g)
                            for tt in range(4):
                                r0 = (g - 1) * 512 + tt * 128
                                sync.dma_start(out=out_h[r0:r0 + 128, :],
                                               in_=fout_sb[:, tt * D:(tt + 1) * D]
                                               ).then_inc(cOS, 16)
                    sync.wait_ge(cFO, 16)
                    for tt in range(4):
                        r0 = 15 * 512 + tt * 128
                        sync.dma_start(out=out_h[r0:r0 + 128, :],
                                       in_=fout_sb[:, tt * D:(tt + 1) * D]
                                       ).then_inc(cOS, 16)
                    sync.wait_ge(cOS, 16 * 64)
                    if DEBUG_TAPS:
                        sync.dma_start(out=dbg_h[:, :], in_=dbg_sb[:, :]).then_inc(cOS, 16)
                        sync.wait_ge(cOS, 16 * 65)

                @cblk.vector
                def _(vector):
                    def epilogue_group(g):
                        if g >= 1:
                            nc.vector.wait_ge(cOS, 64 * g)
                        nc.vector.wait_ge(cOL, 64 * min(g + 2, 16))
                        for tt in range(4):
                            nc.vector.wait_ge(cMM, 4 * g + tt + 1)
                            for half in range(2):
                                bank = ps_c[2 * tt + half]
                                sl_ = slice(half * 512, half * 512 + 512)
                                osl = slice(tt * D + half * 512,
                                            tt * D + half * 512 + 512)
                                vbar(nc.vector.tensor_mul(eb1_sb[:, :],
                                                          bank[:, :],
                                                          SCALE2B[:, sl_]))
                                vbar(nc.vector.tensor_add(eb2_sb[:, :],
                                                          eb1_sb[:, :],
                                                          B2QB[:, sl_]))
                                nc.vector.tensor_add(fout_sb[:, osl],
                                                     eb2_sb[:, :],
                                                     o1_ld[g % 2][:, osl])
                            nc.vector.engine_nop().then_inc(cOE, 1)
                        nc.vector.engine_nop().then_inc(cFO, 1)

                    for g in range(16):
                        vector.wait_ge(cHL, 512 * (g + 1))
                        for fc in range(32):
                            u = 32 * g + fc
                            if u >= 2:
                                vector.wait_ge(cQ2, u - 1)
                            vector.tensor_scalar(
                                hu8_sb[fc % 2][:, :],
                                h_ld[g % 2][:, fc * 512:(fc + 1) * 512],
                                r2b_sb[:, :], None, ALU.mult).then_inc(cQ1, 1)
                        if g >= 1:
                            epilogue_group(g - 1)
                    epilogue_group(15)

                @cblk.scalar
                def _(scalar):
                    for g in range(16):
                        for fc in range(32):
                            u = 32 * g + fc
                            scalar.wait_ge(cQ1, u + 1)
                            scalar.copy(h_ld[g % 2][:, fc * 512:(fc + 1) * 512],
                                        hu8_sb[fc % 2][:, :]).then_inc(cQ2, 1)

                @cblk.tensor
                def _(tensor):
                    for g in range(16):
                        for tt in range(4):
                            tu = 4 * g + tt
                            if tu >= 4:
                                tensor.wait_ge(cOE, tu - 3)
                            b0 = ps_c[2 * tt]
                            b1 = ps_c[2 * tt + 1]
                            for fc in range(32):
                                if tt == 0:
                                    tensor.wait_ge(cQ2, 32 * g + fc + 1)
                                lhs = h_ld[g % 2][:, fc * 512 + tt * 128:
                                                  fc * 512 + (tt + 1) * 128]
                                tensor.matmul(b0[:, :], lhs,
                                              w2qT_sb[:, fc * D: fc * D + 512],
                                              start=(fc == 0), stop=(fc == 31))
                                ins = tensor.matmul(b1[:, :], lhs,
                                                    w2qT_sb[:, fc * D + 512:
                                                            fc * D + 1024],
                                                    start=(fc == 0),
                                                    stop=(fc == 31))
                            ins.then_inc(cMM, 1)

    nc.finalize()
    return nc


def kernel(L0_attn_X, L0_attn_Wq, L0_attn_bq, L0_attn_Wk, L0_attn_bk,
           L0_attn_Wv, L0_attn_bv, L0_ffn_w1, L0_ffn_b1, L0_ffn_w2, L0_ffn_b2):
    X = np.ascontiguousarray(np.asarray(L0_attn_X, dtype=np.float32))
    wq = np.ascontiguousarray(np.asarray(L0_attn_Wq, dtype=np.float32))
    wk = np.ascontiguousarray(np.asarray(L0_attn_Wk, dtype=np.float32))
    wv = np.ascontiguousarray(np.asarray(L0_attn_Wv, dtype=np.float32))
    bq = np.asarray(L0_attn_bq, dtype=np.float32)
    bk = np.asarray(L0_attn_bk, dtype=np.float32)
    bv = np.asarray(L0_attn_bv, dtype=np.float32)
    w1 = np.ascontiguousarray(np.asarray(L0_ffn_w1, dtype=np.float32))
    b1 = np.asarray(L0_ffn_b1, dtype=np.float32)
    w2 = np.ascontiguousarray(np.asarray(L0_ffn_w2, dtype=np.float32))
    b2 = np.asarray(L0_ffn_b2, dtype=np.float32)

    bqr = np.ascontiguousarray(bq.reshape(8, 128).T)
    bkr = np.ascontiguousarray(bk.reshape(8, 128).T)
    bvr = np.ascontiguousarray(bv.reshape(1, D))
    b1r = np.ascontiguousarray(b1.reshape(32, 128).T)
    b2r = np.ascontiguousarray(b2.reshape(1, D))
    ii = np.arange(128)
    mask = np.where(ii[None, :] <= ii[:, None], 0.0, NEG).astype(np.float16)
    qts = np.arange(32)
    cnt = (S - 1 - (qts[None, :] * 128 + ii[:, None])).astype(np.float32)
    onesf = np.ones((1, 128), np.float32)

    nc = build_nc()

    in_maps = []
    for c in range(NCORES):
        xs = np.ascontiguousarray(X[2 * c:2 * c + 2].reshape(T, D))
        in_maps.append(dict(x=xs, wq=wq, wk=wk, wv=wv, bqr=bqr, bkr=bkr,
                            bvr=bvr, w1=w1, b1r=b1r, w2=w2, b2r=b2r,
                            mask=mask, cnt=cnt, onesf=onesf))
    res = run_bass_kernel_spmd(nc, in_maps, core_ids=list(range(NCORES)))
    out = np.concatenate([np.asarray(res.results[c]["out"], np.float32)
                          for c in range(NCORES)], axis=0)
    return out.reshape(16, S, D)


# version 10
# speedup vs baseline: 1.0891x; 1.0203x over previous
import numpy as np
from contextlib import ExitStack

import concourse.bass as bass
import concourse.mybir as mybir
from concourse.bass_utils import run_bass_kernel_spmd

F32 = mybir.dt.float32
F16 = mybir.dt.float16
U8 = mybir.dt.uint8
I8 = mybir.dt.int8
I32 = mybir.dt.int32
ALU = mybir.AluOpType
ACT = mybir.ActivationFunctionType
AX = mybir.AxisListType.X

NCORES = 8
S = 4096
D = 1024
FF = 4096
T = 2 * S
NST = T // 512
SPB = S // 512
EPS_LN = 1e-5
EPS_Q = 1e-8
NEG = -60000.0


def _sgroups():
    out = []
    sgidx = 0
    for st in range(NST):
        sl = st % SPB
        rows = []
        for tt in range(4):
            qt = sl * 4 + tt
            gf = (qt * 128 + 511) // 512
            for g in range(gf):
                ng = min(512, qt * 128 - g * 512)
                rows.append((tt, g, ng, False, sgidx % 8, sgidx))
                sgidx += 1
            rows.append((tt, gf, 128, True, sgidx % 8, sgidx))
            sgidx += 1
        out.append(rows)
    return out


SGROUPS = _sgroups()
LASTUSE = [{b: max(r[5] for r in rows if r[4] == b)
            for b in set(r[4] for r in rows)} for rows in SGROUPS]
BUILD_PHASES = 6
B_STAGES = 4
B_HMAX = True
B_SIMPLE_EVAC = False
SKIP_A = False
SKIP_AR1 = False
SKIP_P = False
DEBUG_TAPS = False


def build_nc():
    nc = bass.Bass("TRN2", target_bir_lowering=False,
                   detect_race_conditions=False)

    x_h = nc.dram_tensor("x", [T, D], F32, kind="ExternalInput")
    wq_h = nc.dram_tensor("wq", [D, D], F32, kind="ExternalInput")
    wk_h = nc.dram_tensor("wk", [D, D], F32, kind="ExternalInput")
    wv_h = nc.dram_tensor("wv", [D, D], F32, kind="ExternalInput")
    bq_h = nc.dram_tensor("bqr", [128, 8], F32, kind="ExternalInput")
    bk_h = nc.dram_tensor("bkr", [128, 8], F32, kind="ExternalInput")
    bv_h = nc.dram_tensor("bvr", [1, D], F32, kind="ExternalInput")
    w1_h = nc.dram_tensor("w1", [FF, D], F32, kind="ExternalInput")
    b1_h = nc.dram_tensor("b1r", [128, 32], F32, kind="ExternalInput")
    w2_h = nc.dram_tensor("w2", [D, FF], F32, kind="ExternalInput")
    b2_h = nc.dram_tensor("b2r", [1, D], F32, kind="ExternalInput")
    mask_h = nc.dram_tensor("mask", [128, 128], F16, kind="ExternalInput")
    cnt_h = nc.dram_tensor("cnt", [128, 32], F32, kind="ExternalInput")
    ones_h = nc.dram_tensor("onesf", [1, 128], F32, kind="ExternalInput")
    out_h = nc.dram_tensor("out", [T, D], F32, kind="ExternalOutput")

    KI = "ExternalOutput" if DEBUG_TAPS else "Internal"
    dbg_h = nc.dram_tensor("dbg", [128, 64], F32, kind=KI)
    y16_h = nc.dram_tensor("y16", [T, D], F16, kind=KI)
    o16_h = nc.dram_tensor("o16", [T, D], F16, kind=KI)
    w1q_h = nc.dram_tensor("w1qs", [FF, D], F16, kind=KI)
    w2q_h = nc.dram_tensor("w2qs", [D, FF], F16, kind=KI)
    h16_h = nc.dram_tensor("h16", [32, 16, 128, 512], F16, kind=KI)
    red1_h = nc.dram_tensor("red1", [1, 128], F32, kind="Internal")
    red2_h = nc.dram_tensor("red2", [1, 128], F32, kind="Internal")
    sw2d_h = nc.dram_tensor("sw2d", [1, D], F32, kind=KI)
    cc1i_h = nc.dram_tensor("cc1i", [128, 4], F32, kind="Internal")
    cc1o_h = nc.dram_tensor("cc1o", [128, 4], F32, kind="Internal")
    cc2i_h = nc.dram_tensor("cc2i", [128, 4], F32, kind="Internal")
    cc2o_h = nc.dram_tensor("cc2o", [128, 4], F32, kind="Internal")

    with ExitStack() as outer:
        def osb(*a):
            return outer.enter_context(nc.sbuf_tensor(*a))

        def sem(name):
            return outer.enter_context(nc.semaphore(name))

        b1_sb = osb("b1_sb", [128, 32], F32)
        sw1_sb = osb("sw1_sb", [128, 32], F32)
        scale1_sb = osb("scale1_sb", [128, 32], F32)
        b1q_sb = osb("b1q_sb", [128, 32], F32)
        sw2_sb = osb("sw2_sb", [128, 8], F32)
        ones_sb = osb("ones_sb", [1, 128], F32)
        ymax_sb = osb("ymax_sb", [128, 2], F32)
        hmax_sb = osb("hmax_sb", [128, 2], F32)
        s1b_sb = osb("s1b_sb", [128, 1], F32)
        r1b_sb = osb("r1b_sb", [128, 1], F32)
        s2b_sb = osb("s2b_sb", [128, 1], F32)
        r2b_sb = osb("r2b_sb", [128, 1], F32)
        g1_sb = osb("g1_sb", [1, 1], F32)
        g2_sb = osb("g2_sb", [1, 1], F32)
        s1v_sb = osb("s1v_sb", [1, 1], F32)
        s2v_sb = osb("s2v_sb", [1, 1], F32)
        redt_sb = osb("redt_sb", [1, 128], F32)
        dbg_sb = osb("dbg_sb", [128, 64], F32)
        cc_sb = osb("cc_sb", [128, 4], F32)

        pLD = sem("pLD"); pW = sem("pW"); pWF = sem("pWF"); pQS = sem("pQS")
        pST = sem("pST"); pSLF = sem("pSLF"); pMM = sem("pMM")
        pLDg = sem("pLDg"); r1T = sem("r1T")
        pW0 = sem("pW0"); pW1 = sem("pW1")
        bHS0 = sem("bHS0"); bHS1 = sem("bHS1"); bHS2 = sem("bHS2"); bHS3 = sem("bHS3")
        aXL0 = sem("aXL0"); aXL1 = sem("aXL1"); aX20 = sem("aX20")
        aX21 = sem("aX21"); aLN = sem("aLN"); aTR0 = sem("aTR0")
        aTR1 = sem("aTR1"); aPJ = sem("aPJ"); aQK = sem("aQK")
        aVE = sem("aVE"); aSG = sem("aSG"); aMN = sem("aMN")
        aES = sem("aES"); aEM = sem("aEM"); aO1 = sem("aO1")
        aOS0 = sem("aOS0"); aOS1 = sem("aOS1"); aY = sem("aY")
        aYS0 = sem("aYS0"); aYS1 = sem("aYS1"); aSQ1 = sem("aSQ1")
        aSQ2 = sem("aSQ2"); aRSTD1 = sem("aRSTD1"); aRSTD2 = sem("aRSTD2")
        aVAR1 = sem("aVAR1"); aVAR2 = sem("aVAR2"); aXLF = sem("aXLF")
        aX2F = sem("aX2F"); aSLF = sem("aSLF")
        r1S = sem("r1S"); r1C = sem("r1C"); r1D = sem("r1D"); r1P = sem("r1P")
        r1Cc = sem("r1Cc"); r2Cc = sem("r2Cc")
        bYL = sem("bYL"); bYQ = sem("bYQ"); bXQ = sem("bXQ"); bXF = sem("bXF")
        bHM = sem("bHM"); bHE = sem("bHE"); bHS = sem("bHS"); bHR = sem("bHR")
        r2S = sem("r2S"); r2C = sem("r2C"); r2D = sem("r2D"); r2P = sem("r2P")
        cHL = sem("cHL"); cHF = sem("cHF"); cOL = sem("cOL"); cQ1 = sem("cQ1")
        cQ2 = sem("cQ2"); cMM = sem("cMM"); cOE = sem("cOE"); cFO = sem("cFO")
        cOS = sem("cOS")
        vBAR = sem("vBAR"); sBAR = sem("sBAR")

        _vb = [0]
        _sb_ = [0]

        def vbar(ins=None):
            if ins is None:
                ins = nc.vector.engine_nop()
            _vb[0] += 1
            ins.then_inc(vBAR, 1)
            nc.vector.wait_ge(vBAR, _vb[0])

        def sbar(ins):
            _sb_[0] += 1
            ins.then_inc(sBAR, 1)
            nc.scalar.wait_ge(sBAR, _sb_[0])

        pa = outer.enter_context(ExitStack())

        def pasb(*a):
            return pa.enter_context(nc.sbuf_tensor(*a))
        wq_sb = pasb("wq_sb", [128, 8 * D], F16)
        wk_sb = pasb("wk_sb", [128, 8 * D], F16)
        wv_sb = pasb("wv_sb", [128, 8 * D], F16)
        bq_sb = pasb("bq_sb", [128, 8], F32)
        bk_sb = pasb("bk_sb", [128, 8], F32)
        mask_sb = pasb("mask_sb", [128, 128], F16)
        cnt_sb = pasb("cnt_sb", [128, 32], F32)
        eps_sb = pasb("eps_sb", [128, 1], F32)
        bv_bc = pasb("bv_bc", [128, D], F16)

        with ExitStack() as pstack:
            def psb(*a):
                return pstack.enter_context(nc.sbuf_tensor(*a))
            wbuf = [psb(f"wbuf{i}", [128, D], F32) for i in range(2)]
            w2buf = [psb(f"w2buf{i}", [128, FF], F32) for i in range(2)]
            i8b = psb("i8b", [128, D], I8)
            i8b2 = psb("i8b2", [128, FF], I8)
            w1qf = [psb(f"w1qf{i}", [128, D], F16) for i in range(2)]
            w2qf = [psb(f"w2qf{i}", [128, FF], F16) for i in range(2)]
            rw_sb = psb("rw_sb", [128, 1], F32)
            rw_in = psb("rw_in", [128, 1], F32)
            bvrow_sb = psb("bvrow_sb", [1, D], F32)
            psB = [pstack.enter_context(nc.psum_tensor(f"psB{i}", [128, 512], F32))
                   for i in range(2)]

            pblk = pstack.enter_context(nc.Block())

            @pblk.sync
            def _(sync):
                if SKIP_P:
                    return
                sync.dma_start(out=bq_sb[:, :], in_=bq_h[:, :]).then_inc(pLD, 16)
                sync.dma_start(out=bk_sb[:, :], in_=bk_h[:, :]).then_inc(pLD, 16)
                sync.dma_start(out=b1_sb[:, :], in_=b1_h[:, :]).then_inc(pLD, 16)
                sync.dma_start(out=mask_sb[:, :], in_=mask_h[:, :]).then_inc(pLD, 16)
                sync.dma_start(out=cnt_sb[:, :], in_=cnt_h[:, :]).then_inc(pLD, 16)
                sync.dma_start(out=ones_sb[:, :], in_=ones_h[:, :]).then_inc(pLD, 16)
                sync.dma_start(out=bvrow_sb[:, :], in_=bv_h[:, :]).then_inc(pLD, 16)
                for it in range(42):
                    if it < 40:
                        if it >= 2:
                            sync.wait_ge(pWF, it - 1)
                        if it < 32:
                            sync.dma_start(out=wbuf[it % 2][:, :],
                                           in_=w1_h[it * 128:(it + 1) * 128, :]
                                           ).then_inc(pW1 if it % 2 else pW0, 16)
                        else:
                            dc = it - 32
                            sync.dma_start(out=w2buf[dc % 2][:, :],
                                           in_=w2_h[dc * 128:(dc + 1) * 128, :]
                                           ).then_inc(pW1 if it % 2 else pW0, 16)
                    j = it - 2
                    if j >= 0:
                        sync.wait_ge(pQS, j + 1)
                        if j < 32:
                            sync.dma_start(out=w1q_h[j * 128:(j + 1) * 128, :],
                                           in_=w1qf[j % 2][:, :]).then_inc(pST, 16)
                        else:
                            dc = j - 32
                            sync.dma_start(out=w2q_h[dc * 128:(dc + 1) * 128, :],
                                           in_=w2qf[dc % 2][:, :]).then_inc(pST, 16)
                            sync.dma_start(out=sw2d_h[0:1, dc * 128:(dc + 1) * 128],
                                           in_=sw2_sb[:, dc:dc + 1]).then_inc(pST, 16)

            @pblk.gpsimd
            def _(gpsimd):
                if SKIP_P:
                    return
                for wh, wsb in ((wq_h, wq_sb), (wk_h, wk_sb), (wv_h, wv_sb)):
                    for c in range(8):
                        gpsimd.dma_start(
                            out=wsb[:, c * D:(c + 1) * D],
                            in_=wh[c * 128:(c + 1) * 128, :]).then_inc(pLDg, 16)
                gpsimd.wait_ge(pLDg, 16 * 24)

            @pblk.tensor
            def _(tensor):
                if SKIP_P:
                    return
                tensor.wait_ge(pLD, 16 * 7)
                tensor.matmul(psB[0][:, :], ones_sb[:, :], bvrow_sb[0:1, 0:512],
                              start=True, stop=True).then_inc(pMM, 1)
                tensor.matmul(psB[1][:, :], ones_sb[:, :], bvrow_sb[0:1, 512:1024],
                              start=True, stop=True).then_inc(pMM, 1)

            @pblk.vector
            def _(vector):
                if SKIP_P:
                    return
                vector.memset(eps_sb[:, :], EPS_LN)
                vector.memset(ymax_sb[:, :], 0.0)
                vector.memset(hmax_sb[:, :], 0.0)
                vector.wait_ge(pMM, 2)
                vector.tensor_copy(bv_bc[:, 0:512], psB[0][:, :])
                vector.tensor_copy(bv_bc[:, 512:1024], psB[1][:, :])
                selfc = 0
                for i in range(40):
                    is1 = i < 32
                    buf = wbuf[i % 2] if is1 else w2buf[(i - 32) % 2]
                    i8t = i8b if is1 else i8b2
                    qf = w1qf[i % 2] if is1 else w2qf[(i - 32) % 2]
                    scol = (sw1_sb[:, i:i + 1] if is1
                            else sw2_sb[:, i - 32:i - 31])
                    vector.wait_ge(pW1 if i % 2 else pW0, 16 * (i // 2 + 1))
                    vector.reduce_max(scol, buf[:, :], AX, apply_absolute_value=True)
                    vbar()
                    selfc += 1
                    vector.reciprocal(rw_sb[:, :], scol).then_inc(pSLF, 1)
                    nc.vector.wait_ge(pSLF, selfc)
                    vector.tensor_scalar(i8t[:, :], buf[:, :], rw_sb[:, :],
                                         127.0, ALU.mult,
                                         ALU.mult).then_inc(pWF, 1)
                    nc.vector.wait_ge(pWF, i + 1)
                    if DEBUG_TAPS:
                        vector.tensor_copy(dbg_sb[:, i:i + 1], rw_sb[:, :])
                        if i < 8:
                            vbar(vector.tensor_copy(dbg_sb[:, 40 + i:41 + i],
                                                    scol))
                            vbar(vector.tensor_copy(dbg_sb[:, 48 + i:49 + i],
                                                    buf[:, 0:1]))
                    if i >= 2:
                        j = i - 1
                        pst_need = 16 * (j + 1) if j < 32 else 512 + 32 * (j - 31)
                        vector.wait_ge(pST, pst_need)
                    vector.tensor_copy(qf[:, :], i8t[:, :]).then_inc(pQS, 1)

        if BUILD_PHASES < 2:
            nc.finalize()
            return nc
        with ExitStack() as astack:
            if SKIP_A:
                pass
            def asb(*a):
                return astack.enter_context(nc.sbuf_tensor(*a))
            K_sb = asb("K_sb", [128, 8 * S], F16)
            xa = [asb(f"xa{i}", [128, D], F32) for i in range(2)]
            x2b = [asb(f"x2b{i}", [128, D], F32) for i in range(2)]
            xn_tm = asb("xn_tm", [128, 4 * D], F16)
            xn_fm = [asb(f"xn_fm{i}", [128, 8 * 512], F16) for i in range(2)]
            q_fm = asb("q_fm", [128, 8 * 512], F16)
            v_sb = asb("v_sb", [128, 4 * D], F16)
            out1b = [asb(f"out1b{i}", [128, D], F16) for i in range(2)]
            ybuf = [asb(f"ybuf{i}", [128, D], F16) for i in range(2)]
            e_dump = asb("e_dump", [128, D], F16)
            masked_sb = asb("masked_sb", [128, 128], F32)
            msum_sb = asb("msum_sb", [128, 2], F32)
            m1_sb = asb("m1_sb", [128, 2], F32)
            mm_sb = asb("mm_sb", [128, 2], F32)
            v1_sb = asb("v1_sb", [128, 2], F32)
            sqacc_sb = asb("sqacc_sb", [128, 2], F32)
            lnt1_sb = asb("lnt1_sb", [128, 2], F32)
            rstd_sb = asb("rstd_sb", [128, 2], F32)
            msum2_sb = asb("msum2_sb", [128, 2], F32)
            m2_sb = asb("m2_sb", [128, 2], F32)
            mm2_sb = asb("mm2_sb", [128, 2], F32)
            v2_sb = asb("v2_sb", [128, 2], F32)
            sqacc2_sb = asb("sqacc2_sb", [128, 2], F32)
            lnt2_sb = asb("lnt2_sb", [128, 2], F32)
            rstd2_sb = asb("rstd2_sb", [128, 2], F32)
            mrun_sb = [asb(f"mrun{i}", [128, 1], F32) for i in range(2)]
            mneg_sb = [asb(f"mneg{i}", [128, 1], F32) for i in range(2)]
            delta_sb = [asb(f"delta{i}", [128, 1], F32) for i in range(2)]
            srun_sb = [asb(f"srun{i}", [128, 1], F32) for i in range(2)]
            esum_sb = [asb(f"esum{i}", [128, 1], F32) for i in range(2)]
            resc_sb = [asb(f"resc{i}", [128, 1], F32) for i in range(2)]
            gm_sb = asb("gm_sb", [128, 1], F32)
            den_sb = asb("den_sb", [128, 1], F32)
            rden_sb = asb("rden_sb", [128, 1], F32)
            rs_sb = asb("rs_sb", [128, 1], F32)
            ymt_sb = asb("ymt_sb", [128, 1], F32)
            ps = [astack.enter_context(
                nc.psum_tensor(f"psA{i}", [128, 512], F32)) for i in range(8)]

            ablk = astack.enter_context(nc.Block())

            NSUB = 4 * NST

            def xl_sem(k):
                return aXL0 if k % 2 == 0 else aXL1

            def x2_sem(k):
                return aX20 if k % 2 == 0 else aX21

            def os_sem(k):
                return aOS0 if k % 2 == 0 else aOS1

            def ys_sem(k):
                return aYS0 if k % 2 == 0 else aYS1

            def tr_sem(stx):
                return aTR0 if stx % 2 == 0 else aTR1

            @ablk.sync
            def _(sync):
                if SKIP_A:
                    return
                for j in range(2):
                    sync.dma_start(out=xa[j][:, :],
                                   in_=x_h[j * 128:(j + 1) * 128, :]
                                   ).then_inc(xl_sem(j), 16)
                    sync.dma_start(out=x2b[j][:, :],
                                   in_=x_h[j * 128:(j + 1) * 128, :]
                                   ).then_inc(x2_sem(j), 16)
                for j in range(2, 8):
                    sync.wait_ge(aXLF, j - 1)
                    sync.wait_ge(aSQ1, j - 1)
                    sync.dma_start(out=xa[j % 2][:, :],
                                   in_=x_h[j * 128:(j + 1) * 128, :]
                                   ).then_inc(xl_sem(j), 16)
                for p in range(NSUB):
                    j2 = p + 2
                    if j2 < NSUB:
                        sync.wait_ge(aX2F, p + 1)
                        sync.dma_start(out=x2b[j2 % 2][:, :],
                                       in_=x_h[j2 * 128:(j2 + 1) * 128, :]
                                       ).then_inc(x2_sem(j2), 16)
                    ja = p + 8
                    if ja < NSUB:
                        sync.wait_ge(aXLF, ja - 1)
                        sync.wait_ge(aSQ1, ja - 1)
                        sync.dma_start(out=xa[ja % 2][:, :],
                                       in_=x_h[ja * 128:(ja + 1) * 128, :]
                                       ).then_inc(xl_sem(ja), 16)

            @ablk.gpsimd
            def _(gpsimd):
                if SKIP_A:
                    return

            @ablk.scalar
            def _(scalar):
                if SKIP_A:
                    return

                def fe_sc(kN):
                    p = kN % 2
                    stN = kN // 4
                    ttn = kN % 4
                    scalar.wait_ge(xl_sem(kN), 16 * (kN // 2 + 1))
                    scalar.activation(e_dump[:, 0:D], xa[p][:, :], ACT.Square,
                                      accum_out=sqacc_sb[:, p:p + 1]
                                      ).then_inc(aSQ1, 1)
                    scalar.wait_ge(aVAR1, kN + 1)
                    sbar(scalar.activation(lnt1_sb[:, p:p + 1],
                                           v1_sb[:, p:p + 1], ACT.Ln,
                                           bias=eps_sb[:, :], scale=1.0))
                    scalar.activation(rstd_sb[:, p:p + 1], lnt1_sb[:, p:p + 1],
                                      ACT.Exp, scale=-0.5).then_inc(aRSTD1, 1)
                    scalar.wait_ge(aLN, kN + 1)
                    if stN >= 2:
                        scalar.wait_ge(aPJ, 24 * (stN - 1))
                    buf = xn_fm[stN % 2]
                    for c in range(8):
                        scalar.dma_start_transpose(
                            buf[:, c * 512 + ttn * 128: c * 512 + (ttn + 1) * 128],
                            xn_tm[:, ttn * D + c * 128: ttn * D + (c + 1) * 128]
                        ).then_inc(tr_sem(stN), 16)

                for j in range(7):
                    fe_sc(j)
                for st in range(NST):
                    sl = st % SPB
                    for u in range(16):
                        scalar.wait_ge(aPJ, 24 * st + u + 1)
                        bank = ps[u % 4]
                        if u < 8:
                            dest = q_fm[:, u * 512:(u + 1) * 512]
                            bias = bq_sb[:, u:u + 1]
                        else:
                            c = u - 8
                            dest = K_sb[:, c * S + sl * 512: c * S + (sl + 1) * 512]
                            bias = bk_sb[:, c:c + 1]
                        scalar.activation(dest, bank[:, :], ACT.Identity,
                                          bias=bias, scale=1.0).then_inc(aQK, 1)
                    grp_by_tt = {}
                    for rec in SGROUPS[st]:
                        grp_by_tt.setdefault(rec[0], []).append(rec)
                    for tt in range(4):
                        k = 4 * st + tt
                        kp = k % 2
                        for (_tt, g, ng, is_diag, bank, sgidx) in grp_by_tt[tt]:
                            scalar.wait_ge(aMN, sgidx + 1)
                            par = g % 2
                            scalar.activation(resc_sb[par][:, :],
                                              delta_sb[par][:, :],
                                              ACT.Exp, scale=1.0 / 32.0)
                            src = (masked_sb[:, 0:128] if is_diag
                                   else ps[bank][:, 0:ng])
                            scalar.activation(e_dump[:, 0:ng], src, ACT.Exp,
                                              scale=1.0 / 32.0,
                                              bias=mneg_sb[par][:, :],
                                              accum_out=esum_sb[par][:, :]
                                              ).then_inc(aES, 1)
                            if is_diag:
                                scalar.activation(resc_sb[(g + 1) % 2][:, :],
                                                  mneg_sb[par][:, :], ACT.Exp,
                                                  scale=1.0).then_inc(aEM, 1)
                        scalar.wait_ge(aO1, k + 1)
                        scalar.activation(e_dump[:, 0:D], out1b[kp][:, :],
                                          ACT.Square,
                                          accum_out=sqacc2_sb[:, kp:kp + 1]
                                          ).then_inc(aSQ2, 1)
                        scalar.dma_start(out=o16_h[k * 128:(k + 1) * 128, :],
                                         in_=out1b[kp][:, :]
                                         ).then_inc(os_sem(k), 16)
                        scalar.wait_ge(aVAR2, k + 1)
                        sbar(scalar.activation(lnt2_sb[:, kp:kp + 1],
                                               v2_sb[:, kp:kp + 1], ACT.Ln,
                                               bias=eps_sb[:, :], scale=1.0))
                        scalar.activation(rstd2_sb[:, kp:kp + 1],
                                          lnt2_sb[:, kp:kp + 1], ACT.Exp,
                                          scale=-0.5).then_inc(aRSTD2, 1)
                        if k >= 1:
                            scalar.wait_ge(aY, k)
                            scalar.dma_start(
                                out=y16_h[(k - 1) * 128:k * 128, :],
                                in_=ybuf[(k - 1) % 2][:, :]
                            ).then_inc(ys_sem(k - 1), 16)
                        if k + 7 < NSUB:
                            fe_sc(k + 7)
                scalar.wait_ge(aY, NSUB)
                scalar.dma_start(out=y16_h[(NSUB - 1) * 128:NSUB * 128, :],
                                 in_=ybuf[(NSUB - 1) % 2][:, :]
                                 ).then_inc(ys_sem(NSUB - 1), 16)
                scalar.wait_ge(aOS0, 16 * (NSUB // 2))
                scalar.wait_ge(aOS1, 16 * (NSUB // 2))
                scalar.wait_ge(aYS0, 16 * (NSUB // 2))
                scalar.wait_ge(aYS1, 16 * (NSUB // 2))

            @ablk.tensor
            def _(tensor):
                if SKIP_A:
                    return
                for st in range(NST):
                    sl = st % SPB
                    buf = xn_fm[st % 2]
                    tensor.wait_ge(tr_sem(st), 512 * (st // 2 + 1))
                    for u in range(24):
                        pu = 24 * st + u
                        if u < 4:
                            if st >= 1 and u in LASTUSE[st - 1]:
                                tensor.wait_ge(aES, LASTUSE[st - 1][u] + 1)
                        elif pu >= 4:
                            prev = pu - 4
                            pm = prev % 24
                            if pm < 16:
                                tensor.wait_ge(aQK, 16 * (prev // 24) + pm + 1)
                            else:
                                tensor.wait_ge(aVE, 8 * (prev // 24) + pm - 16 + 1)
                        bank = ps[u % 4]
                        if u < 8:
                            for c in range(8):
                                ins = tensor.matmul(
                                    bank[:, :],
                                    wq_sb[:, c * D + u * 128: c * D + (u + 1) * 128],
                                    buf[:, c * 512:(c + 1) * 512],
                                    start=(c == 0), stop=(c == 7))
                        elif u < 16:
                            co = u - 8
                            for c in range(8):
                                ins = tensor.matmul(
                                    bank[:, :],
                                    wk_sb[:, c * D + co * 128: c * D + (co + 1) * 128],
                                    buf[:, c * 512:(c + 1) * 512],
                                    start=(c == 0), stop=(c == 7))
                        else:
                            tt = (u - 16) // 2
                            half = (u - 16) % 2
                            for c in range(8):
                                ins = tensor.matmul(
                                    bank[:, :],
                                    buf[:, c * 512 + tt * 128: c * 512 + (tt + 1) * 128],
                                    wv_sb[:, c * D + half * 512: c * D + half * 512 + 512],
                                    start=(c == 0), stop=(c == 7))
                        ins.then_inc(aPJ, 1)
                    tensor.wait_ge(aQK, 16 * (st + 1))
                    seen = set()
                    for (tt, g, ng, is_diag, bank, sgidx) in SGROUPS[st]:
                        qt = sl * 4 + tt
                        if bank < 4 and bank not in seen:
                            seen.add(bank)
                            tensor.wait_ge(aVE, 8 * st + bank + 5)
                        if sgidx >= 8:
                            tensor.wait_ge(aES, sgidx - 7)
                        k0 = qt * 128 if is_diag else g * 512
                        for c in range(8):
                            ins = tensor.matmul(
                                ps[bank][:, 0:ng],
                                q_fm[:, c * 512 + tt * 128: c * 512 + (tt + 1) * 128],
                                K_sb[:, c * S + k0: c * S + k0 + ng],
                                start=(c == 0), stop=(c == 7))
                        ins.then_inc(aSG, 1)

            @ablk.vector
            def _(vector):
                if SKIP_A:
                    return
                slf = [0]

                def self_wait(ins):
                    slf[0] += 1
                    ins.then_inc(aSLF, 1)
                    nc.vector.wait_ge(aSLF, slf[0])

                def fe_dve(kN):
                    p = kN % 2
                    vector.wait_ge(xl_sem(kN), 16 * (kN // 2 + 1))
                    vector.reduce_sum(msum_sb[:, p:p + 1], xa[p][:, :], AX)
                    vbar()
                    vbar(vector.tensor_scalar(m1_sb[:, p:p + 1],
                                              msum_sb[:, p:p + 1],
                                              1.0 / D, None, ALU.mult))
                    vector.wait_ge(aSQ1, kN + 1)
                    vbar(vector.tensor_mul(mm_sb[:, p:p + 1], m1_sb[:, p:p + 1],
                                           m1_sb[:, p:p + 1]))
                    vector.scalar_tensor_tensor(
                        v1_sb[:, p:p + 1], sqacc_sb[:, p:p + 1], 1.0 / D,
                        mm_sb[:, p:p + 1], ALU.mult,
                        ALU.subtract).then_inc(aVAR1, 1)
                    nc.vector.wait_ge(aVAR1, kN + 1)
                    vector.wait_ge(aRSTD1, kN + 1)
                    if kN >= 4:
                        stp = (kN - 4) // 4
                        vector.wait_ge(tr_sem(stp), 512 * (stp // 2 + 1))
                    ttn = kN % 4
                    vector.tensor_scalar(xn_tm[:, ttn * D:(ttn + 1) * D],
                                         xa[p][:, :], m1_sb[:, p:p + 1],
                                         rstd_sb[:, p:p + 1],
                                         ALU.subtract, ALU.mult)
                    nc.vector.engine_nop().then_inc(aLN, 1)
                    nc.vector.engine_nop().then_inc(aXLF, 1)

                for j in range(7):
                    fe_dve(j)
                for st in range(NST):
                    sl = st % SPB
                    for u in range(8):
                        vector.wait_ge(aPJ, 24 * st + 16 + u + 1)
                        bank = ps[(16 + u) % 4]
                        tt = u // 2
                        half = u % 2
                        dst = v_sb[:, tt * D + half * 512: tt * D + half * 512 + 512]
                        vector.tensor_add(dst, bank[:, :],
                                          bv_bc[:, half * 512: half * 512 + 512]
                                          ).then_inc(aVE, 1)
                    grp_by_tt = {}
                    for rec in SGROUPS[st]:
                        grp_by_tt.setdefault(rec[0], []).append(rec)
                    for tt in range(4):
                        k = 4 * st + tt
                        kp = k % 2
                        qt = sl * 4 + tt
                        recs = grp_by_tt[tt]
                        vector.memset(mrun_sb[0][:, :], 0.0)
                        vector.memset(srun_sb[0][:, :], 0.0)
                        for (_tt, g, ng, is_diag, bank, sgidx) in recs:
                            par = g % 2
                            nxt = (g + 1) % 2
                            vector.wait_ge(aSG, sgidx + 1)
                            if is_diag:
                                vbar(vector.tensor_add(masked_sb[:, :],
                                                       ps[bank][:, 0:128],
                                                       mask_sb[:, :]))
                                src = masked_sb[:, :]
                            else:
                                src = ps[bank][:, 0:ng]
                            vector.reduce_max(gm_sb[:, :], src, AX)
                            vbar()
                            vbar(vector.tensor_max(mrun_sb[nxt][:, :],
                                                   mrun_sb[par][:, :],
                                                   gm_sb[:, :]))
                            vector.tensor_sub(delta_sb[par][:, :],
                                              mrun_sb[par][:, :], mrun_sb[nxt][:, :])
                            vector.tensor_scalar(mneg_sb[par][:, :],
                                                 mrun_sb[nxt][:, :], -1.0 / 32.0,
                                                 None, ALU.mult).then_inc(aMN, 1)
                            vector.wait_ge(aES, sgidx + 1)
                            vector.scalar_tensor_tensor(
                                srun_sb[nxt][:, :], srun_sb[par][:, :],
                                resc_sb[par][:, :], esum_sb[par][:, :],
                                ALU.mult, ALU.add)
                        lastpar = len(recs) % 2
                        vector.wait_ge(aEM, 4 * st + tt + 1)
                        vbar(vector.scalar_tensor_tensor(
                            den_sb[:, :], resc_sb[lastpar][:, :],
                            cnt_sb[:, qt:qt + 1], srun_sb[lastpar][:, :],
                            ALU.mult, ALU.add))
                        vbar(vector.reciprocal(rden_sb[:, :], den_sb[:, :]))
                        self_wait(vector.tensor_mul(rs_sb[:, :], den_sb[:, :],
                                                    rden_sb[:, :]))
                        vector.wait_ge(x2_sem(k), 16 * (k // 2 + 1))
                        if k >= 2:
                            vector.wait_ge(os_sem(k), 16 * (k // 2))
                        vector.scalar_tensor_tensor(
                            out1b[kp][:, :],
                            v_sb[:, tt * D:(tt + 1) * D], rs_sb[:, :],
                            x2b[kp][:, :], ALU.mult, ALU.add)
                        nc.vector.engine_nop().then_inc(aO1, 1)
                        nc.vector.engine_nop().then_inc(aX2F, 1)
                        vbar()
                        vector.reduce_sum(msum2_sb[:, kp:kp + 1],
                                          out1b[kp][:, :], AX)
                        vbar()
                        vbar(vector.tensor_scalar(m2_sb[:, kp:kp + 1],
                                                  msum2_sb[:, kp:kp + 1],
                                                  1.0 / D, None, ALU.mult))
                        vector.wait_ge(aSQ2, k + 1)
                        vbar(vector.tensor_mul(mm2_sb[:, kp:kp + 1],
                                               m2_sb[:, kp:kp + 1],
                                               m2_sb[:, kp:kp + 1]))
                        vector.scalar_tensor_tensor(
                            v2_sb[:, kp:kp + 1], sqacc2_sb[:, kp:kp + 1],
                            1.0 / D, mm2_sb[:, kp:kp + 1], ALU.mult,
                            ALU.subtract).then_inc(aVAR2, 1)
                        nc.vector.wait_ge(aVAR2, k + 1)
                        if k + 7 < NSUB:
                            fe_dve(k + 7)
                        vector.wait_ge(aRSTD2, k + 1)
                        if k >= 2:
                            vector.wait_ge(ys_sem(k), 16 * (k // 2))
                        vector.tensor_scalar(ybuf[kp][:, :], out1b[kp][:, :],
                                             m2_sb[:, kp:kp + 1],
                                             rstd2_sb[:, kp:kp + 1],
                                             ALU.subtract, ALU.mult)
                        nc.vector.engine_nop().then_inc(aY, 1)
                        vbar()
                        vector.reduce_max(ymt_sb[:, :], ybuf[kp][:, :], AX)
                        vbar()
                        vbar(vector.tensor_max(
                            ymax_sb[:, (k + 1) % 2:(k + 1) % 2 + 1],
                            ymax_sb[:, k % 2:k % 2 + 1], ymt_sb[:, :]))

        pa.close()
        if BUILD_PHASES < 3:
            nc.finalize()
            return nc

        with ExitStack() as ccom:
            def csb(*a):
                return ccom.enter_context(nc.sbuf_tensor(*a))
            w2qT_sb = csb("w2qT_sb", [128, 32 * D], F16)
            prstack = ExitStack()
            psR = [prstack.enter_context(nc.psum_tensor(f"psR{i}", [128, 512], F32))
                   for i in range(2)]

            with ExitStack() as bstack:
                def bsb(*a):
                    return bstack.enter_context(nc.sbuf_tensor(*a))
                w1qT_sb = bsb("w1qT_sb", [128, 8 * FF], F16)
                y_fm = [bsb(f"y_fm{i}", [128, 8 * 512], F16) for i in range(2)]
                xq_sb = [bsb(f"xq{i}", [128, 8 * 512], F16) for i in range(2)]
                yq_u8 = bsb("yq_u8", [128, 512], U8)
                h_tile = [bsb(f"h_tile{i}", [128, 512], F16) for i in range(4)]
                hmc_sb = bsb("hmc_sb", [128, 32], F32)
                hmt_sb = bsb("hmt_sb", [128, 1], F32)
                b1i_sb = bsb("b1i_sb", [128, 32], I32)
                b1f_sb = bsb("b1f_sb", [128, 32], F32)
                rcs1_sb = bsb("rcs1_sb", [128, 32], F32)
                ps_b = [bstack.enter_context(
                    nc.psum_tensor(f"psHB{i}", [128, 512], F32)) for i in range(4)]

                ar1blk = nc.Block()
                ar1blk.__enter__()

                @ar1blk.sync
                def _(sync):
                    if SKIP_AR1:
                        return
                    ymfin = ymax_sb[:, NST % 2:NST % 2 + 1]
                    sync.dma_start(out=red1_h[:, :], in_=ymfin).then_inc(r1S, 16)
                    sync.wait_ge(r1S, 16)
                    sync.dma_start(out=redt_sb[:, :],
                                   in_=red1_h[:, :]).then_inc(r1S, 16)
                    sync.wait_ge(r1D, 1)
                    sync.dma_start(out=cc1i_h[:, :],
                                   in_=cc_sb[:, :]).then_inc(r1S, 16)
                    sync.wait_ge(r1Cc, 1)
                    sync.dma_start(out=g1_sb[:, :],
                                   in_=cc1o_h[0:1, 0:1]).then_inc(r1C, 16)

                @ar1blk.vector
                def _(vector):
                    if SKIP_AR1:
                        return
                    vector.wait_ge(r1S, 32)
                    vector.reduce_max(g1_sb[:, :], redt_sb[:, :], AX)
                    vbar()
                    vbar(vector.memset(cc_sb[:, :], 0.0))
                    vector.tensor_copy(cc_sb[0:1, 0:1], g1_sb[:, :]).then_inc(r1D, 1)
                    vector.wait_ge(r1C, 16)
                    vector.tensor_scalar(s1v_sb[:, :], g1_sb[:, :], EPS_Q,
                                         1.0 / 255.0, ALU.max,
                                         ALU.mult).then_inc(r1D, 1)
                    vector.wait_ge(r1P, 1)
                    vector.tensor_copy(s1b_sb[:, :],
                                       psR[0][:, 0:1]).then_inc(r1D, 1)
                    nc.vector.wait_ge(r1D, 3)
                    vector.reciprocal(r1b_sb[:, :], s1b_sb[:, :])
                    vector.tensor_scalar(scale1_sb[:, :], sw1_sb[:, :],
                                         s1b_sb[:, :], 1.0 / 127.0, ALU.mult,
                                         ALU.mult).then_inc(r1D, 1)
                    nc.vector.wait_ge(r1D, 4)
                    vbar(vector.reciprocal(rcs1_sb[:, :], scale1_sb[:, :]))
                    vbar(vector.tensor_mul(b1f_sb[:, :], b1_sb[:, :], rcs1_sb[:, :]))
                    vbar(vector.tensor_copy(b1i_sb[:, :], b1f_sb[:, :]))
                    vbar(vector.tensor_copy(b1f_sb[:, :], b1i_sb[:, :]))
                    vbar(vector.tensor_mul(b1q_sb[:, :], b1f_sb[:, :],
                                           scale1_sb[:, :]))

                @ar1blk.gpsimd
                def _(gpsimd):
                    if SKIP_AR1:
                        return
                    gpsimd.wait_ge(r1S, 48)
                    gpsimd.collective_compute(
                        "AllReduce", ALU.max,
                        replica_groups=[list(range(NCORES))],
                        ins=[cc1i_h.ap().opt()],
                        outs=[cc1o_h.ap().opt()]).then_inc(r1Cc, 1)

                @ar1blk.tensor
                def _(tensor):
                    if SKIP_AR1:
                        return
                    tensor.wait_ge(r1D, 2)
                    tensor.matmul(psR[0][:, 0:1], ones_sb[:, :], s1v_sb[:, :],
                                  start=True, stop=True).then_inc(r1P, 1)

                @ar1blk.scalar
                def _(scalar):
                    if SKIP_AR1:
                        return
                    for dc in range(8):
                        scalar.dma_start_transpose(
                            w1qT_sb[:, dc * FF:(dc + 1) * FF],
                            w1q_h[0:FF, dc * 128:(dc + 1) * 128]).then_inc(r1T, 16)
                    for fc in range(32):
                        scalar.dma_start_transpose(
                            w2qT_sb[:, fc * D:(fc + 1) * D],
                            w2q_h[0:D, fc * 128:(fc + 1) * 128]).then_inc(r1T, 16)
                    scalar.wait_ge(r1T, 16 * 40)

                ar1blk.__exit__(None, None, None)
                if BUILD_PHASES < 4:
                    nc.finalize()
                    return nc
                bblk = bstack.enter_context(nc.Block())

                @bblk.vector
                def _(vector):
                    def hmax_group(g):
                        for fc in range(32):
                            u = 32 * g + fc
                            nc.vector.wait_ge(bHE, u + 1)
                            nc.vector.reduce_max(hmc_sb[:, fc:fc + 1],
                                                 h_tile[fc % 4][:, :], AX)
                            nc.vector.engine_nop().then_inc(bHR, 1)
                        nc.vector.wait_ge(bHR, 32 * (g + 1))
                        nc.vector.reduce_max(hmt_sb[:, :], hmc_sb[:, :], AX)
                        vbar()
                        nc.vector.tensor_max(
                            hmax_sb[:, (g + 1) % 2:(g + 1) % 2 + 1],
                            hmax_sb[:, g % 2:g % 2 + 1], hmt_sb[:, :])

                    for g in range(16):
                        vector.wait_ge(bYL, 128 * (g + 1))
                        if g >= 2 and B_STAGES >= 2:
                            vector.wait_ge(bHM, 32 * (g - 1))
                        for c in range(8):
                            sl_ = slice(c * 512, (c + 1) * 512)
                            vbar(vector.tensor_scalar(yq_u8[:, :],
                                                      y_fm[g % 2][:, sl_],
                                                      r1b_sb[:, :], None,
                                                      ALU.mult))
                            vector.tensor_copy(xq_sb[g % 2][:, sl_], yq_u8[:, :])
                        nc.vector.engine_nop().then_inc(bYQ, 1)
                        nc.vector.engine_nop().then_inc(bXQ, 1)
                        if g >= 1 and B_STAGES >= 3 and B_HMAX:
                            hmax_group(g - 1)
                    if B_STAGES >= 3 and B_HMAX:
                        hmax_group(15)

                @bblk.tensor
                def _(tensor):
                    if B_STAGES < 2:
                        return
                    for g in range(16):
                        tensor.wait_ge(bXQ, g + 1)
                        for fc in range(32):
                            u = 32 * g + fc
                            if u >= 4 and B_STAGES >= 3:
                                tensor.wait_ge(bHE, u - 3)
                            for dc in range(8):
                                ins = tensor.matmul(
                                    ps_b[fc % 4][:, :],
                                    w1qT_sb[:, dc * FF + fc * 128:
                                            dc * FF + (fc + 1) * 128],
                                    xq_sb[g % 2][:, dc * 512:(dc + 1) * 512],
                                    start=(dc == 0), stop=(dc == 7))
                            ins.then_inc(bHM, 1)

                @bblk.scalar
                def _(scalar):
                    def yfm_loads(g):
                        if g >= 2:
                            scalar.wait_ge(bYQ, g - 1)
                        for c in range(8):
                            scalar.dma_start_transpose(
                                y_fm[g % 2][:, c * 512:(c + 1) * 512],
                                y16_h[g * 512:(g + 1) * 512,
                                      c * 128:(c + 1) * 128]).then_inc(bYL, 16)

                    def evac_group(g):
                        for fc in range(32):
                            u = 32 * g + fc
                            scalar.wait_ge(bHM, u + 1)
                            if u >= 4:
                                if B_STAGES >= 4:
                                    scalar.wait_ge([bHS0, bHS1, bHS2, bHS3][u % 4],
                                                   16 * (u // 4))
                                if B_HMAX:
                                    scalar.wait_ge(bHR, u - 3)
                            if B_SIMPLE_EVAC:
                                scalar.copy(h_tile[fc % 4][:, :],
                                            ps_b[fc % 4][:, :]).then_inc(bHE, 1)
                            else:
                                scalar.activation(h_tile[fc % 4][:, :],
                                                  ps_b[fc % 4][:, :], ACT.Relu,
                                                  bias=b1q_sb[:, fc:fc + 1],
                                                  scale=scale1_sb[:, fc:fc + 1]
                                                  ).then_inc(bHE, 1)
                            if B_STAGES >= 4:
                                scalar.wait_ge(bHE, u + 1)
                                scalar.dma_start(out=h16_h[fc, g],
                                                 in_=h_tile[fc % 4][:, :]
                                                 ).then_inc([bHS0, bHS1, bHS2, bHS3][u % 4], 16)

                    for g in range(16):
                        yfm_loads(g)
                        if B_STAGES >= 3 and g >= 1:
                            evac_group(g - 1)
                    if B_STAGES >= 3:
                        evac_group(15)
                    if B_STAGES >= 4:
                        for bb in range(4):
                            scalar.wait_ge([bHS0, bHS1, bHS2, bHS3][bb], 16 * 8 * 16)

            if BUILD_PHASES < 5:
                nc.finalize()
                return nc
            with ExitStack() as cstack:
                def c2sb(*a):
                    return cstack.enter_context(nc.sbuf_tensor(*a))
                SCALE2B = c2sb("SCALE2B", [128, D], F32)
                B2QB = c2sb("B2QB", [128, D], F32)

                arstack = cstack.enter_context(ExitStack())

                def arsb(*a):
                    return arstack.enter_context(nc.sbuf_tensor(*a))
                b2row_sb = arsb("b2row_sb", [1, D], F32)
                s2wrow_sb = arsb("s2wrow_sb", [1, D], F32)
                scale2row_sb = arsb("scale2row_sb", [1, D], F32)
                b2qrow_sb = arsb("b2qrow_sb", [1, D], F32)
                rc2row_sb = arsb("rc2row_sb", [1, D], F32)
                tmp1r_sb = arsb("tmp1r_sb", [1, D], F32)
                tmpi1r_sb = arsb("tmpi1r_sb", [1, D], I32)

                ar2blk = nc.Block()
                ar2blk.__enter__()

                @ar2blk.sync
                def _(sync):
                    hmfin = hmax_sb[:, 16 % 2:16 % 2 + 1]
                    sync.dma_start(out=red2_h[:, :], in_=hmfin).then_inc(r2S, 16)
                    sync.wait_ge(r2S, 16)
                    sync.dma_start(out=redt_sb[:, :],
                                   in_=red2_h[:, :]).then_inc(r2S, 16)
                    sync.dma_start(out=s2wrow_sb[:, :],
                                   in_=sw2d_h[:, :]).then_inc(r2S, 16)
                    sync.dma_start(out=b2row_sb[:, :],
                                   in_=b2_h[:, :]).then_inc(r2S, 16)
                    sync.wait_ge(r2D, 1)
                    sync.dma_start(out=cc2i_h[:, :],
                                   in_=cc_sb[:, :]).then_inc(r2S, 16)
                    sync.wait_ge(r2Cc, 1)
                    sync.dma_start(out=g2_sb[:, :],
                                   in_=cc2o_h[0:1, 0:1]).then_inc(r2C, 16)

                @ar2blk.vector
                def _(vector):
                    vector.wait_ge(r2S, 32)
                    vector.reduce_max(g2_sb[:, :], redt_sb[:, :], AX)
                    vbar()
                    vbar(vector.memset(cc_sb[:, :], 0.0))
                    vector.tensor_copy(cc_sb[0:1, 0:1], g2_sb[:, :]).then_inc(r2D, 1)
                    vector.wait_ge(r2C, 16)
                    vector.tensor_scalar(s2v_sb[:, :], g2_sb[:, :], EPS_Q,
                                         1.0 / 255.0, ALU.max,
                                         ALU.mult).then_inc(r2D, 1)
                    nc.vector.wait_ge(r2D, 2)
                    vector.wait_ge(r2S, 64)
                    vbar(vector.tensor_scalar(scale2row_sb[:, :], s2wrow_sb[:, :],
                                              s2v_sb[:, :], 1.0 / 127.0, ALU.mult,
                                              ALU.mult))
                    vbar(vector.reciprocal(rc2row_sb[:, :], scale2row_sb[:, :]))
                    vbar(vector.tensor_mul(tmp1r_sb[:, :], b2row_sb[:, :],
                                           rc2row_sb[:, :]))
                    vbar(vector.tensor_copy(tmpi1r_sb[:, :], tmp1r_sb[:, :]))
                    vbar(vector.tensor_copy(tmp1r_sb[:, :], tmpi1r_sb[:, :]))
                    vector.tensor_mul(b2qrow_sb[:, :], tmp1r_sb[:, :],
                                      scale2row_sb[:, :]).then_inc(r2D, 1)
                    vector.wait_ge(r2P, 2)
                    vector.tensor_copy(SCALE2B[:, 0:512], psR[0][:, :])
                    vector.tensor_copy(SCALE2B[:, 512:1024],
                                       psR[1][:, :]).then_inc(r2D, 1)
                    vector.wait_ge(r2P, 4)
                    vector.tensor_copy(B2QB[:, 0:512], psR[0][:, :])
                    vector.tensor_copy(B2QB[:, 512:1024],
                                       psR[1][:, :]).then_inc(r2D, 1)
                    vector.wait_ge(r2P, 5)
                    vector.tensor_copy(s2b_sb[:, :],
                                       psR[0][:, 0:1]).then_inc(r2D, 1)
                    nc.vector.wait_ge(r2D, 6)
                    vector.reciprocal(r2b_sb[:, :], s2b_sb[:, :])

                @ar2blk.gpsimd
                def _(gpsimd):
                    gpsimd.wait_ge(r2S, 80)
                    gpsimd.collective_compute(
                        "AllReduce", ALU.max,
                        replica_groups=[list(range(NCORES))],
                        ins=[cc2i_h.ap().opt()],
                        outs=[cc2o_h.ap().opt()]).then_inc(r2Cc, 1)

                @ar2blk.tensor
                def _(tensor):
                    tensor.wait_ge(r2D, 3)
                    tensor.matmul(psR[0][:, :], ones_sb[:, :],
                                  scale2row_sb[0:1, 0:512], start=True,
                                  stop=True).then_inc(r2P, 1)
                    tensor.matmul(psR[1][:, :], ones_sb[:, :],
                                  scale2row_sb[0:1, 512:1024], start=True,
                                  stop=True).then_inc(r2P, 1)
                    tensor.wait_ge(r2D, 4)
                    tensor.matmul(psR[0][:, :], ones_sb[:, :],
                                  b2qrow_sb[0:1, 0:512], start=True,
                                  stop=True).then_inc(r2P, 1)
                    tensor.matmul(psR[1][:, :], ones_sb[:, :],
                                  b2qrow_sb[0:1, 512:1024], start=True,
                                  stop=True).then_inc(r2P, 1)
                    tensor.wait_ge(r2D, 5)
                    tensor.matmul(psR[0][:, 0:1], ones_sb[:, :], s2v_sb[:, :],
                                  start=True, stop=True).then_inc(r2P, 1)

                ar2blk.__exit__(None, None, None)
                arstack.close()
                prstack.close()
                if BUILD_PHASES < 6:
                    nc.finalize()
                    return nc
                ps_c = [cstack.enter_context(
                    nc.psum_tensor(f"psOC{i}", [128, 512], F32)) for i in range(8)]
                h_ld = [c2sb(f"h_ld{i}", [128, 32 * 512], F16) for i in range(2)]
                o1_ld = [c2sb(f"o1_ld{i}", [128, 4 * D], F16) for i in range(2)]
                fout_sb = c2sb("fout_sb", [128, 4 * D], F32)
                hu8_sb = [c2sb(f"hu8{i}", [128, 512], U8) for i in range(2)]
                eb1_sb = c2sb("eb1_sb", [128, 512], F32)
                eb2_sb = c2sb("eb2_sb", [128, 512], F32)

                cblk = cstack.enter_context(nc.Block())

                @cblk.sync
                def _(sync):
                    for g in range(16):
                        if g >= 2:
                            sync.wait_ge(cMM, 4 * (g - 1))
                            sync.wait_ge(cFO, g - 1)
                        for fc in range(32):
                            sync.dma_start(
                                out=h_ld[g % 2][:, fc * 512:(fc + 1) * 512],
                                in_=h16_h[fc, g]).then_inc(cHL, 16)
                        for tt in range(4):
                            r0 = g * 512 + tt * 128
                            sync.dma_start(out=o1_ld[g % 2][:, tt * D:(tt + 1) * D],
                                           in_=o16_h[r0:r0 + 128, :]
                                           ).then_inc(cOL, 16)
                        if g >= 1:
                            sync.wait_ge(cFO, g)
                            for tt in range(4):
                                r0 = (g - 1) * 512 + tt * 128
                                sync.dma_start(out=out_h[r0:r0 + 128, :],
                                               in_=fout_sb[:, tt * D:(tt + 1) * D]
                                               ).then_inc(cOS, 16)
                    sync.wait_ge(cFO, 16)
                    for tt in range(4):
                        r0 = 15 * 512 + tt * 128
                        sync.dma_start(out=out_h[r0:r0 + 128, :],
                                       in_=fout_sb[:, tt * D:(tt + 1) * D]
                                       ).then_inc(cOS, 16)
                    sync.wait_ge(cOS, 16 * 64)
                    if DEBUG_TAPS:
                        sync.dma_start(out=dbg_h[:, :], in_=dbg_sb[:, :]).then_inc(cOS, 16)
                        sync.wait_ge(cOS, 16 * 65)

                @cblk.vector
                def _(vector):
                    def epilogue_group(g):
                        if g >= 1:
                            nc.vector.wait_ge(cOS, 64 * g)
                        nc.vector.wait_ge(cOL, 64 * min(g + 2, 16))
                        for tt in range(4):
                            nc.vector.wait_ge(cMM, 4 * g + tt + 1)
                            for half in range(2):
                                bank = ps_c[2 * tt + half]
                                sl_ = slice(half * 512, half * 512 + 512)
                                osl = slice(tt * D + half * 512,
                                            tt * D + half * 512 + 512)
                                vbar(nc.vector.tensor_mul(eb1_sb[:, :],
                                                          bank[:, :],
                                                          SCALE2B[:, sl_]))
                                vbar(nc.vector.tensor_add(eb2_sb[:, :],
                                                          eb1_sb[:, :],
                                                          B2QB[:, sl_]))
                                nc.vector.tensor_add(fout_sb[:, osl],
                                                     eb2_sb[:, :],
                                                     o1_ld[g % 2][:, osl])
                            nc.vector.engine_nop().then_inc(cOE, 1)
                        nc.vector.engine_nop().then_inc(cFO, 1)

                    for g in range(16):
                        vector.wait_ge(cHL, 512 * (g + 1))
                        for fc in range(32):
                            u = 32 * g + fc
                            if u >= 2:
                                vector.wait_ge(cQ2, u - 1)
                            vector.tensor_scalar(
                                hu8_sb[fc % 2][:, :],
                                h_ld[g % 2][:, fc * 512:(fc + 1) * 512],
                                r2b_sb[:, :], None, ALU.mult).then_inc(cQ1, 1)
                        if g >= 1:
                            epilogue_group(g - 1)
                    epilogue_group(15)

                @cblk.scalar
                def _(scalar):
                    for g in range(16):
                        for fc in range(32):
                            u = 32 * g + fc
                            scalar.wait_ge(cQ1, u + 1)
                            scalar.copy(h_ld[g % 2][:, fc * 512:(fc + 1) * 512],
                                        hu8_sb[fc % 2][:, :]).then_inc(cQ2, 1)

                @cblk.tensor
                def _(tensor):
                    for g in range(16):
                        for tt in range(4):
                            tu = 4 * g + tt
                            if tu >= 4:
                                tensor.wait_ge(cOE, tu - 3)
                            b0 = ps_c[2 * tt]
                            b1 = ps_c[2 * tt + 1]
                            for fc in range(32):
                                if tt == 0:
                                    tensor.wait_ge(cQ2, 32 * g + fc + 1)
                                lhs = h_ld[g % 2][:, fc * 512 + tt * 128:
                                                  fc * 512 + (tt + 1) * 128]
                                tensor.matmul(b0[:, :], lhs,
                                              w2qT_sb[:, fc * D: fc * D + 512],
                                              start=(fc == 0), stop=(fc == 31))
                                ins = tensor.matmul(b1[:, :], lhs,
                                                    w2qT_sb[:, fc * D + 512:
                                                            fc * D + 1024],
                                                    start=(fc == 0),
                                                    stop=(fc == 31))
                            ins.then_inc(cMM, 1)

    nc.finalize()
    return nc


def kernel(L0_attn_X, L0_attn_Wq, L0_attn_bq, L0_attn_Wk, L0_attn_bk,
           L0_attn_Wv, L0_attn_bv, L0_ffn_w1, L0_ffn_b1, L0_ffn_w2, L0_ffn_b2):
    X = np.ascontiguousarray(np.asarray(L0_attn_X, dtype=np.float32))
    wq = np.ascontiguousarray(np.asarray(L0_attn_Wq, dtype=np.float32))
    wk = np.ascontiguousarray(np.asarray(L0_attn_Wk, dtype=np.float32))
    wv = np.ascontiguousarray(np.asarray(L0_attn_Wv, dtype=np.float32))
    bq = np.asarray(L0_attn_bq, dtype=np.float32)
    bk = np.asarray(L0_attn_bk, dtype=np.float32)
    bv = np.asarray(L0_attn_bv, dtype=np.float32)
    w1 = np.ascontiguousarray(np.asarray(L0_ffn_w1, dtype=np.float32))
    b1 = np.asarray(L0_ffn_b1, dtype=np.float32)
    w2 = np.ascontiguousarray(np.asarray(L0_ffn_w2, dtype=np.float32))
    b2 = np.asarray(L0_ffn_b2, dtype=np.float32)

    bqr = np.ascontiguousarray(bq.reshape(8, 128).T)
    bkr = np.ascontiguousarray(bk.reshape(8, 128).T)
    bvr = np.ascontiguousarray(bv.reshape(1, D))
    b1r = np.ascontiguousarray(b1.reshape(32, 128).T)
    b2r = np.ascontiguousarray(b2.reshape(1, D))
    ii = np.arange(128)
    mask = np.where(ii[None, :] <= ii[:, None], 0.0, NEG).astype(np.float16)
    qts = np.arange(32)
    cnt = (S - 1 - (qts[None, :] * 128 + ii[:, None])).astype(np.float32)
    onesf = np.ones((1, 128), np.float32)

    nc = build_nc()

    in_maps = []
    for c in range(NCORES):
        xs = np.ascontiguousarray(X[2 * c:2 * c + 2].reshape(T, D))
        in_maps.append(dict(x=xs, wq=wq, wk=wk, wv=wv, bqr=bqr, bkr=bkr,
                            bvr=bvr, w1=w1, b1r=b1r, w2=w2, b2r=b2r,
                            mask=mask, cnt=cnt, onesf=onesf))
    res = run_bass_kernel_spmd(nc, in_maps, core_ids=list(range(NCORES)))
    out = np.concatenate([np.asarray(res.results[c]["out"], np.float32)
                          for c in range(NCORES)], axis=0)
    return out.reshape(16, S, D)


# version 11
# speedup vs baseline: 1.1020x; 1.0118x over previous
import numpy as np
from contextlib import ExitStack

import concourse.bass as bass
import concourse.mybir as mybir
from concourse.bass_utils import run_bass_kernel_spmd

F32 = mybir.dt.float32
F16 = mybir.dt.float16
U8 = mybir.dt.uint8
I8 = mybir.dt.int8
I32 = mybir.dt.int32
ALU = mybir.AluOpType
ACT = mybir.ActivationFunctionType
AX = mybir.AxisListType.X

NCORES = 8
S = 4096
D = 1024
FF = 4096
T = 2 * S
NST = T // 512
SPB = S // 512
EPS_LN = 1e-5
EPS_Q = 1e-8
NEG = -60000.0


def _sgroups():
    out = []
    sgidx = 0
    for st in range(NST):
        sl = st % SPB
        rows = []
        for tt in range(4):
            qt = sl * 4 + tt
            gf = (qt * 128 + 511) // 512
            for g in range(gf):
                ng = min(512, qt * 128 - g * 512)
                rows.append((tt, g, ng, False, sgidx % 8, sgidx))
                sgidx += 1
            rows.append((tt, gf, 128, True, sgidx % 8, sgidx))
            sgidx += 1
        out.append(rows)
    return out


SGROUPS = _sgroups()
LASTUSE = [{b: max(r[5] for r in rows if r[4] == b)
            for b in set(r[4] for r in rows)} for rows in SGROUPS]
BUILD_PHASES = 6
B_STAGES = 4
B_HMAX = True
B_SIMPLE_EVAC = False
SKIP_A = False
SKIP_AR1 = False
SKIP_P = False
DEBUG_TAPS = False


def build_nc():
    nc = bass.Bass("TRN2", target_bir_lowering=False,
                   detect_race_conditions=False)

    x_h = nc.dram_tensor("x", [T, D], F32, kind="ExternalInput")
    wq_h = nc.dram_tensor("wq", [D, D], F32, kind="ExternalInput")
    wk_h = nc.dram_tensor("wk", [D, D], F32, kind="ExternalInput")
    wv_h = nc.dram_tensor("wv", [D, D], F32, kind="ExternalInput")
    bq_h = nc.dram_tensor("bqr", [128, 8], F32, kind="ExternalInput")
    bk_h = nc.dram_tensor("bkr", [128, 8], F32, kind="ExternalInput")
    bv_h = nc.dram_tensor("bvr", [1, D], F32, kind="ExternalInput")
    w1_h = nc.dram_tensor("w1", [FF, D], F32, kind="ExternalInput")
    b1_h = nc.dram_tensor("b1r", [128, 32], F32, kind="ExternalInput")
    w2_h = nc.dram_tensor("w2", [D, FF], F32, kind="ExternalInput")
    b2_h = nc.dram_tensor("b2r", [1, D], F32, kind="ExternalInput")
    mask_h = nc.dram_tensor("mask", [128, 128], F16, kind="ExternalInput")
    cnt_h = nc.dram_tensor("cnt", [128, 32], F32, kind="ExternalInput")
    ones_h = nc.dram_tensor("onesf", [1, 128], F32, kind="ExternalInput")
    out_h = nc.dram_tensor("out", [T, D], F32, kind="ExternalOutput")

    KI = "ExternalOutput" if DEBUG_TAPS else "Internal"
    dbg_h = nc.dram_tensor("dbg", [128, 64], F32, kind=KI)
    y16_h = nc.dram_tensor("y16", [T, D], F16, kind=KI)
    o16_h = nc.dram_tensor("o16", [T, D], F16, kind=KI)
    w1q_h = nc.dram_tensor("w1qs", [FF, D], F16, kind=KI)
    w2q_h = nc.dram_tensor("w2qs", [D, FF], F16, kind=KI)
    h16_h = nc.dram_tensor("h16", [32, 16, 128, 512], F16, kind=KI)
    red1_h = nc.dram_tensor("red1", [1, 128], F32, kind="Internal")
    red2_h = nc.dram_tensor("red2", [1, 128], F32, kind="Internal")
    sw2d_h = nc.dram_tensor("sw2d", [1, D], F32, kind=KI)
    cc1i_h = nc.dram_tensor("cc1i", [128, 4], F32, kind="Internal")
    cc1o_h = nc.dram_tensor("cc1o", [128, 4], F32, kind="Internal")
    cc2i_h = nc.dram_tensor("cc2i", [128, 4], F32, kind="Internal")
    cc2o_h = nc.dram_tensor("cc2o", [128, 4], F32, kind="Internal")

    with ExitStack() as outer:
        def osb(*a):
            return outer.enter_context(nc.sbuf_tensor(*a))

        def sem(name):
            return outer.enter_context(nc.semaphore(name))

        b1_sb = osb("b1_sb", [128, 32], F32)
        sw1_sb = osb("sw1_sb", [128, 32], F32)
        scale1_sb = osb("scale1_sb", [128, 32], F32)
        b1q_sb = osb("b1q_sb", [128, 32], F32)
        sw2_sb = osb("sw2_sb", [128, 8], F32)
        ones_sb = osb("ones_sb", [1, 128], F32)
        ymax_sb = osb("ymax_sb", [128, 2], F32)
        hmax_sb = osb("hmax_sb", [128, 2], F32)
        s1b_sb = osb("s1b_sb", [128, 1], F32)
        r1b_sb = osb("r1b_sb", [128, 1], F32)
        s2b_sb = osb("s2b_sb", [128, 1], F32)
        r2b_sb = osb("r2b_sb", [128, 1], F32)
        g1_sb = osb("g1_sb", [1, 1], F32)
        g2_sb = osb("g2_sb", [1, 1], F32)
        s1v_sb = osb("s1v_sb", [1, 1], F32)
        s2v_sb = osb("s2v_sb", [1, 1], F32)
        redt_sb = osb("redt_sb", [1, 128], F32)
        dbg_sb = osb("dbg_sb", [128, 64], F32)
        cc_sb = osb("cc_sb", [128, 4], F32)

        pLD = sem("pLD"); pW = sem("pW"); pWF = sem("pWF"); pQS = sem("pQS")
        pST = sem("pST"); pSLF = sem("pSLF"); pMM = sem("pMM")
        pLDg = sem("pLDg"); r1T = sem("r1T")
        pW0 = sem("pW0"); pW1 = sem("pW1")
        bHS0 = sem("bHS0"); bHS1 = sem("bHS1"); bHS2 = sem("bHS2"); bHS3 = sem("bHS3")
        aXL0 = sem("aXL0"); aXL1 = sem("aXL1"); aX20 = sem("aX20")
        aX21 = sem("aX21"); aLN = sem("aLN"); aTR0 = sem("aTR0")
        aTR1 = sem("aTR1"); aPJ = sem("aPJ"); aQK = sem("aQK")
        aVE = sem("aVE"); aSG = sem("aSG"); aMN = sem("aMN")
        aES = sem("aES"); aEM = sem("aEM"); aO1 = sem("aO1")
        aOS0 = sem("aOS0"); aOS1 = sem("aOS1"); aY = sem("aY")
        aYS0 = sem("aYS0"); aYS1 = sem("aYS1"); aSQ1 = sem("aSQ1")
        aSQ2 = sem("aSQ2"); aRSTD1 = sem("aRSTD1"); aRSTD2 = sem("aRSTD2")
        aVAR1 = sem("aVAR1"); aVAR2 = sem("aVAR2"); aXLF = sem("aXLF")
        aX2F = sem("aX2F"); aSLF = sem("aSLF")
        r1S = sem("r1S"); r1C = sem("r1C"); r1D = sem("r1D"); r1P = sem("r1P")
        r1Cc = sem("r1Cc"); r2Cc = sem("r2Cc")
        bYL = sem("bYL"); bYQ = sem("bYQ"); bXQ = sem("bXQ"); bXF = sem("bXF")
        bHM = sem("bHM"); bHE = sem("bHE"); bHS = sem("bHS"); bHR = sem("bHR")
        r2S = sem("r2S"); r2C = sem("r2C"); r2D = sem("r2D"); r2P = sem("r2P")
        cHL = sem("cHL"); cHF = sem("cHF"); cOL = sem("cOL"); cQ1 = sem("cQ1")
        cQ2 = sem("cQ2"); cMM = sem("cMM"); cOE = sem("cOE"); cFO = sem("cFO")
        cOS = sem("cOS")
        vBAR = sem("vBAR"); sBAR = sem("sBAR")

        _vb = [0]
        _sb_ = [0]

        def vbar(ins=None):
            if ins is None:
                ins = nc.vector.engine_nop()
            _vb[0] += 1
            ins.then_inc(vBAR, 1)
            nc.vector.wait_ge(vBAR, _vb[0])

        def sbar(ins):
            _sb_[0] += 1
            ins.then_inc(sBAR, 1)
            nc.scalar.wait_ge(sBAR, _sb_[0])

        pa = outer.enter_context(ExitStack())

        def pasb(*a):
            return pa.enter_context(nc.sbuf_tensor(*a))
        wq_sb = pasb("wq_sb", [128, 8 * D], F16)
        wk_sb = pasb("wk_sb", [128, 8 * D], F16)
        wv_sb = pasb("wv_sb", [128, 8 * D], F16)
        bq_sb = pasb("bq_sb", [128, 8], F32)
        bk_sb = pasb("bk_sb", [128, 8], F32)
        mask_sb = pasb("mask_sb", [128, 128], F16)
        cnt_sb = pasb("cnt_sb", [128, 32], F32)
        eps_sb = pasb("eps_sb", [128, 1], F32)
        bv_bc = pasb("bv_bc", [128, D], F16)

        with ExitStack() as pstack:
            def psb(*a):
                return pstack.enter_context(nc.sbuf_tensor(*a))
            wbuf = [psb(f"wbuf{i}", [128, D], F32) for i in range(2)]
            w2buf = [psb(f"w2buf{i}", [128, FF], F32) for i in range(2)]
            i8b = psb("i8b", [128, D], I8)
            i8b2 = psb("i8b2", [128, FF], I8)
            w1qf = [psb(f"w1qf{i}", [128, D], F16) for i in range(2)]
            w2qf = [psb(f"w2qf{i}", [128, FF], F16) for i in range(2)]
            rw_sb = psb("rw_sb", [128, 1], F32)
            rw_in = psb("rw_in", [128, 1], F32)
            bvrow_sb = psb("bvrow_sb", [1, D], F32)
            psB = [pstack.enter_context(nc.psum_tensor(f"psB{i}", [128, 512], F32))
                   for i in range(2)]

            pblk = pstack.enter_context(nc.Block())

            @pblk.sync
            def _(sync):
                if SKIP_P:
                    return
                sync.dma_start(out=bq_sb[:, :], in_=bq_h[:, :]).then_inc(pLD, 16)
                sync.dma_start(out=bk_sb[:, :], in_=bk_h[:, :]).then_inc(pLD, 16)
                sync.dma_start(out=b1_sb[:, :], in_=b1_h[:, :]).then_inc(pLD, 16)
                sync.dma_start(out=mask_sb[:, :], in_=mask_h[:, :]).then_inc(pLD, 16)
                sync.dma_start(out=cnt_sb[:, :], in_=cnt_h[:, :]).then_inc(pLD, 16)
                sync.dma_start(out=ones_sb[:, :], in_=ones_h[:, :]).then_inc(pLD, 16)
                sync.dma_start(out=bvrow_sb[:, :], in_=bv_h[:, :]).then_inc(pLD, 16)
                for it in range(42):
                    if it < 40:
                        if it >= 2:
                            sync.wait_ge(pWF, it - 1)
                        if it < 32:
                            sync.dma_start(out=wbuf[it % 2][:, :],
                                           in_=w1_h[it * 128:(it + 1) * 128, :]
                                           ).then_inc(pW1 if it % 2 else pW0, 16)
                        else:
                            dc = it - 32
                            sync.dma_start(out=w2buf[dc % 2][:, :],
                                           in_=w2_h[dc * 128:(dc + 1) * 128, :]
                                           ).then_inc(pW1 if it % 2 else pW0, 16)
                    j = it - 2
                    if j >= 0:
                        sync.wait_ge(pQS, j + 1)
                        if j < 32:
                            sync.dma_start(out=w1q_h[j * 128:(j + 1) * 128, :],
                                           in_=w1qf[j % 2][:, :]).then_inc(pST, 16)
                        else:
                            dc = j - 32
                            sync.dma_start(out=w2q_h[dc * 128:(dc + 1) * 128, :],
                                           in_=w2qf[dc % 2][:, :]).then_inc(pST, 16)
                            sync.dma_start(out=sw2d_h[0:1, dc * 128:(dc + 1) * 128],
                                           in_=sw2_sb[:, dc:dc + 1]).then_inc(pST, 16)

            @pblk.gpsimd
            def _(gpsimd):
                if SKIP_P:
                    return
                for wh, wsb in ((wq_h, wq_sb), (wk_h, wk_sb), (wv_h, wv_sb)):
                    for c in range(8):
                        gpsimd.dma_start(
                            out=wsb[:, c * D:(c + 1) * D],
                            in_=wh[c * 128:(c + 1) * 128, :]).then_inc(pLDg, 16)
                gpsimd.wait_ge(pLDg, 16 * 24)

            @pblk.tensor
            def _(tensor):
                if SKIP_P:
                    return
                tensor.wait_ge(pLD, 16 * 7)
                tensor.matmul(psB[0][:, :], ones_sb[:, :], bvrow_sb[0:1, 0:512],
                              start=True, stop=True).then_inc(pMM, 1)
                tensor.matmul(psB[1][:, :], ones_sb[:, :], bvrow_sb[0:1, 512:1024],
                              start=True, stop=True).then_inc(pMM, 1)

            @pblk.vector
            def _(vector):
                if SKIP_P:
                    return
                vector.memset(eps_sb[:, :], EPS_LN)
                vector.memset(ymax_sb[:, :], 0.0)
                vector.memset(hmax_sb[:, :], 0.0)
                vector.wait_ge(pMM, 2)
                vector.tensor_copy(bv_bc[:, 0:512], psB[0][:, :])
                vector.tensor_copy(bv_bc[:, 512:1024], psB[1][:, :])
                selfc = 0
                for i in range(40):
                    is1 = i < 32
                    buf = wbuf[i % 2] if is1 else w2buf[(i - 32) % 2]
                    i8t = i8b if is1 else i8b2
                    qf = w1qf[i % 2] if is1 else w2qf[(i - 32) % 2]
                    scol = (sw1_sb[:, i:i + 1] if is1
                            else sw2_sb[:, i - 32:i - 31])
                    vector.wait_ge(pW1 if i % 2 else pW0, 16 * (i // 2 + 1))
                    vector.reduce_max(scol, buf[:, :], AX, apply_absolute_value=True)
                    vbar()
                    selfc += 1
                    vector.reciprocal(rw_sb[:, :], scol).then_inc(pSLF, 1)
                    nc.vector.wait_ge(pSLF, selfc)
                    vector.tensor_scalar(i8t[:, :], buf[:, :], rw_sb[:, :],
                                         127.0, ALU.mult,
                                         ALU.mult).then_inc(pWF, 1)
                    nc.vector.wait_ge(pWF, i + 1)
                    if DEBUG_TAPS:
                        vector.tensor_copy(dbg_sb[:, i:i + 1], rw_sb[:, :])
                        if i < 8:
                            vbar(vector.tensor_copy(dbg_sb[:, 40 + i:41 + i],
                                                    scol))
                            vbar(vector.tensor_copy(dbg_sb[:, 48 + i:49 + i],
                                                    buf[:, 0:1]))
                    if i >= 2:
                        j = i - 1
                        pst_need = 16 * (j + 1) if j < 32 else 512 + 32 * (j - 31)
                        vector.wait_ge(pST, pst_need)
                    vector.tensor_copy(qf[:, :], i8t[:, :]).then_inc(pQS, 1)

        if BUILD_PHASES < 2:
            nc.finalize()
            return nc
        with ExitStack() as astack:
            if SKIP_A:
                pass
            def asb(*a):
                return astack.enter_context(nc.sbuf_tensor(*a))
            K_sb = asb("K_sb", [128, 8 * S], F16)
            xa = [asb(f"xa{i}", [128, D], F32) for i in range(2)]
            x2b = [asb(f"x2b{i}", [128, D], F32) for i in range(2)]
            xn_tm = asb("xn_tm", [128, 4 * D], F16)
            xn_fm = [asb(f"xn_fm{i}", [128, 8 * 512], F16) for i in range(2)]
            q_fm = asb("q_fm", [128, 8 * 512], F16)
            v_sb = asb("v_sb", [128, 4 * D], F16)
            out1b = [asb(f"out1b{i}", [128, D], F16) for i in range(2)]
            ybuf = [asb(f"ybuf{i}", [128, D], F16) for i in range(2)]
            e_dump = asb("e_dump", [128, D], F16)
            masked_sb = asb("masked_sb", [128, 128], F32)
            msum_sb = asb("msum_sb", [128, 2], F32)
            m1_sb = asb("m1_sb", [128, 2], F32)
            mm_sb = asb("mm_sb", [128, 2], F32)
            v1_sb = asb("v1_sb", [128, 2], F32)
            sqacc_sb = asb("sqacc_sb", [128, 2], F32)
            lnt1_sb = asb("lnt1_sb", [128, 2], F32)
            rstd_sb = asb("rstd_sb", [128, 2], F32)
            msum2_sb = asb("msum2_sb", [128, 2], F32)
            m2_sb = asb("m2_sb", [128, 2], F32)
            mm2_sb = asb("mm2_sb", [128, 2], F32)
            v2_sb = asb("v2_sb", [128, 2], F32)
            sqacc2_sb = asb("sqacc2_sb", [128, 2], F32)
            lnt2_sb = asb("lnt2_sb", [128, 2], F32)
            rstd2_sb = asb("rstd2_sb", [128, 2], F32)
            mrun_sb = [asb(f"mrun{i}", [128, 1], F32) for i in range(2)]
            mneg_sb = [asb(f"mneg{i}", [128, 1], F32) for i in range(2)]
            delta_sb = [asb(f"delta{i}", [128, 1], F32) for i in range(2)]
            srun_sb = [asb(f"srun{i}", [128, 1], F32) for i in range(2)]
            esum_sb = [asb(f"esum{i}", [128, 1], F32) for i in range(2)]
            resc_sb = [asb(f"resc{i}", [128, 1], F32) for i in range(2)]
            gm_sb = asb("gm_sb", [128, 2], F32)
            den_sb = asb("den_sb", [128, 1], F32)
            rden_sb = asb("rden_sb", [128, 1], F32)
            rs_sb = asb("rs_sb", [128, 1], F32)
            ymt_sb = asb("ymt_sb", [128, 1], F32)
            ps = [astack.enter_context(
                nc.psum_tensor(f"psA{i}", [128, 512], F32)) for i in range(8)]

            ablk = astack.enter_context(nc.Block())

            NSUB = 4 * NST

            def xl_sem(k):
                return aXL0 if k % 2 == 0 else aXL1

            def x2_sem(k):
                return aX20 if k % 2 == 0 else aX21

            def os_sem(k):
                return aOS0 if k % 2 == 0 else aOS1

            def ys_sem(k):
                return aYS0 if k % 2 == 0 else aYS1

            def tr_sem(stx):
                return aTR0 if stx % 2 == 0 else aTR1

            @ablk.sync
            def _(sync):
                if SKIP_A:
                    return
                for j in range(2):
                    sync.dma_start(out=xa[j][:, :],
                                   in_=x_h[j * 128:(j + 1) * 128, :]
                                   ).then_inc(xl_sem(j), 16)
                    sync.dma_start(out=x2b[j][:, :],
                                   in_=x_h[j * 128:(j + 1) * 128, :]
                                   ).then_inc(x2_sem(j), 16)
                for j in range(2, 8):
                    sync.wait_ge(aXLF, j - 1)
                    sync.wait_ge(aSQ1, j - 1)
                    sync.dma_start(out=xa[j % 2][:, :],
                                   in_=x_h[j * 128:(j + 1) * 128, :]
                                   ).then_inc(xl_sem(j), 16)
                for p in range(NSUB):
                    j2 = p + 2
                    if j2 < NSUB:
                        sync.wait_ge(aX2F, p + 1)
                        sync.dma_start(out=x2b[j2 % 2][:, :],
                                       in_=x_h[j2 * 128:(j2 + 1) * 128, :]
                                       ).then_inc(x2_sem(j2), 16)
                    ja = p + 8
                    if ja < NSUB:
                        sync.wait_ge(aXLF, ja - 1)
                        sync.wait_ge(aSQ1, ja - 1)
                        sync.dma_start(out=xa[ja % 2][:, :],
                                       in_=x_h[ja * 128:(ja + 1) * 128, :]
                                       ).then_inc(xl_sem(ja), 16)

            @ablk.gpsimd
            def _(gpsimd):
                if SKIP_A:
                    return

            @ablk.scalar
            def _(scalar):
                if SKIP_A:
                    return

                def fe_sc(kN):
                    p = kN % 2
                    stN = kN // 4
                    ttn = kN % 4
                    scalar.wait_ge(xl_sem(kN), 16 * (kN // 2 + 1))
                    scalar.activation(e_dump[:, 0:D], xa[p][:, :], ACT.Square,
                                      accum_out=sqacc_sb[:, p:p + 1]
                                      ).then_inc(aSQ1, 1)
                    scalar.wait_ge(aVAR1, kN + 1)
                    sbar(scalar.activation(lnt1_sb[:, p:p + 1],
                                           v1_sb[:, p:p + 1], ACT.Ln,
                                           bias=eps_sb[:, :], scale=1.0))
                    scalar.activation(rstd_sb[:, p:p + 1], lnt1_sb[:, p:p + 1],
                                      ACT.Exp, scale=-0.5).then_inc(aRSTD1, 1)
                    scalar.wait_ge(aLN, kN + 1)
                    if stN >= 2:
                        scalar.wait_ge(aPJ, 24 * (stN - 1))
                    buf = xn_fm[stN % 2]
                    for c in range(8):
                        scalar.dma_start_transpose(
                            buf[:, c * 512 + ttn * 128: c * 512 + (ttn + 1) * 128],
                            xn_tm[:, ttn * D + c * 128: ttn * D + (c + 1) * 128]
                        ).then_inc(tr_sem(stN), 16)

                for j in range(7):
                    fe_sc(j)
                for st in range(NST):
                    sl = st % SPB
                    for u in range(16):
                        scalar.wait_ge(aPJ, 24 * st + u + 1)
                        bank = ps[u % 4]
                        if u < 8:
                            dest = q_fm[:, u * 512:(u + 1) * 512]
                            bias = bq_sb[:, u:u + 1]
                        else:
                            c = u - 8
                            dest = K_sb[:, c * S + sl * 512: c * S + (sl + 1) * 512]
                            bias = bk_sb[:, c:c + 1]
                        scalar.activation(dest, bank[:, :], ACT.Identity,
                                          bias=bias, scale=1.0).then_inc(aQK, 1)
                    grp_by_tt = {}
                    for rec in SGROUPS[st]:
                        grp_by_tt.setdefault(rec[0], []).append(rec)
                    for tt in range(4):
                        k = 4 * st + tt
                        kp = k % 2
                        for (_tt, g, ng, is_diag, bank, sgidx) in grp_by_tt[tt]:
                            scalar.wait_ge(aMN, sgidx + 1)
                            par = g % 2
                            nxt = (g + 1) % 2
                            sbar(scalar.activation(mneg_sb[par][:, :],
                                                   mrun_sb[nxt][:, :],
                                                   ACT.Identity,
                                                   scale=-1.0 / 32.0))
                            scalar.activation(resc_sb[par][:, :],
                                              delta_sb[par][:, :],
                                              ACT.Exp, scale=1.0 / 32.0)
                            src = (masked_sb[:, 0:128] if is_diag
                                   else ps[bank][:, 0:ng])
                            scalar.activation(e_dump[:, 0:ng], src, ACT.Exp,
                                              scale=1.0 / 32.0,
                                              bias=mneg_sb[par][:, :],
                                              accum_out=esum_sb[par][:, :]
                                              ).then_inc(aES, 1)
                            if is_diag:
                                scalar.activation(resc_sb[(g + 1) % 2][:, :],
                                                  mneg_sb[par][:, :], ACT.Exp,
                                                  scale=1.0).then_inc(aEM, 1)
                        scalar.wait_ge(aO1, k + 1)
                        scalar.activation(e_dump[:, 0:D], out1b[kp][:, :],
                                          ACT.Square,
                                          accum_out=sqacc2_sb[:, kp:kp + 1]
                                          ).then_inc(aSQ2, 1)
                        scalar.dma_start(out=o16_h[k * 128:(k + 1) * 128, :],
                                         in_=out1b[kp][:, :]
                                         ).then_inc(os_sem(k), 16)
                        scalar.wait_ge(aVAR2, k + 1)
                        sbar(scalar.activation(lnt2_sb[:, kp:kp + 1],
                                               v2_sb[:, kp:kp + 1], ACT.Ln,
                                               bias=eps_sb[:, :], scale=1.0))
                        scalar.activation(rstd2_sb[:, kp:kp + 1],
                                          lnt2_sb[:, kp:kp + 1], ACT.Exp,
                                          scale=-0.5).then_inc(aRSTD2, 1)
                        if k >= 1:
                            scalar.wait_ge(aY, k)
                            scalar.dma_start(
                                out=y16_h[(k - 1) * 128:k * 128, :],
                                in_=ybuf[(k - 1) % 2][:, :]
                            ).then_inc(ys_sem(k - 1), 16)
                        if k + 7 < NSUB:
                            fe_sc(k + 7)
                scalar.wait_ge(aY, NSUB)
                scalar.dma_start(out=y16_h[(NSUB - 1) * 128:NSUB * 128, :],
                                 in_=ybuf[(NSUB - 1) % 2][:, :]
                                 ).then_inc(ys_sem(NSUB - 1), 16)
                scalar.wait_ge(aOS0, 16 * (NSUB // 2))
                scalar.wait_ge(aOS1, 16 * (NSUB // 2))
                scalar.wait_ge(aYS0, 16 * (NSUB // 2))
                scalar.wait_ge(aYS1, 16 * (NSUB // 2))

            @ablk.tensor
            def _(tensor):
                if SKIP_A:
                    return
                for st in range(NST):
                    sl = st % SPB
                    buf = xn_fm[st % 2]
                    tensor.wait_ge(tr_sem(st), 512 * (st // 2 + 1))
                    for u in range(24):
                        pu = 24 * st + u
                        if u < 4:
                            if st >= 1 and u in LASTUSE[st - 1]:
                                tensor.wait_ge(aES, LASTUSE[st - 1][u] + 1)
                        elif pu >= 4:
                            prev = pu - 4
                            pm = prev % 24
                            if pm < 16:
                                tensor.wait_ge(aQK, 16 * (prev // 24) + pm + 1)
                            else:
                                tensor.wait_ge(aVE, 8 * (prev // 24) + pm - 16 + 1)
                        bank = ps[u % 4]
                        if u < 8:
                            for c in range(8):
                                ins = tensor.matmul(
                                    bank[:, :],
                                    wq_sb[:, c * D + u * 128: c * D + (u + 1) * 128],
                                    buf[:, c * 512:(c + 1) * 512],
                                    start=(c == 0), stop=(c == 7))
                        elif u < 16:
                            co = u - 8
                            for c in range(8):
                                ins = tensor.matmul(
                                    bank[:, :],
                                    wk_sb[:, c * D + co * 128: c * D + (co + 1) * 128],
                                    buf[:, c * 512:(c + 1) * 512],
                                    start=(c == 0), stop=(c == 7))
                        else:
                            tt = (u - 16) // 2
                            half = (u - 16) % 2
                            for c in range(8):
                                ins = tensor.matmul(
                                    bank[:, :],
                                    buf[:, c * 512 + tt * 128: c * 512 + (tt + 1) * 128],
                                    wv_sb[:, c * D + half * 512: c * D + half * 512 + 512],
                                    start=(c == 0), stop=(c == 7))
                        ins.then_inc(aPJ, 1)
                    tensor.wait_ge(aQK, 16 * (st + 1))
                    seen = set()
                    for (tt, g, ng, is_diag, bank, sgidx) in SGROUPS[st]:
                        qt = sl * 4 + tt
                        if bank < 4 and bank not in seen:
                            seen.add(bank)
                            tensor.wait_ge(aVE, 8 * st + bank + 5)
                        if sgidx >= 8:
                            tensor.wait_ge(aES, sgidx - 7)
                        k0 = qt * 128 if is_diag else g * 512
                        for c in range(8):
                            ins = tensor.matmul(
                                ps[bank][:, 0:ng],
                                q_fm[:, c * 512 + tt * 128: c * 512 + (tt + 1) * 128],
                                K_sb[:, c * S + k0: c * S + k0 + ng],
                                start=(c == 0), stop=(c == 7))
                        ins.then_inc(aSG, 1)

            @ablk.vector
            def _(vector):
                if SKIP_A:
                    return
                slf = [0]

                def self_wait(ins):
                    slf[0] += 1
                    ins.then_inc(aSLF, 1)
                    nc.vector.wait_ge(aSLF, slf[0])

                def fe_dve(kN):
                    p = kN % 2
                    vector.wait_ge(xl_sem(kN), 16 * (kN // 2 + 1))
                    vector.reduce_sum(msum_sb[:, p:p + 1], xa[p][:, :], AX)
                    vbar()
                    vbar(vector.tensor_scalar(m1_sb[:, p:p + 1],
                                              msum_sb[:, p:p + 1],
                                              1.0 / D, None, ALU.mult))
                    vector.wait_ge(aSQ1, kN + 1)
                    vbar(vector.tensor_mul(mm_sb[:, p:p + 1], m1_sb[:, p:p + 1],
                                           m1_sb[:, p:p + 1]))
                    vector.scalar_tensor_tensor(
                        v1_sb[:, p:p + 1], sqacc_sb[:, p:p + 1], 1.0 / D,
                        mm_sb[:, p:p + 1], ALU.mult,
                        ALU.subtract).then_inc(aVAR1, 1)
                    nc.vector.wait_ge(aVAR1, kN + 1)
                    vector.wait_ge(aRSTD1, kN + 1)
                    if kN >= 4:
                        stp = (kN - 4) // 4
                        vector.wait_ge(tr_sem(stp), 512 * (stp // 2 + 1))
                    ttn = kN % 4
                    vector.tensor_scalar(xn_tm[:, ttn * D:(ttn + 1) * D],
                                         xa[p][:, :], m1_sb[:, p:p + 1],
                                         rstd_sb[:, p:p + 1],
                                         ALU.subtract, ALU.mult)
                    nc.vector.engine_nop().then_inc(aLN, 1)
                    nc.vector.engine_nop().then_inc(aXLF, 1)

                for j in range(7):
                    fe_dve(j)
                for st in range(NST):
                    sl = st % SPB
                    for u in range(8):
                        vector.wait_ge(aPJ, 24 * st + 16 + u + 1)
                        bank = ps[(16 + u) % 4]
                        tt = u // 2
                        half = u % 2
                        dst = v_sb[:, tt * D + half * 512: tt * D + half * 512 + 512]
                        vector.tensor_add(dst, bank[:, :],
                                          bv_bc[:, half * 512: half * 512 + 512]
                                          ).then_inc(aVE, 1)
                    grp_by_tt = {}
                    for rec in SGROUPS[st]:
                        grp_by_tt.setdefault(rec[0], []).append(rec)
                    for tt in range(4):
                        k = 4 * st + tt
                        kp = k % 2
                        qt = sl * 4 + tt
                        recs = grp_by_tt[tt]
                        vector.memset(mrun_sb[0][:, :], 0.0)
                        vector.memset(srun_sb[0][:, :], 0.0)

                        def red(i):
                            (_t, _g, ngi, dia, bki, sgi) = recs[i][0], recs[i][1], recs[i][2], recs[i][3], recs[i][4], recs[i][5]
                            vector.wait_ge(aSG, sgi + 1)
                            if dia:
                                vbar(vector.tensor_add(masked_sb[:, :],
                                                       ps[bki][:, 0:128],
                                                       mask_sb[:, :]))
                                srci = masked_sb[:, :]
                            else:
                                srci = ps[bki][:, 0:ngi]
                            vector.reduce_max(gm_sb[:, i % 2:i % 2 + 1], srci, AX)

                        red(0)
                        for i, (_tt, g, ng, is_diag, bank, sgidx) in enumerate(recs):
                            par = g % 2
                            nxt = (g + 1) % 2
                            if i + 1 < len(recs):
                                red(i + 1)
                            vbar()
                            vbar(vector.tensor_max(mrun_sb[nxt][:, :],
                                                   mrun_sb[par][:, :],
                                                   gm_sb[:, i % 2:i % 2 + 1]))
                            vector.tensor_sub(delta_sb[par][:, :],
                                              mrun_sb[par][:, :],
                                              mrun_sb[nxt][:, :]).then_inc(aMN, 1)
                            vector.wait_ge(aES, sgidx + 1)
                            vector.scalar_tensor_tensor(
                                srun_sb[nxt][:, :], srun_sb[par][:, :],
                                resc_sb[par][:, :], esum_sb[par][:, :],
                                ALU.mult, ALU.add)
                        lastpar = len(recs) % 2
                        vector.wait_ge(aEM, 4 * st + tt + 1)
                        vbar(vector.scalar_tensor_tensor(
                            den_sb[:, :], resc_sb[lastpar][:, :],
                            cnt_sb[:, qt:qt + 1], srun_sb[lastpar][:, :],
                            ALU.mult, ALU.add))
                        vbar(vector.reciprocal(rden_sb[:, :], den_sb[:, :]))
                        self_wait(vector.tensor_mul(rs_sb[:, :], den_sb[:, :],
                                                    rden_sb[:, :]))
                        vector.wait_ge(x2_sem(k), 16 * (k // 2 + 1))
                        if k >= 2:
                            vector.wait_ge(os_sem(k), 16 * (k // 2))
                        vector.scalar_tensor_tensor(
                            out1b[kp][:, :],
                            v_sb[:, tt * D:(tt + 1) * D], rs_sb[:, :],
                            x2b[kp][:, :], ALU.mult, ALU.add)
                        nc.vector.engine_nop().then_inc(aO1, 1)
                        nc.vector.engine_nop().then_inc(aX2F, 1)
                        vbar()
                        vector.reduce_sum(msum2_sb[:, kp:kp + 1],
                                          out1b[kp][:, :], AX)
                        vbar()
                        vbar(vector.tensor_scalar(m2_sb[:, kp:kp + 1],
                                                  msum2_sb[:, kp:kp + 1],
                                                  1.0 / D, None, ALU.mult))
                        vector.wait_ge(aSQ2, k + 1)
                        vbar(vector.tensor_mul(mm2_sb[:, kp:kp + 1],
                                               m2_sb[:, kp:kp + 1],
                                               m2_sb[:, kp:kp + 1]))
                        vector.scalar_tensor_tensor(
                            v2_sb[:, kp:kp + 1], sqacc2_sb[:, kp:kp + 1],
                            1.0 / D, mm2_sb[:, kp:kp + 1], ALU.mult,
                            ALU.subtract).then_inc(aVAR2, 1)
                        nc.vector.wait_ge(aVAR2, k + 1)
                        if k + 7 < NSUB:
                            fe_dve(k + 7)
                        vector.wait_ge(aRSTD2, k + 1)
                        if k >= 2:
                            vector.wait_ge(ys_sem(k), 16 * (k // 2))
                        vector.tensor_scalar(ybuf[kp][:, :], out1b[kp][:, :],
                                             m2_sb[:, kp:kp + 1],
                                             rstd2_sb[:, kp:kp + 1],
                                             ALU.subtract, ALU.mult)
                        nc.vector.engine_nop().then_inc(aY, 1)
                        vbar()
                        vector.reduce_max(ymt_sb[:, :], ybuf[kp][:, :], AX)
                        vbar()
                        vbar(vector.tensor_max(
                            ymax_sb[:, (k + 1) % 2:(k + 1) % 2 + 1],
                            ymax_sb[:, k % 2:k % 2 + 1], ymt_sb[:, :]))

        pa.close()
        if BUILD_PHASES < 3:
            nc.finalize()
            return nc

        with ExitStack() as ccom:
            def csb(*a):
                return ccom.enter_context(nc.sbuf_tensor(*a))
            w2qT_sb = csb("w2qT_sb", [128, 32 * D], F16)
            prstack = ExitStack()
            psR = [prstack.enter_context(nc.psum_tensor(f"psR{i}", [128, 512], F32))
                   for i in range(2)]

            with ExitStack() as bstack:
                def bsb(*a):
                    return bstack.enter_context(nc.sbuf_tensor(*a))
                w1qT_sb = bsb("w1qT_sb", [128, 8 * FF], F16)
                y_fm = [bsb(f"y_fm{i}", [128, 8 * 512], F16) for i in range(2)]
                xq_sb = [bsb(f"xq{i}", [128, 8 * 512], F16) for i in range(2)]
                yq_u8 = bsb("yq_u8", [128, 512], U8)
                h_tile = [bsb(f"h_tile{i}", [128, 512], F16) for i in range(4)]
                hmc_sb = bsb("hmc_sb", [128, 32], F32)
                hmt_sb = bsb("hmt_sb", [128, 1], F32)
                b1i_sb = bsb("b1i_sb", [128, 32], I32)
                b1f_sb = bsb("b1f_sb", [128, 32], F32)
                rcs1_sb = bsb("rcs1_sb", [128, 32], F32)
                ps_b = [bstack.enter_context(
                    nc.psum_tensor(f"psHB{i}", [128, 512], F32)) for i in range(4)]

                ar1blk = nc.Block()
                ar1blk.__enter__()

                @ar1blk.sync
                def _(sync):
                    if SKIP_AR1:
                        return
                    ymfin = ymax_sb[:, NST % 2:NST % 2 + 1]
                    sync.dma_start(out=red1_h[:, :], in_=ymfin).then_inc(r1S, 16)
                    sync.wait_ge(r1S, 16)
                    sync.dma_start(out=redt_sb[:, :],
                                   in_=red1_h[:, :]).then_inc(r1S, 16)
                    sync.wait_ge(r1D, 1)
                    sync.dma_start(out=cc1i_h[:, :],
                                   in_=cc_sb[:, :]).then_inc(r1S, 16)
                    sync.wait_ge(r1Cc, 1)
                    sync.dma_start(out=g1_sb[:, :],
                                   in_=cc1o_h[0:1, 0:1]).then_inc(r1C, 16)

                @ar1blk.vector
                def _(vector):
                    if SKIP_AR1:
                        return
                    vector.wait_ge(r1S, 32)
                    vector.reduce_max(g1_sb[:, :], redt_sb[:, :], AX)
                    vbar()
                    vbar(vector.memset(cc_sb[:, :], 0.0))
                    vector.tensor_copy(cc_sb[0:1, 0:1], g1_sb[:, :]).then_inc(r1D, 1)
                    vector.wait_ge(r1C, 16)
                    vector.tensor_scalar(s1v_sb[:, :], g1_sb[:, :], EPS_Q,
                                         1.0 / 255.0, ALU.max,
                                         ALU.mult).then_inc(r1D, 1)
                    vector.wait_ge(r1P, 1)
                    vector.tensor_copy(s1b_sb[:, :],
                                       psR[0][:, 0:1]).then_inc(r1D, 1)
                    nc.vector.wait_ge(r1D, 3)
                    vector.reciprocal(r1b_sb[:, :], s1b_sb[:, :])
                    vector.tensor_scalar(scale1_sb[:, :], sw1_sb[:, :],
                                         s1b_sb[:, :], 1.0 / 127.0, ALU.mult,
                                         ALU.mult).then_inc(r1D, 1)
                    nc.vector.wait_ge(r1D, 4)
                    vbar(vector.reciprocal(rcs1_sb[:, :], scale1_sb[:, :]))
                    vbar(vector.tensor_mul(b1f_sb[:, :], b1_sb[:, :], rcs1_sb[:, :]))
                    vbar(vector.tensor_copy(b1i_sb[:, :], b1f_sb[:, :]))
                    vbar(vector.tensor_copy(b1f_sb[:, :], b1i_sb[:, :]))
                    vbar(vector.tensor_mul(b1q_sb[:, :], b1f_sb[:, :],
                                           scale1_sb[:, :]))

                @ar1blk.gpsimd
                def _(gpsimd):
                    if SKIP_AR1:
                        return
                    gpsimd.wait_ge(r1S, 48)
                    gpsimd.collective_compute(
                        "AllReduce", ALU.max,
                        replica_groups=[list(range(NCORES))],
                        ins=[cc1i_h.ap().opt()],
                        outs=[cc1o_h.ap().opt()]).then_inc(r1Cc, 1)

                @ar1blk.tensor
                def _(tensor):
                    if SKIP_AR1:
                        return
                    tensor.wait_ge(r1D, 2)
                    tensor.matmul(psR[0][:, 0:1], ones_sb[:, :], s1v_sb[:, :],
                                  start=True, stop=True).then_inc(r1P, 1)

                @ar1blk.scalar
                def _(scalar):
                    if SKIP_AR1:
                        return
                    for dc in range(8):
                        scalar.dma_start_transpose(
                            w1qT_sb[:, dc * FF:(dc + 1) * FF],
                            w1q_h[0:FF, dc * 128:(dc + 1) * 128]).then_inc(r1T, 16)
                    for fc in range(32):
                        scalar.dma_start_transpose(
                            w2qT_sb[:, fc * D:(fc + 1) * D],
                            w2q_h[0:D, fc * 128:(fc + 1) * 128]).then_inc(r1T, 16)
                    scalar.wait_ge(r1T, 16 * 40)

                ar1blk.__exit__(None, None, None)
                if BUILD_PHASES < 4:
                    nc.finalize()
                    return nc
                bblk = bstack.enter_context(nc.Block())

                @bblk.sync
                def _(sync):
                    for g in range(16):
                        if g >= 2:
                            sync.wait_ge(bYQ, g - 1)
                        for c in range(8):
                            sync.dma_start_transpose(
                                y_fm[g % 2][:, c * 512:(c + 1) * 512],
                                y16_h[g * 512:(g + 1) * 512,
                                      c * 128:(c + 1) * 128]).then_inc(bYL, 16)

                @bblk.vector
                def _(vector):
                    def hmax_group(g):
                        for fc in range(32):
                            u = 32 * g + fc
                            nc.vector.wait_ge(bHE, u + 1)
                            nc.vector.reduce_max(hmc_sb[:, fc:fc + 1],
                                                 h_tile[fc % 4][:, :], AX)
                            nc.vector.engine_nop().then_inc(bHR, 1)
                        nc.vector.wait_ge(bHR, 32 * (g + 1))
                        nc.vector.reduce_max(hmt_sb[:, :], hmc_sb[:, :], AX)
                        vbar()
                        nc.vector.tensor_max(
                            hmax_sb[:, (g + 1) % 2:(g + 1) % 2 + 1],
                            hmax_sb[:, g % 2:g % 2 + 1], hmt_sb[:, :])

                    for g in range(16):
                        vector.wait_ge(bYL, 128 * (g + 1))
                        if g >= 2 and B_STAGES >= 2:
                            vector.wait_ge(bHM, 32 * (g - 1))
                        for c in range(8):
                            sl_ = slice(c * 512, (c + 1) * 512)
                            vbar(vector.tensor_scalar(yq_u8[:, :],
                                                      y_fm[g % 2][:, sl_],
                                                      r1b_sb[:, :], None,
                                                      ALU.mult))
                            vector.tensor_copy(xq_sb[g % 2][:, sl_], yq_u8[:, :])
                        nc.vector.engine_nop().then_inc(bYQ, 1)
                        nc.vector.engine_nop().then_inc(bXQ, 1)
                        if g >= 1 and B_STAGES >= 3 and B_HMAX:
                            hmax_group(g - 1)
                    if B_STAGES >= 3 and B_HMAX:
                        hmax_group(15)

                @bblk.tensor
                def _(tensor):
                    if B_STAGES < 2:
                        return
                    for g in range(16):
                        tensor.wait_ge(bXQ, g + 1)
                        for fc in range(32):
                            u = 32 * g + fc
                            if u >= 4 and B_STAGES >= 3:
                                tensor.wait_ge(bHE, u - 3)
                            for dc in range(8):
                                ins = tensor.matmul(
                                    ps_b[fc % 4][:, :],
                                    w1qT_sb[:, dc * FF + fc * 128:
                                            dc * FF + (fc + 1) * 128],
                                    xq_sb[g % 2][:, dc * 512:(dc + 1) * 512],
                                    start=(dc == 0), stop=(dc == 7))
                            ins.then_inc(bHM, 1)

                @bblk.scalar
                def _(scalar):
                    def evac_group(g):
                        for fc in range(32):
                            u = 32 * g + fc
                            scalar.wait_ge(bHM, u + 1)
                            if u >= 4:
                                if B_STAGES >= 4:
                                    scalar.wait_ge([bHS0, bHS1, bHS2, bHS3][u % 4],
                                                   16 * (u // 4))
                                if B_HMAX:
                                    scalar.wait_ge(bHR, u - 3)
                            if B_SIMPLE_EVAC:
                                scalar.copy(h_tile[fc % 4][:, :],
                                            ps_b[fc % 4][:, :]).then_inc(bHE, 1)
                            else:
                                scalar.activation(h_tile[fc % 4][:, :],
                                                  ps_b[fc % 4][:, :], ACT.Relu,
                                                  bias=b1q_sb[:, fc:fc + 1],
                                                  scale=scale1_sb[:, fc:fc + 1]
                                                  ).then_inc(bHE, 1)
                            if B_STAGES >= 4:
                                scalar.wait_ge(bHE, u + 1)
                                scalar.dma_start(out=h16_h[fc, g],
                                                 in_=h_tile[fc % 4][:, :]
                                                 ).then_inc([bHS0, bHS1, bHS2, bHS3][u % 4], 16)

                    for g in range(16):
                        if B_STAGES >= 3 and g >= 1:
                            evac_group(g - 1)
                    if B_STAGES >= 3:
                        evac_group(15)
                    if B_STAGES >= 4:
                        for bb in range(4):
                            scalar.wait_ge([bHS0, bHS1, bHS2, bHS3][bb], 16 * 8 * 16)

            if BUILD_PHASES < 5:
                nc.finalize()
                return nc
            with ExitStack() as cstack:
                def c2sb(*a):
                    return cstack.enter_context(nc.sbuf_tensor(*a))
                SCALE2B = c2sb("SCALE2B", [128, D], F32)
                B2QB = c2sb("B2QB", [128, D], F32)

                arstack = cstack.enter_context(ExitStack())

                def arsb(*a):
                    return arstack.enter_context(nc.sbuf_tensor(*a))
                b2row_sb = arsb("b2row_sb", [1, D], F32)
                s2wrow_sb = arsb("s2wrow_sb", [1, D], F32)
                scale2row_sb = arsb("scale2row_sb", [1, D], F32)
                b2qrow_sb = arsb("b2qrow_sb", [1, D], F32)
                rc2row_sb = arsb("rc2row_sb", [1, D], F32)
                tmp1r_sb = arsb("tmp1r_sb", [1, D], F32)
                tmpi1r_sb = arsb("tmpi1r_sb", [1, D], I32)

                ar2blk = nc.Block()
                ar2blk.__enter__()

                @ar2blk.sync
                def _(sync):
                    hmfin = hmax_sb[:, 16 % 2:16 % 2 + 1]
                    sync.dma_start(out=red2_h[:, :], in_=hmfin).then_inc(r2S, 16)
                    sync.wait_ge(r2S, 16)
                    sync.dma_start(out=redt_sb[:, :],
                                   in_=red2_h[:, :]).then_inc(r2S, 16)
                    sync.dma_start(out=s2wrow_sb[:, :],
                                   in_=sw2d_h[:, :]).then_inc(r2S, 16)
                    sync.dma_start(out=b2row_sb[:, :],
                                   in_=b2_h[:, :]).then_inc(r2S, 16)
                    sync.wait_ge(r2D, 1)
                    sync.dma_start(out=cc2i_h[:, :],
                                   in_=cc_sb[:, :]).then_inc(r2S, 16)
                    sync.wait_ge(r2Cc, 1)
                    sync.dma_start(out=g2_sb[:, :],
                                   in_=cc2o_h[0:1, 0:1]).then_inc(r2C, 16)

                @ar2blk.vector
                def _(vector):
                    vector.wait_ge(r2S, 32)
                    vector.reduce_max(g2_sb[:, :], redt_sb[:, :], AX)
                    vbar()
                    vbar(vector.memset(cc_sb[:, :], 0.0))
                    vector.tensor_copy(cc_sb[0:1, 0:1], g2_sb[:, :]).then_inc(r2D, 1)
                    vector.wait_ge(r2C, 16)
                    vector.tensor_scalar(s2v_sb[:, :], g2_sb[:, :], EPS_Q,
                                         1.0 / 255.0, ALU.max,
                                         ALU.mult).then_inc(r2D, 1)
                    nc.vector.wait_ge(r2D, 2)
                    vector.wait_ge(r2S, 64)
                    vbar(vector.tensor_scalar(scale2row_sb[:, :], s2wrow_sb[:, :],
                                              s2v_sb[:, :], 1.0 / 127.0, ALU.mult,
                                              ALU.mult))
                    vbar(vector.reciprocal(rc2row_sb[:, :], scale2row_sb[:, :]))
                    vbar(vector.tensor_mul(tmp1r_sb[:, :], b2row_sb[:, :],
                                           rc2row_sb[:, :]))
                    vbar(vector.tensor_copy(tmpi1r_sb[:, :], tmp1r_sb[:, :]))
                    vbar(vector.tensor_copy(tmp1r_sb[:, :], tmpi1r_sb[:, :]))
                    vector.tensor_mul(b2qrow_sb[:, :], tmp1r_sb[:, :],
                                      scale2row_sb[:, :]).then_inc(r2D, 1)
                    vector.wait_ge(r2P, 2)
                    vector.tensor_copy(SCALE2B[:, 0:512], psR[0][:, :])
                    vector.tensor_copy(SCALE2B[:, 512:1024],
                                       psR[1][:, :]).then_inc(r2D, 1)
                    vector.wait_ge(r2P, 4)
                    vector.tensor_copy(B2QB[:, 0:512], psR[0][:, :])
                    vector.tensor_copy(B2QB[:, 512:1024],
                                       psR[1][:, :]).then_inc(r2D, 1)
                    vector.wait_ge(r2P, 5)
                    vector.tensor_copy(s2b_sb[:, :],
                                       psR[0][:, 0:1]).then_inc(r2D, 1)
                    nc.vector.wait_ge(r2D, 6)
                    vector.reciprocal(r2b_sb[:, :], s2b_sb[:, :])

                @ar2blk.gpsimd
                def _(gpsimd):
                    gpsimd.wait_ge(r2S, 80)
                    gpsimd.collective_compute(
                        "AllReduce", ALU.max,
                        replica_groups=[list(range(NCORES))],
                        ins=[cc2i_h.ap().opt()],
                        outs=[cc2o_h.ap().opt()]).then_inc(r2Cc, 1)

                @ar2blk.tensor
                def _(tensor):
                    tensor.wait_ge(r2D, 3)
                    tensor.matmul(psR[0][:, :], ones_sb[:, :],
                                  scale2row_sb[0:1, 0:512], start=True,
                                  stop=True).then_inc(r2P, 1)
                    tensor.matmul(psR[1][:, :], ones_sb[:, :],
                                  scale2row_sb[0:1, 512:1024], start=True,
                                  stop=True).then_inc(r2P, 1)
                    tensor.wait_ge(r2D, 4)
                    tensor.matmul(psR[0][:, :], ones_sb[:, :],
                                  b2qrow_sb[0:1, 0:512], start=True,
                                  stop=True).then_inc(r2P, 1)
                    tensor.matmul(psR[1][:, :], ones_sb[:, :],
                                  b2qrow_sb[0:1, 512:1024], start=True,
                                  stop=True).then_inc(r2P, 1)
                    tensor.wait_ge(r2D, 5)
                    tensor.matmul(psR[0][:, 0:1], ones_sb[:, :], s2v_sb[:, :],
                                  start=True, stop=True).then_inc(r2P, 1)

                ar2blk.__exit__(None, None, None)
                arstack.close()
                prstack.close()
                if BUILD_PHASES < 6:
                    nc.finalize()
                    return nc
                ps_c = [cstack.enter_context(
                    nc.psum_tensor(f"psOC{i}", [128, 512], F32)) for i in range(8)]
                h_ld = [c2sb(f"h_ld{i}", [128, 32 * 512], F16) for i in range(2)]
                o1_ld = [c2sb(f"o1_ld{i}", [128, 4 * D], F16) for i in range(2)]
                fout_sb = c2sb("fout_sb", [128, 4 * D], F32)
                hu8_sb = [c2sb(f"hu8{i}", [128, 512], U8) for i in range(2)]
                eb1_sb = c2sb("eb1_sb", [128, 512], F32)
                eb2_sb = c2sb("eb2_sb", [128, 512], F32)

                cblk = cstack.enter_context(nc.Block())

                @cblk.sync
                def _(sync):
                    for g in range(16):
                        if g >= 2:
                            sync.wait_ge(cMM, 4 * (g - 1))
                            sync.wait_ge(cFO, g - 1)
                        for fc in range(32):
                            sync.dma_start(
                                out=h_ld[g % 2][:, fc * 512:(fc + 1) * 512],
                                in_=h16_h[fc, g]).then_inc(cHL, 16)
                        for tt in range(4):
                            r0 = g * 512 + tt * 128
                            sync.dma_start(out=o1_ld[g % 2][:, tt * D:(tt + 1) * D],
                                           in_=o16_h[r0:r0 + 128, :]
                                           ).then_inc(cOL, 16)
                        if g >= 1:
                            sync.wait_ge(cFO, g)
                            for tt in range(4):
                                r0 = (g - 1) * 512 + tt * 128
                                sync.dma_start(out=out_h[r0:r0 + 128, :],
                                               in_=fout_sb[:, tt * D:(tt + 1) * D]
                                               ).then_inc(cOS, 16)
                    sync.wait_ge(cFO, 16)
                    for tt in range(4):
                        r0 = 15 * 512 + tt * 128
                        sync.dma_start(out=out_h[r0:r0 + 128, :],
                                       in_=fout_sb[:, tt * D:(tt + 1) * D]
                                       ).then_inc(cOS, 16)
                    sync.wait_ge(cOS, 16 * 64)
                    if DEBUG_TAPS:
                        sync.dma_start(out=dbg_h[:, :], in_=dbg_sb[:, :]).then_inc(cOS, 16)
                        sync.wait_ge(cOS, 16 * 65)

                @cblk.vector
                def _(vector):
                    def epilogue_group(g):
                        if g >= 1:
                            nc.vector.wait_ge(cOS, 64 * g)
                        nc.vector.wait_ge(cOL, 64 * min(g + 2, 16))
                        for tt in range(4):
                            nc.vector.wait_ge(cMM, 4 * g + tt + 1)
                            for half in range(2):
                                bank = ps_c[2 * tt + half]
                                sl_ = slice(half * 512, half * 512 + 512)
                                osl = slice(tt * D + half * 512,
                                            tt * D + half * 512 + 512)
                                vbar(nc.vector.tensor_mul(eb1_sb[:, :],
                                                          bank[:, :],
                                                          SCALE2B[:, sl_]))
                                vbar(nc.vector.tensor_add(eb2_sb[:, :],
                                                          eb1_sb[:, :],
                                                          B2QB[:, sl_]))
                                nc.vector.tensor_add(fout_sb[:, osl],
                                                     eb2_sb[:, :],
                                                     o1_ld[g % 2][:, osl])
                            nc.vector.engine_nop().then_inc(cOE, 1)
                        nc.vector.engine_nop().then_inc(cFO, 1)

                    for g in range(16):
                        vector.wait_ge(cHL, 512 * (g + 1))
                        for fc in range(32):
                            u = 32 * g + fc
                            if u >= 2:
                                vector.wait_ge(cQ2, u - 1)
                            vector.tensor_scalar(
                                hu8_sb[fc % 2][:, :],
                                h_ld[g % 2][:, fc * 512:(fc + 1) * 512],
                                r2b_sb[:, :], None, ALU.mult).then_inc(cQ1, 1)
                        if g >= 1:
                            epilogue_group(g - 1)
                    epilogue_group(15)

                @cblk.scalar
                def _(scalar):
                    for g in range(16):
                        for fc in range(32):
                            u = 32 * g + fc
                            scalar.wait_ge(cQ1, u + 1)
                            scalar.copy(h_ld[g % 2][:, fc * 512:(fc + 1) * 512],
                                        hu8_sb[fc % 2][:, :]).then_inc(cQ2, 1)

                @cblk.tensor
                def _(tensor):
                    for g in range(16):
                        for tt in range(4):
                            tu = 4 * g + tt
                            if tu >= 4:
                                tensor.wait_ge(cOE, tu - 3)
                            b0 = ps_c[2 * tt]
                            b1 = ps_c[2 * tt + 1]
                            for fc in range(32):
                                if tt == 0:
                                    tensor.wait_ge(cQ2, 32 * g + fc + 1)
                                lhs = h_ld[g % 2][:, fc * 512 + tt * 128:
                                                  fc * 512 + (tt + 1) * 128]
                                tensor.matmul(b0[:, :], lhs,
                                              w2qT_sb[:, fc * D: fc * D + 512],
                                              start=(fc == 0), stop=(fc == 31))
                                ins = tensor.matmul(b1[:, :], lhs,
                                                    w2qT_sb[:, fc * D + 512:
                                                            fc * D + 1024],
                                                    start=(fc == 0),
                                                    stop=(fc == 31))
                            ins.then_inc(cMM, 1)

    nc.finalize()
    return nc


def kernel(L0_attn_X, L0_attn_Wq, L0_attn_bq, L0_attn_Wk, L0_attn_bk,
           L0_attn_Wv, L0_attn_bv, L0_ffn_w1, L0_ffn_b1, L0_ffn_w2, L0_ffn_b2):
    X = np.ascontiguousarray(np.asarray(L0_attn_X, dtype=np.float32))
    wq = np.ascontiguousarray(np.asarray(L0_attn_Wq, dtype=np.float32))
    wk = np.ascontiguousarray(np.asarray(L0_attn_Wk, dtype=np.float32))
    wv = np.ascontiguousarray(np.asarray(L0_attn_Wv, dtype=np.float32))
    bq = np.asarray(L0_attn_bq, dtype=np.float32)
    bk = np.asarray(L0_attn_bk, dtype=np.float32)
    bv = np.asarray(L0_attn_bv, dtype=np.float32)
    w1 = np.ascontiguousarray(np.asarray(L0_ffn_w1, dtype=np.float32))
    b1 = np.asarray(L0_ffn_b1, dtype=np.float32)
    w2 = np.ascontiguousarray(np.asarray(L0_ffn_w2, dtype=np.float32))
    b2 = np.asarray(L0_ffn_b2, dtype=np.float32)

    bqr = np.ascontiguousarray(bq.reshape(8, 128).T)
    bkr = np.ascontiguousarray(bk.reshape(8, 128).T)
    bvr = np.ascontiguousarray(bv.reshape(1, D))
    b1r = np.ascontiguousarray(b1.reshape(32, 128).T)
    b2r = np.ascontiguousarray(b2.reshape(1, D))
    ii = np.arange(128)
    mask = np.where(ii[None, :] <= ii[:, None], 0.0, NEG).astype(np.float16)
    qts = np.arange(32)
    cnt = (S - 1 - (qts[None, :] * 128 + ii[:, None])).astype(np.float32)
    onesf = np.ones((1, 128), np.float32)

    nc = build_nc()

    in_maps = []
    for c in range(NCORES):
        xs = np.ascontiguousarray(X[2 * c:2 * c + 2].reshape(T, D))
        in_maps.append(dict(x=xs, wq=wq, wk=wk, wv=wv, bqr=bqr, bkr=bkr,
                            bvr=bvr, w1=w1, b1r=b1r, w2=w2, b2r=b2r,
                            mask=mask, cnt=cnt, onesf=onesf))
    res = run_bass_kernel_spmd(nc, in_maps, core_ids=list(range(NCORES)))
    out = np.concatenate([np.asarray(res.results[c]["out"], np.float32)
                          for c in range(NCORES)], axis=0)
    return out.reshape(16, S, D)


# version 12
# speedup vs baseline: 1.1370x; 1.0318x over previous
import numpy as np
from contextlib import ExitStack

import concourse.bass as bass
import concourse.mybir as mybir
from concourse.bass_utils import run_bass_kernel_spmd

F32 = mybir.dt.float32
F16 = mybir.dt.float16
U8 = mybir.dt.uint8
I8 = mybir.dt.int8
I32 = mybir.dt.int32
ALU = mybir.AluOpType
ACT = mybir.ActivationFunctionType
AX = mybir.AxisListType.X

NCORES = 8
S = 4096
D = 1024
FF = 4096
T = 2 * S
NST = T // 512
SPB = S // 512
EPS_LN = 1e-5
EPS_Q = 1e-8
NEG = -60000.0


def _sgroups():
    out = []
    sgidx = 0
    for st in range(NST):
        sl = st % SPB
        rows = []
        for tt in range(4):
            qt = sl * 4 + tt
            gf = (qt * 128 + 511) // 512
            for g in range(gf):
                ng = min(512, qt * 128 - g * 512)
                rows.append((tt, g, ng, False, sgidx % 8, sgidx))
                sgidx += 1
            rows.append((tt, gf, 128, True, sgidx % 8, sgidx))
            sgidx += 1
        out.append(rows)
    return out


SGROUPS = _sgroups()
LASTUSE = [{b: max(r[5] for r in rows if r[4] == b)
            for b in set(r[4] for r in rows)} for rows in SGROUPS]
BUILD_PHASES = 6
B_STAGES = 4
B_HMAX = True
B_SIMPLE_EVAC = False
SKIP_A = False
SKIP_AR1 = False
SKIP_P = False
DEBUG_TAPS = False


def build_nc():
    nc = bass.Bass("TRN2", target_bir_lowering=False,
                   detect_race_conditions=False)

    x_h = nc.dram_tensor("x", [T, D], F32, kind="ExternalInput")
    wq_h = nc.dram_tensor("wq", [D, D], F32, kind="ExternalInput")
    wk_h = nc.dram_tensor("wk", [D, D], F32, kind="ExternalInput")
    wv_h = nc.dram_tensor("wv", [D, D], F32, kind="ExternalInput")
    bq_h = nc.dram_tensor("bqr", [128, 8], F32, kind="ExternalInput")
    bk_h = nc.dram_tensor("bkr", [128, 8], F32, kind="ExternalInput")
    bv_h = nc.dram_tensor("bvr", [1, D], F32, kind="ExternalInput")
    w1_h = nc.dram_tensor("w1", [FF, D], F32, kind="ExternalInput")
    b1_h = nc.dram_tensor("b1r", [128, 32], F32, kind="ExternalInput")
    w2_h = nc.dram_tensor("w2", [D, FF], F32, kind="ExternalInput")
    b2_h = nc.dram_tensor("b2r", [1, D], F32, kind="ExternalInput")
    mask_h = nc.dram_tensor("mask", [128, 128], F16, kind="ExternalInput")
    cnt_h = nc.dram_tensor("cnt", [128, 32], F32, kind="ExternalInput")
    ones_h = nc.dram_tensor("onesf", [1, 128], F32, kind="ExternalInput")
    out_h = nc.dram_tensor("out", [T, D], F32, kind="ExternalOutput")

    KI = "ExternalOutput" if DEBUG_TAPS else "Internal"
    dbg_h = nc.dram_tensor("dbg", [128, 64], F32, kind=KI)
    y16_h = nc.dram_tensor("y16", [T, D], F16, kind=KI)
    o16_h = nc.dram_tensor("o16", [T, D], F16, kind=KI)
    w1q_h = nc.dram_tensor("w1qs", [FF, D], F16, kind=KI)
    w2q_h = nc.dram_tensor("w2qs", [D, FF], F16, kind=KI)
    h16_h = nc.dram_tensor("h16", [32, 16, 128, 512], F16, kind=KI)
    red1_h = nc.dram_tensor("red1", [1, 128], F32, kind="Internal")
    red2_h = nc.dram_tensor("red2", [1, 128], F32, kind="Internal")
    sw2d_h = nc.dram_tensor("sw2d", [1, D], F32, kind=KI)
    cc1i_h = nc.dram_tensor("cc1i", [128, 4], F32, kind="Internal")
    cc1o_h = nc.dram_tensor("cc1o", [128, 4], F32, kind="Internal")
    cc2i_h = nc.dram_tensor("cc2i", [128, 4], F32, kind="Internal")
    cc2o_h = nc.dram_tensor("cc2o", [128, 4], F32, kind="Internal")

    with ExitStack() as outer:
        def osb(*a):
            return outer.enter_context(nc.sbuf_tensor(*a))

        def sem(name):
            return outer.enter_context(nc.semaphore(name))

        b1_sb = osb("b1_sb", [128, 32], F32)
        sw1_sb = osb("sw1_sb", [128, 32], F32)
        scale1_sb = osb("scale1_sb", [128, 32], F32)
        b1q_sb = osb("b1q_sb", [128, 32], F32)
        sw2_sb = osb("sw2_sb", [128, 8], F32)
        ones_sb = osb("ones_sb", [1, 128], F32)
        ymax_sb = osb("ymax_sb", [128, 2], F32)
        hmax_sb = osb("hmax_sb", [128, 2], F32)
        s1b_sb = osb("s1b_sb", [128, 1], F32)
        r1b_sb = osb("r1b_sb", [128, 1], F32)
        s2b_sb = osb("s2b_sb", [128, 1], F32)
        r2b_sb = osb("r2b_sb", [128, 1], F32)
        g1_sb = osb("g1_sb", [1, 1], F32)
        g2_sb = osb("g2_sb", [1, 1], F32)
        s1v_sb = osb("s1v_sb", [1, 1], F32)
        s2v_sb = osb("s2v_sb", [1, 1], F32)
        redt_sb = osb("redt_sb", [1, 128], F32)
        dbg_sb = osb("dbg_sb", [128, 64], F32)
        cc_sb = osb("cc_sb", [128, 4], F32)

        pLD = sem("pLD"); pW = sem("pW"); pWF = sem("pWF"); pQS = sem("pQS")
        pST = sem("pST"); pSLF = sem("pSLF"); pMM = sem("pMM")
        pLDg = sem("pLDg"); r1T = sem("r1T")
        pW0 = sem("pW0"); pW1 = sem("pW1")
        bHS0 = sem("bHS0"); bHS1 = sem("bHS1"); bHS2 = sem("bHS2"); bHS3 = sem("bHS3")
        aXL0 = sem("aXL0"); aXL1 = sem("aXL1"); aX20 = sem("aX20")
        aX21 = sem("aX21"); aLN = sem("aLN"); aTR0 = sem("aTR0")
        aTR1 = sem("aTR1"); aPJ = sem("aPJ"); aQK = sem("aQK")
        aVE = sem("aVE"); aSG = sem("aSG"); aMN = sem("aMN")
        aES = sem("aES"); aEM = sem("aEM"); aO1 = sem("aO1")
        aOS0 = sem("aOS0"); aOS1 = sem("aOS1"); aY = sem("aY")
        aYS0 = sem("aYS0"); aYS1 = sem("aYS1"); aSQ1 = sem("aSQ1")
        aSQ2 = sem("aSQ2"); aRSTD1 = sem("aRSTD1"); aRSTD2 = sem("aRSTD2")
        aVAR1 = sem("aVAR1"); aVAR2 = sem("aVAR2"); aXLF = sem("aXLF")
        aX2F = sem("aX2F"); aSLF = sem("aSLF")
        r1S = sem("r1S"); r1C = sem("r1C"); r1D = sem("r1D"); r1P = sem("r1P")
        r1Cc = sem("r1Cc"); r2Cc = sem("r2Cc")
        bYL = sem("bYL"); bYQ = sem("bYQ"); bXQ = sem("bXQ"); bXF = sem("bXF")
        bHM = sem("bHM"); bHE = sem("bHE"); bHS = sem("bHS"); bHR = sem("bHR")
        r2S = sem("r2S"); r2C = sem("r2C"); r2D = sem("r2D"); r2P = sem("r2P")
        cHL = sem("cHL"); cHF = sem("cHF"); cOL = sem("cOL"); cQ1 = sem("cQ1")
        cQ2 = sem("cQ2"); cMM = sem("cMM"); cOE = sem("cOE"); cFO = sem("cFO")
        cOS = sem("cOS")
        vBAR = sem("vBAR"); sBAR = sem("sBAR")

        _vb = [0]
        _sb_ = [0]

        def vbar(ins=None):
            if ins is None:
                ins = nc.vector.engine_nop()
            _vb[0] += 1
            ins.then_inc(vBAR, 1)
            nc.vector.wait_ge(vBAR, _vb[0])

        def sbar(ins):
            _sb_[0] += 1
            ins.then_inc(sBAR, 1)
            nc.scalar.wait_ge(sBAR, _sb_[0])

        pa = outer.enter_context(ExitStack())

        def pasb(*a):
            return pa.enter_context(nc.sbuf_tensor(*a))
        wq_sb = pasb("wq_sb", [128, 8 * D], F16)
        wk_sb = pasb("wk_sb", [128, 8 * D], F16)
        wv_sb = pasb("wv_sb", [128, 8 * D], F16)
        bq_sb = pasb("bq_sb", [128, 8], F32)
        bk_sb = pasb("bk_sb", [128, 8], F32)
        mask_sb = pasb("mask_sb", [128, 128], F16)
        cnt_sb = pasb("cnt_sb", [128, 32], F32)
        eps_sb = pasb("eps_sb", [128, 1], F32)
        bv_bc = pasb("bv_bc", [128, D], F16)

        with ExitStack() as pstack:
            def psb(*a):
                return pstack.enter_context(nc.sbuf_tensor(*a))
            wbuf = [psb(f"wbuf{i}", [128, D], F32) for i in range(2)]
            w2buf = [psb(f"w2buf{i}", [128, FF], F32) for i in range(2)]
            i8b = psb("i8b", [128, D], I8)
            i8b2 = psb("i8b2", [128, FF], I8)
            w1qf = [psb(f"w1qf{i}", [128, D], F16) for i in range(2)]
            w2qf = [psb(f"w2qf{i}", [128, FF], F16) for i in range(2)]
            rw_sb = psb("rw_sb", [128, 1], F32)
            rw_in = psb("rw_in", [128, 1], F32)
            bvrow_sb = psb("bvrow_sb", [1, D], F32)
            psB = [pstack.enter_context(nc.psum_tensor(f"psB{i}", [128, 512], F32))
                   for i in range(2)]

            pblk = pstack.enter_context(nc.Block())

            @pblk.sync
            def _(sync):
                if SKIP_P:
                    return
                sync.dma_start(out=bq_sb[:, :], in_=bq_h[:, :]).then_inc(pLD, 16)
                sync.dma_start(out=bk_sb[:, :], in_=bk_h[:, :]).then_inc(pLD, 16)
                sync.dma_start(out=b1_sb[:, :], in_=b1_h[:, :]).then_inc(pLD, 16)
                sync.dma_start(out=mask_sb[:, :], in_=mask_h[:, :]).then_inc(pLD, 16)
                sync.dma_start(out=cnt_sb[:, :], in_=cnt_h[:, :]).then_inc(pLD, 16)
                sync.dma_start(out=ones_sb[:, :], in_=ones_h[:, :]).then_inc(pLD, 16)
                sync.dma_start(out=bvrow_sb[:, :], in_=bv_h[:, :]).then_inc(pLD, 16)
                for it in range(42):
                    if it < 40:
                        if it >= 2:
                            sync.wait_ge(pWF, it - 1)
                        if it < 32:
                            sync.dma_start(out=wbuf[it % 2][:, :],
                                           in_=w1_h[it * 128:(it + 1) * 128, :]
                                           ).then_inc(pW1 if it % 2 else pW0, 16)
                        else:
                            dc = it - 32
                            sync.dma_start(out=w2buf[dc % 2][:, :],
                                           in_=w2_h[dc * 128:(dc + 1) * 128, :]
                                           ).then_inc(pW1 if it % 2 else pW0, 16)
                    j = it - 2
                    if j >= 0:
                        sync.wait_ge(pQS, j + 1)
                        if j < 32:
                            sync.dma_start(out=w1q_h[j * 128:(j + 1) * 128, :],
                                           in_=w1qf[j % 2][:, :]).then_inc(pST, 16)
                        else:
                            dc = j - 32
                            sync.dma_start(out=w2q_h[dc * 128:(dc + 1) * 128, :],
                                           in_=w2qf[dc % 2][:, :]).then_inc(pST, 16)
                            sync.dma_start(out=sw2d_h[0:1, dc * 128:(dc + 1) * 128],
                                           in_=sw2_sb[:, dc:dc + 1]).then_inc(pST, 16)

            @pblk.gpsimd
            def _(gpsimd):
                if SKIP_P:
                    return
                for wh, wsb in ((wq_h, wq_sb), (wk_h, wk_sb), (wv_h, wv_sb)):
                    for c in range(8):
                        gpsimd.dma_start(
                            out=wsb[:, c * D:(c + 1) * D],
                            in_=wh[c * 128:(c + 1) * 128, :]).then_inc(pLDg, 16)
                gpsimd.wait_ge(pLDg, 16 * 24)

            @pblk.tensor
            def _(tensor):
                if SKIP_P:
                    return
                tensor.wait_ge(pLD, 16 * 7)
                tensor.matmul(psB[0][:, :], ones_sb[:, :], bvrow_sb[0:1, 0:512],
                              start=True, stop=True).then_inc(pMM, 1)
                tensor.matmul(psB[1][:, :], ones_sb[:, :], bvrow_sb[0:1, 512:1024],
                              start=True, stop=True).then_inc(pMM, 1)

            @pblk.vector
            def _(vector):
                if SKIP_P:
                    return
                vector.memset(eps_sb[:, :], EPS_LN)
                vector.memset(ymax_sb[:, :], 0.0)
                vector.memset(hmax_sb[:, :], 0.0)
                vector.wait_ge(pMM, 2)
                vector.tensor_copy(bv_bc[:, 0:512], psB[0][:, :])
                vector.tensor_copy(bv_bc[:, 512:1024], psB[1][:, :])
                selfc = 0
                for i in range(40):
                    is1 = i < 32
                    buf = wbuf[i % 2] if is1 else w2buf[(i - 32) % 2]
                    i8t = i8b if is1 else i8b2
                    qf = w1qf[i % 2] if is1 else w2qf[(i - 32) % 2]
                    scol = (sw1_sb[:, i:i + 1] if is1
                            else sw2_sb[:, i - 32:i - 31])
                    vector.wait_ge(pW1 if i % 2 else pW0, 16 * (i // 2 + 1))
                    vector.reduce_max(scol, buf[:, :], AX, apply_absolute_value=True)
                    vbar()
                    selfc += 1
                    vector.reciprocal(rw_sb[:, :], scol).then_inc(pSLF, 1)
                    nc.vector.wait_ge(pSLF, selfc)
                    vector.tensor_scalar(i8t[:, :], buf[:, :], rw_sb[:, :],
                                         127.0, ALU.mult,
                                         ALU.mult).then_inc(pWF, 1)
                    nc.vector.wait_ge(pWF, i + 1)
                    if DEBUG_TAPS:
                        vector.tensor_copy(dbg_sb[:, i:i + 1], rw_sb[:, :])
                        if i < 8:
                            vbar(vector.tensor_copy(dbg_sb[:, 40 + i:41 + i],
                                                    scol))
                            vbar(vector.tensor_copy(dbg_sb[:, 48 + i:49 + i],
                                                    buf[:, 0:1]))
                    if i >= 2:
                        j = i - 1
                        pst_need = 16 * (j + 1) if j < 32 else 512 + 32 * (j - 31)
                        vector.wait_ge(pST, pst_need)
                    vector.tensor_copy(qf[:, :], i8t[:, :]).then_inc(pQS, 1)

        if BUILD_PHASES < 2:
            nc.finalize()
            return nc
        with ExitStack() as astack:
            if SKIP_A:
                pass
            def asb(*a):
                return astack.enter_context(nc.sbuf_tensor(*a))
            K_sb = asb("K_sb", [128, 8 * S], F16)
            xa = [asb(f"xa{i}", [128, D], F32) for i in range(2)]
            x2b = [asb(f"x2b{i}", [128, D], F32) for i in range(2)]
            xn_tm = asb("xn_tm", [128, 4 * D], F16)
            xn_fm = [asb(f"xn_fm{i}", [128, 8 * 512], F16) for i in range(2)]
            q_fm = asb("q_fm", [128, 8 * 512], F16)
            v_sb = asb("v_sb", [128, 4 * D], F16)
            out1b = [asb(f"out1b{i}", [128, D], F16) for i in range(2)]
            ybuf = [asb(f"ybuf{i}", [128, D], F16) for i in range(2)]
            e_dump = asb("e_dump", [128, D], F16)
            masked_sb = asb("masked_sb", [128, 128], F32)
            msum_sb = asb("msum_sb", [128, 2], F32)
            m1_sb = asb("m1_sb", [128, 2], F32)
            mm_sb = asb("mm_sb", [128, 2], F32)
            v1_sb = asb("v1_sb", [128, 2], F32)
            sqacc_sb = asb("sqacc_sb", [128, 2], F32)
            lnt1_sb = asb("lnt1_sb", [128, 2], F32)
            rstd_sb = asb("rstd_sb", [128, 2], F32)
            msum2_sb = asb("msum2_sb", [128, 2], F32)
            m2_sb = asb("m2_sb", [128, 2], F32)
            mm2_sb = asb("mm2_sb", [128, 2], F32)
            v2_sb = asb("v2_sb", [128, 2], F32)
            sqacc2_sb = asb("sqacc2_sb", [128, 2], F32)
            lnt2_sb = asb("lnt2_sb", [128, 2], F32)
            rstd2_sb = asb("rstd2_sb", [128, 2], F32)
            mrun_sb = [asb(f"mrun{i}", [128, 1], F32) for i in range(2)]
            mneg_sb = [asb(f"mneg{i}", [128, 1], F32) for i in range(2)]
            delta_sb = [asb(f"delta{i}", [128, 1], F32) for i in range(2)]
            srun_sb = [asb(f"srun{i}", [128, 1], F32) for i in range(2)]
            esum_sb = [asb(f"esum{i}", [128, 1], F32) for i in range(2)]
            resc_sb = [asb(f"resc{i}", [128, 1], F32) for i in range(2)]
            gm_sb = asb("gm_sb", [128, 2], F32)
            den_sb = asb("den_sb", [128, 1], F32)
            rden_sb = asb("rden_sb", [128, 1], F32)
            rs_sb = asb("rs_sb", [128, 1], F32)
            ymt_sb = asb("ymt_sb", [128, 1], F32)
            ps = [astack.enter_context(
                nc.psum_tensor(f"psA{i}", [128, 512], F32)) for i in range(8)]

            ablk = astack.enter_context(nc.Block())

            NSUB = 4 * NST

            def xl_sem(k):
                return aXL0 if k % 2 == 0 else aXL1

            def x2_sem(k):
                return aX20 if k % 2 == 0 else aX21

            def os_sem(k):
                return aOS0 if k % 2 == 0 else aOS1

            def ys_sem(k):
                return aYS0 if k % 2 == 0 else aYS1

            def tr_sem(stx):
                return aTR0 if stx % 2 == 0 else aTR1

            @ablk.sync
            def _(sync):
                if SKIP_A:
                    return
                for j in range(2):
                    sync.dma_start(out=xa[j][:, :],
                                   in_=x_h[j * 128:(j + 1) * 128, :]
                                   ).then_inc(xl_sem(j), 16)
                    sync.dma_start(out=x2b[j][:, :],
                                   in_=x_h[j * 128:(j + 1) * 128, :]
                                   ).then_inc(x2_sem(j), 16)
                for j in range(2, 8):
                    sync.wait_ge(aXLF, j - 1)
                    sync.wait_ge(aSQ1, j - 1)
                    sync.dma_start(out=xa[j % 2][:, :],
                                   in_=x_h[j * 128:(j + 1) * 128, :]
                                   ).then_inc(xl_sem(j), 16)
                for p in range(NSUB):
                    j2 = p + 2
                    if j2 < NSUB:
                        sync.wait_ge(aX2F, p + 1)
                        sync.dma_start(out=x2b[j2 % 2][:, :],
                                       in_=x_h[j2 * 128:(j2 + 1) * 128, :]
                                       ).then_inc(x2_sem(j2), 16)
                    ja = p + 8
                    if ja < NSUB:
                        sync.wait_ge(aXLF, ja - 1)
                        sync.wait_ge(aSQ1, ja - 1)
                        sync.dma_start(out=xa[ja % 2][:, :],
                                       in_=x_h[ja * 128:(ja + 1) * 128, :]
                                       ).then_inc(xl_sem(ja), 16)

            @ablk.gpsimd
            def _(gpsimd):
                if SKIP_A:
                    return

            @ablk.scalar
            def _(scalar):
                if SKIP_A:
                    return

                def fe_sc(kN):
                    p = kN % 2
                    stN = kN // 4
                    ttn = kN % 4
                    scalar.wait_ge(xl_sem(kN), 16 * (kN // 2 + 1))
                    scalar.activation(e_dump[:, 0:D], xa[p][:, :], ACT.Square,
                                      accum_out=sqacc_sb[:, p:p + 1]
                                      ).then_inc(aSQ1, 1)
                    scalar.wait_ge(aVAR1, kN + 1)
                    sbar(scalar.activation(lnt1_sb[:, p:p + 1],
                                           v1_sb[:, p:p + 1], ACT.Ln,
                                           bias=eps_sb[:, :], scale=1.0))
                    scalar.activation(rstd_sb[:, p:p + 1], lnt1_sb[:, p:p + 1],
                                      ACT.Exp, scale=-0.5).then_inc(aRSTD1, 1)
                    scalar.wait_ge(aLN, kN + 1)
                    if stN >= 2:
                        scalar.wait_ge(aPJ, 24 * (stN - 1))
                    buf = xn_fm[stN % 2]
                    for c in range(8):
                        scalar.dma_start_transpose(
                            buf[:, c * 512 + ttn * 128: c * 512 + (ttn + 1) * 128],
                            xn_tm[:, ttn * D + c * 128: ttn * D + (c + 1) * 128]
                        ).then_inc(tr_sem(stN), 16)

                for j in range(7):
                    fe_sc(j)
                for st in range(NST):
                    sl = st % SPB
                    grp_by_tt = {}
                    for rec in SGROUPS[st]:
                        grp_by_tt.setdefault(rec[0], []).append(rec)
                    for tt in range(4):
                        k = 4 * st + tt
                        kp = k % 2
                        for (_tt, g, ng, is_diag, bank, sgidx) in grp_by_tt[tt]:
                            scalar.wait_ge(aMN, sgidx + 1)
                            par = g % 2
                            scalar.activation(resc_sb[par][:, :],
                                              delta_sb[par][:, :],
                                              ACT.Exp, scale=1.0 / 32.0)
                            src = (masked_sb[:, 0:128] if is_diag
                                   else ps[bank][:, 0:ng])
                            scalar.activation(e_dump[:, 0:ng], src, ACT.Exp,
                                              scale=1.0 / 32.0,
                                              bias=mneg_sb[par][:, :],
                                              accum_out=esum_sb[par][:, :]
                                              ).then_inc(aES, 1)
                            if is_diag:
                                scalar.activation(resc_sb[(g + 1) % 2][:, :],
                                                  mneg_sb[par][:, :], ACT.Exp,
                                                  scale=1.0).then_inc(aEM, 1)
                        scalar.wait_ge(aO1, k + 1)
                        scalar.activation(e_dump[:, 0:D], out1b[kp][:, :],
                                          ACT.Square,
                                          accum_out=sqacc2_sb[:, kp:kp + 1]
                                          ).then_inc(aSQ2, 1)
                        scalar.dma_start(out=o16_h[k * 128:(k + 1) * 128, :],
                                         in_=out1b[kp][:, :]
                                         ).then_inc(os_sem(k), 16)
                        scalar.wait_ge(aVAR2, k + 1)
                        sbar(scalar.activation(lnt2_sb[:, kp:kp + 1],
                                               v2_sb[:, kp:kp + 1], ACT.Ln,
                                               bias=eps_sb[:, :], scale=1.0))
                        scalar.activation(rstd2_sb[:, kp:kp + 1],
                                          lnt2_sb[:, kp:kp + 1], ACT.Exp,
                                          scale=-0.5).then_inc(aRSTD2, 1)
                        if k >= 1:
                            scalar.wait_ge(aY, k)
                            scalar.dma_start(
                                out=y16_h[(k - 1) * 128:k * 128, :],
                                in_=ybuf[(k - 1) % 2][:, :]
                            ).then_inc(ys_sem(k - 1), 16)
                        if k + 7 < NSUB:
                            fe_sc(k + 7)
                scalar.wait_ge(aY, NSUB)
                scalar.dma_start(out=y16_h[(NSUB - 1) * 128:NSUB * 128, :],
                                 in_=ybuf[(NSUB - 1) % 2][:, :]
                                 ).then_inc(ys_sem(NSUB - 1), 16)
                scalar.wait_ge(aOS0, 16 * (NSUB // 2))
                scalar.wait_ge(aOS1, 16 * (NSUB // 2))
                scalar.wait_ge(aYS0, 16 * (NSUB // 2))
                scalar.wait_ge(aYS1, 16 * (NSUB // 2))

            @ablk.tensor
            def _(tensor):
                if SKIP_A:
                    return
                for st in range(NST):
                    sl = st % SPB
                    buf = xn_fm[st % 2]
                    tensor.wait_ge(tr_sem(st), 512 * (st // 2 + 1))
                    for u in range(24):
                        pu = 24 * st + u
                        if u < 4:
                            if st >= 1 and u in LASTUSE[st - 1]:
                                tensor.wait_ge(aES, LASTUSE[st - 1][u] + 1)
                        elif pu >= 4:
                            prev = pu - 4
                            pm = prev % 24
                            if pm < 16:
                                tensor.wait_ge(aQK, 16 * (prev // 24) + pm + 1)
                            else:
                                tensor.wait_ge(aVE, 8 * (prev // 24) + pm - 16 + 1)
                        bank = ps[u % 4]
                        if u < 8:
                            for c in range(8):
                                ins = tensor.matmul(
                                    bank[:, :],
                                    wq_sb[:, c * D + u * 128: c * D + (u + 1) * 128],
                                    buf[:, c * 512:(c + 1) * 512],
                                    start=(c == 0), stop=(c == 7))
                        elif u < 16:
                            co = u - 8
                            for c in range(8):
                                ins = tensor.matmul(
                                    bank[:, :],
                                    wk_sb[:, c * D + co * 128: c * D + (co + 1) * 128],
                                    buf[:, c * 512:(c + 1) * 512],
                                    start=(c == 0), stop=(c == 7))
                        else:
                            tt = (u - 16) // 2
                            half = (u - 16) % 2
                            for c in range(8):
                                ins = tensor.matmul(
                                    bank[:, :],
                                    buf[:, c * 512 + tt * 128: c * 512 + (tt + 1) * 128],
                                    wv_sb[:, c * D + half * 512: c * D + half * 512 + 512],
                                    start=(c == 0), stop=(c == 7))
                        ins.then_inc(aPJ, 1)
                    tensor.wait_ge(aQK, 16 * (st + 1))
                    seen = set()
                    for (tt, g, ng, is_diag, bank, sgidx) in SGROUPS[st]:
                        qt = sl * 4 + tt
                        if bank < 4 and bank not in seen:
                            seen.add(bank)
                            tensor.wait_ge(aVE, 8 * st + bank + 5)
                        if sgidx >= 8:
                            tensor.wait_ge(aES, sgidx - 7)
                        k0 = qt * 128 if is_diag else g * 512
                        for c in range(8):
                            ins = tensor.matmul(
                                ps[bank][:, 0:ng],
                                q_fm[:, c * 512 + tt * 128: c * 512 + (tt + 1) * 128],
                                K_sb[:, c * S + k0: c * S + k0 + ng],
                                start=(c == 0), stop=(c == 7))
                        ins.then_inc(aSG, 1)

            @ablk.vector
            def _(vector):
                if SKIP_A:
                    return
                slf = [0]

                def self_wait(ins):
                    slf[0] += 1
                    ins.then_inc(aSLF, 1)
                    nc.vector.wait_ge(aSLF, slf[0])

                def fe_dve(kN):
                    p = kN % 2
                    vector.wait_ge(xl_sem(kN), 16 * (kN // 2 + 1))
                    vector.reduce_sum(msum_sb[:, p:p + 1], xa[p][:, :], AX)
                    vbar()
                    vbar(vector.tensor_scalar(m1_sb[:, p:p + 1],
                                              msum_sb[:, p:p + 1],
                                              1.0 / D, None, ALU.mult))
                    vector.wait_ge(aSQ1, kN + 1)
                    vbar(vector.tensor_mul(mm_sb[:, p:p + 1], m1_sb[:, p:p + 1],
                                           m1_sb[:, p:p + 1]))
                    vector.scalar_tensor_tensor(
                        v1_sb[:, p:p + 1], sqacc_sb[:, p:p + 1], 1.0 / D,
                        mm_sb[:, p:p + 1], ALU.mult,
                        ALU.subtract).then_inc(aVAR1, 1)
                    nc.vector.wait_ge(aVAR1, kN + 1)
                    vector.wait_ge(aRSTD1, kN + 1)
                    if kN >= 4:
                        stp = (kN - 4) // 4
                        vector.wait_ge(tr_sem(stp), 512 * (stp // 2 + 1))
                    ttn = kN % 4
                    vector.tensor_scalar(xn_tm[:, ttn * D:(ttn + 1) * D],
                                         xa[p][:, :], m1_sb[:, p:p + 1],
                                         rstd_sb[:, p:p + 1],
                                         ALU.subtract, ALU.mult)
                    nc.vector.engine_nop().then_inc(aLN, 1)
                    nc.vector.engine_nop().then_inc(aXLF, 1)

                def qk_evacs(stq):
                    slq = stq % SPB
                    for u in range(16):
                        vector.wait_ge(aPJ, 24 * stq + u + 1)
                        bank = ps[u % 4]
                        if u < 8:
                            dest = q_fm[:, u * 512:(u + 1) * 512]
                            bias = bq_sb[:, u:u + 1]
                        else:
                            c = u - 8
                            dest = K_sb[:, c * S + slq * 512:
                                        c * S + (slq + 1) * 512]
                            bias = bk_sb[:, c:c + 1]
                        vector.tensor_scalar(dest, bank[:, :], bias, None,
                                             ALU.add).then_inc(aQK, 1)

                for j in range(7):
                    fe_dve(j)
                qk_evacs(0)
                for st in range(NST):
                    sl = st % SPB
                    for u in range(8):
                        vector.wait_ge(aPJ, 24 * st + 16 + u + 1)
                        bank = ps[(16 + u) % 4]
                        tt = u // 2
                        half = u % 2
                        dst = v_sb[:, tt * D + half * 512: tt * D + half * 512 + 512]
                        vector.tensor_add(dst, bank[:, :],
                                          bv_bc[:, half * 512: half * 512 + 512]
                                          ).then_inc(aVE, 1)
                    grp_by_tt = {}
                    for rec in SGROUPS[st]:
                        grp_by_tt.setdefault(rec[0], []).append(rec)
                    for tt in range(4):
                        k = 4 * st + tt
                        kp = k % 2
                        qt = sl * 4 + tt
                        recs = grp_by_tt[tt]
                        vector.memset(mrun_sb[0][:, :], 0.0)
                        vector.memset(srun_sb[0][:, :], 0.0)

                        def red(i):
                            (_t, _g, ngi, dia, bki, sgi) = recs[i][0], recs[i][1], recs[i][2], recs[i][3], recs[i][4], recs[i][5]
                            vector.wait_ge(aSG, sgi + 1)
                            if dia:
                                vbar(vector.tensor_add(masked_sb[:, :],
                                                       ps[bki][:, 0:128],
                                                       mask_sb[:, :]))
                                srci = masked_sb[:, :]
                            else:
                                srci = ps[bki][:, 0:ngi]
                            vector.reduce_max(gm_sb[:, i % 2:i % 2 + 1], srci, AX)

                        red(0)
                        for i, (_tt, g, ng, is_diag, bank, sgidx) in enumerate(recs):
                            par = g % 2
                            nxt = (g + 1) % 2
                            if i + 1 < len(recs):
                                red(i + 1)
                            vbar()
                            vbar(vector.tensor_max(mrun_sb[nxt][:, :],
                                                   mrun_sb[par][:, :],
                                                   gm_sb[:, i % 2:i % 2 + 1]))
                            vector.tensor_sub(delta_sb[par][:, :],
                                              mrun_sb[par][:, :],
                                              mrun_sb[nxt][:, :])
                            vector.tensor_scalar(mneg_sb[par][:, :],
                                                 mrun_sb[nxt][:, :], -1.0 / 32.0,
                                                 None, ALU.mult).then_inc(aMN, 1)
                            vector.wait_ge(aES, sgidx + 1)
                            vector.scalar_tensor_tensor(
                                srun_sb[nxt][:, :], srun_sb[par][:, :],
                                resc_sb[par][:, :], esum_sb[par][:, :],
                                ALU.mult, ALU.add)
                        if tt == 3 and st + 1 < NST:
                            qk_evacs(st + 1)
                        lastpar = len(recs) % 2
                        vector.wait_ge(aEM, 4 * st + tt + 1)
                        vbar(vector.scalar_tensor_tensor(
                            den_sb[:, :], resc_sb[lastpar][:, :],
                            cnt_sb[:, qt:qt + 1], srun_sb[lastpar][:, :],
                            ALU.mult, ALU.add))
                        vbar(vector.reciprocal(rden_sb[:, :], den_sb[:, :]))
                        self_wait(vector.tensor_mul(rs_sb[:, :], den_sb[:, :],
                                                    rden_sb[:, :]))
                        vector.wait_ge(x2_sem(k), 16 * (k // 2 + 1))
                        if k >= 2:
                            vector.wait_ge(os_sem(k), 16 * (k // 2))
                        vector.scalar_tensor_tensor(
                            out1b[kp][:, :],
                            v_sb[:, tt * D:(tt + 1) * D], rs_sb[:, :],
                            x2b[kp][:, :], ALU.mult, ALU.add)
                        nc.vector.engine_nop().then_inc(aO1, 1)
                        nc.vector.engine_nop().then_inc(aX2F, 1)
                        vbar()
                        vector.reduce_sum(msum2_sb[:, kp:kp + 1],
                                          out1b[kp][:, :], AX)
                        vbar()
                        vbar(vector.tensor_scalar(m2_sb[:, kp:kp + 1],
                                                  msum2_sb[:, kp:kp + 1],
                                                  1.0 / D, None, ALU.mult))
                        vector.wait_ge(aSQ2, k + 1)
                        vbar(vector.tensor_mul(mm2_sb[:, kp:kp + 1],
                                               m2_sb[:, kp:kp + 1],
                                               m2_sb[:, kp:kp + 1]))
                        vector.scalar_tensor_tensor(
                            v2_sb[:, kp:kp + 1], sqacc2_sb[:, kp:kp + 1],
                            1.0 / D, mm2_sb[:, kp:kp + 1], ALU.mult,
                            ALU.subtract).then_inc(aVAR2, 1)
                        nc.vector.wait_ge(aVAR2, k + 1)
                        if k + 7 < NSUB:
                            fe_dve(k + 7)
                        vector.wait_ge(aRSTD2, k + 1)
                        if k >= 2:
                            vector.wait_ge(ys_sem(k), 16 * (k // 2))
                        vector.tensor_scalar(ybuf[kp][:, :], out1b[kp][:, :],
                                             m2_sb[:, kp:kp + 1],
                                             rstd2_sb[:, kp:kp + 1],
                                             ALU.subtract, ALU.mult)
                        nc.vector.engine_nop().then_inc(aY, 1)
                        vbar()
                        vector.reduce_max(ymt_sb[:, :], ybuf[kp][:, :], AX)
                        vbar()
                        vbar(vector.tensor_max(
                            ymax_sb[:, (k + 1) % 2:(k + 1) % 2 + 1],
                            ymax_sb[:, k % 2:k % 2 + 1], ymt_sb[:, :]))

        pa.close()
        if BUILD_PHASES < 3:
            nc.finalize()
            return nc

        with ExitStack() as ccom:
            def csb(*a):
                return ccom.enter_context(nc.sbuf_tensor(*a))
            w2qT_sb = csb("w2qT_sb", [128, 32 * D], F16)
            prstack = ExitStack()
            psR = [prstack.enter_context(nc.psum_tensor(f"psR{i}", [128, 512], F32))
                   for i in range(2)]

            with ExitStack() as bstack:
                def bsb(*a):
                    return bstack.enter_context(nc.sbuf_tensor(*a))
                w1qT_sb = bsb("w1qT_sb", [128, 8 * FF], F16)
                y_fm = [bsb(f"y_fm{i}", [128, 8 * 512], F16) for i in range(2)]
                xq_sb = [bsb(f"xq{i}", [128, 8 * 512], F16) for i in range(2)]
                yq_u8 = bsb("yq_u8", [128, 512], U8)
                h_tile = [bsb(f"h_tile{i}", [128, 512], F16) for i in range(4)]
                hmc_sb = bsb("hmc_sb", [128, 32], F32)
                hmt_sb = bsb("hmt_sb", [128, 1], F32)
                b1i_sb = bsb("b1i_sb", [128, 32], I32)
                b1f_sb = bsb("b1f_sb", [128, 32], F32)
                rcs1_sb = bsb("rcs1_sb", [128, 32], F32)
                ps_b = [bstack.enter_context(
                    nc.psum_tensor(f"psHB{i}", [128, 512], F32)) for i in range(4)]

                ar1blk = nc.Block()
                ar1blk.__enter__()

                @ar1blk.sync
                def _(sync):
                    if SKIP_AR1:
                        return
                    ymfin = ymax_sb[:, NST % 2:NST % 2 + 1]
                    sync.dma_start(out=red1_h[:, :], in_=ymfin).then_inc(r1S, 16)
                    sync.wait_ge(r1S, 16)
                    sync.dma_start(out=redt_sb[:, :],
                                   in_=red1_h[:, :]).then_inc(r1S, 16)
                    sync.wait_ge(r1D, 1)
                    sync.dma_start(out=cc1i_h[:, :],
                                   in_=cc_sb[:, :]).then_inc(r1S, 16)
                    sync.wait_ge(r1Cc, 1)
                    sync.dma_start(out=g1_sb[:, :],
                                   in_=cc1o_h[0:1, 0:1]).then_inc(r1C, 16)

                @ar1blk.vector
                def _(vector):
                    if SKIP_AR1:
                        return
                    vector.wait_ge(r1S, 32)
                    vector.reduce_max(g1_sb[:, :], redt_sb[:, :], AX)
                    vbar()
                    vbar(vector.memset(cc_sb[:, :], 0.0))
                    vector.tensor_copy(cc_sb[0:1, 0:1], g1_sb[:, :]).then_inc(r1D, 1)
                    vector.wait_ge(r1C, 16)
                    vector.tensor_scalar(s1v_sb[:, :], g1_sb[:, :], EPS_Q,
                                         1.0 / 255.0, ALU.max,
                                         ALU.mult).then_inc(r1D, 1)
                    vector.wait_ge(r1P, 1)
                    vector.tensor_copy(s1b_sb[:, :],
                                       psR[0][:, 0:1]).then_inc(r1D, 1)
                    nc.vector.wait_ge(r1D, 3)
                    vector.reciprocal(r1b_sb[:, :], s1b_sb[:, :])
                    vector.tensor_scalar(scale1_sb[:, :], sw1_sb[:, :],
                                         s1b_sb[:, :], 1.0 / 127.0, ALU.mult,
                                         ALU.mult).then_inc(r1D, 1)
                    nc.vector.wait_ge(r1D, 4)
                    vbar(vector.reciprocal(rcs1_sb[:, :], scale1_sb[:, :]))
                    vbar(vector.tensor_mul(b1f_sb[:, :], b1_sb[:, :], rcs1_sb[:, :]))
                    vbar(vector.tensor_copy(b1i_sb[:, :], b1f_sb[:, :]))
                    vbar(vector.tensor_copy(b1f_sb[:, :], b1i_sb[:, :]))
                    vbar(vector.tensor_mul(b1q_sb[:, :], b1f_sb[:, :],
                                           scale1_sb[:, :]))

                @ar1blk.gpsimd
                def _(gpsimd):
                    if SKIP_AR1:
                        return
                    gpsimd.wait_ge(r1S, 48)
                    gpsimd.collective_compute(
                        "AllReduce", ALU.max,
                        replica_groups=[list(range(NCORES))],
                        ins=[cc1i_h.ap().opt()],
                        outs=[cc1o_h.ap().opt()]).then_inc(r1Cc, 1)

                @ar1blk.tensor
                def _(tensor):
                    if SKIP_AR1:
                        return
                    tensor.wait_ge(r1D, 2)
                    tensor.matmul(psR[0][:, 0:1], ones_sb[:, :], s1v_sb[:, :],
                                  start=True, stop=True).then_inc(r1P, 1)

                @ar1blk.scalar
                def _(scalar):
                    if SKIP_AR1:
                        return
                    for dc in range(8):
                        scalar.dma_start_transpose(
                            w1qT_sb[:, dc * FF:(dc + 1) * FF],
                            w1q_h[0:FF, dc * 128:(dc + 1) * 128]).then_inc(r1T, 16)
                    for fc in range(32):
                        scalar.dma_start_transpose(
                            w2qT_sb[:, fc * D:(fc + 1) * D],
                            w2q_h[0:D, fc * 128:(fc + 1) * 128]).then_inc(r1T, 16)
                    scalar.wait_ge(r1T, 16 * 40)

                ar1blk.__exit__(None, None, None)
                if BUILD_PHASES < 4:
                    nc.finalize()
                    return nc
                bblk = bstack.enter_context(nc.Block())

                @bblk.sync
                def _(sync):
                    for g in range(16):
                        if g >= 2:
                            sync.wait_ge(bYQ, g - 1)
                        for c in range(8):
                            sync.dma_start_transpose(
                                y_fm[g % 2][:, c * 512:(c + 1) * 512],
                                y16_h[g * 512:(g + 1) * 512,
                                      c * 128:(c + 1) * 128]).then_inc(bYL, 16)

                @bblk.vector
                def _(vector):
                    def hmax_group(g):
                        for fc in range(32):
                            u = 32 * g + fc
                            nc.vector.wait_ge(bHE, u + 1)
                            nc.vector.reduce_max(hmc_sb[:, fc:fc + 1],
                                                 h_tile[fc % 4][:, :], AX)
                            nc.vector.engine_nop().then_inc(bHR, 1)
                        nc.vector.wait_ge(bHR, 32 * (g + 1))
                        nc.vector.reduce_max(hmt_sb[:, :], hmc_sb[:, :], AX)
                        vbar()
                        nc.vector.tensor_max(
                            hmax_sb[:, (g + 1) % 2:(g + 1) % 2 + 1],
                            hmax_sb[:, g % 2:g % 2 + 1], hmt_sb[:, :])

                    for g in range(16):
                        vector.wait_ge(bYL, 128 * (g + 1))
                        if g >= 2 and B_STAGES >= 2:
                            vector.wait_ge(bHM, 32 * (g - 1))
                        for c in range(8):
                            sl_ = slice(c * 512, (c + 1) * 512)
                            vbar(vector.tensor_scalar(yq_u8[:, :],
                                                      y_fm[g % 2][:, sl_],
                                                      r1b_sb[:, :], None,
                                                      ALU.mult))
                            vector.tensor_copy(xq_sb[g % 2][:, sl_], yq_u8[:, :])
                        nc.vector.engine_nop().then_inc(bYQ, 1)
                        nc.vector.engine_nop().then_inc(bXQ, 1)
                        if g >= 1 and B_STAGES >= 3 and B_HMAX:
                            hmax_group(g - 1)
                    if B_STAGES >= 3 and B_HMAX:
                        hmax_group(15)

                @bblk.tensor
                def _(tensor):
                    if B_STAGES < 2:
                        return
                    for g in range(16):
                        tensor.wait_ge(bXQ, g + 1)
                        for fc in range(32):
                            u = 32 * g + fc
                            if u >= 4 and B_STAGES >= 3:
                                tensor.wait_ge(bHE, u - 3)
                            for dc in range(8):
                                ins = tensor.matmul(
                                    ps_b[fc % 4][:, :],
                                    w1qT_sb[:, dc * FF + fc * 128:
                                            dc * FF + (fc + 1) * 128],
                                    xq_sb[g % 2][:, dc * 512:(dc + 1) * 512],
                                    start=(dc == 0), stop=(dc == 7))
                            ins.then_inc(bHM, 1)

                @bblk.scalar
                def _(scalar):
                    def evac_group(g):
                        for fc in range(32):
                            u = 32 * g + fc
                            scalar.wait_ge(bHM, u + 1)
                            if u >= 4:
                                if B_STAGES >= 4:
                                    scalar.wait_ge([bHS0, bHS1, bHS2, bHS3][u % 4],
                                                   16 * (u // 4))
                                if B_HMAX:
                                    scalar.wait_ge(bHR, u - 3)
                            if B_SIMPLE_EVAC:
                                scalar.copy(h_tile[fc % 4][:, :],
                                            ps_b[fc % 4][:, :]).then_inc(bHE, 1)
                            else:
                                scalar.activation(h_tile[fc % 4][:, :],
                                                  ps_b[fc % 4][:, :], ACT.Relu,
                                                  bias=b1q_sb[:, fc:fc + 1],
                                                  scale=scale1_sb[:, fc:fc + 1]
                                                  ).then_inc(bHE, 1)
                            if B_STAGES >= 4:
                                scalar.wait_ge(bHE, u + 1)
                                scalar.dma_start(out=h16_h[fc, g],
                                                 in_=h_tile[fc % 4][:, :]
                                                 ).then_inc([bHS0, bHS1, bHS2, bHS3][u % 4], 16)

                    for g in range(16):
                        if B_STAGES >= 3 and g >= 1:
                            evac_group(g - 1)
                    if B_STAGES >= 3:
                        evac_group(15)
                    if B_STAGES >= 4:
                        for bb in range(4):
                            scalar.wait_ge([bHS0, bHS1, bHS2, bHS3][bb], 16 * 8 * 16)

            if BUILD_PHASES < 5:
                nc.finalize()
                return nc
            with ExitStack() as cstack:
                def c2sb(*a):
                    return cstack.enter_context(nc.sbuf_tensor(*a))
                SCALE2B = c2sb("SCALE2B", [128, D], F32)
                B2QB = c2sb("B2QB", [128, D], F32)

                arstack = cstack.enter_context(ExitStack())

                def arsb(*a):
                    return arstack.enter_context(nc.sbuf_tensor(*a))
                b2row_sb = arsb("b2row_sb", [1, D], F32)
                s2wrow_sb = arsb("s2wrow_sb", [1, D], F32)
                scale2row_sb = arsb("scale2row_sb", [1, D], F32)
                b2qrow_sb = arsb("b2qrow_sb", [1, D], F32)
                rc2row_sb = arsb("rc2row_sb", [1, D], F32)
                tmp1r_sb = arsb("tmp1r_sb", [1, D], F32)
                tmpi1r_sb = arsb("tmpi1r_sb", [1, D], I32)

                ar2blk = nc.Block()
                ar2blk.__enter__()

                @ar2blk.sync
                def _(sync):
                    hmfin = hmax_sb[:, 16 % 2:16 % 2 + 1]
                    sync.dma_start(out=red2_h[:, :], in_=hmfin).then_inc(r2S, 16)
                    sync.wait_ge(r2S, 16)
                    sync.dma_start(out=redt_sb[:, :],
                                   in_=red2_h[:, :]).then_inc(r2S, 16)
                    sync.dma_start(out=s2wrow_sb[:, :],
                                   in_=sw2d_h[:, :]).then_inc(r2S, 16)
                    sync.dma_start(out=b2row_sb[:, :],
                                   in_=b2_h[:, :]).then_inc(r2S, 16)
                    sync.wait_ge(r2D, 1)
                    sync.dma_start(out=cc2i_h[:, :],
                                   in_=cc_sb[:, :]).then_inc(r2S, 16)
                    sync.wait_ge(r2Cc, 1)
                    sync.dma_start(out=g2_sb[:, :],
                                   in_=cc2o_h[0:1, 0:1]).then_inc(r2C, 16)

                @ar2blk.vector
                def _(vector):
                    vector.wait_ge(r2S, 32)
                    vector.reduce_max(g2_sb[:, :], redt_sb[:, :], AX)
                    vbar()
                    vbar(vector.memset(cc_sb[:, :], 0.0))
                    vector.tensor_copy(cc_sb[0:1, 0:1], g2_sb[:, :]).then_inc(r2D, 1)
                    vector.wait_ge(r2C, 16)
                    vector.tensor_scalar(s2v_sb[:, :], g2_sb[:, :], EPS_Q,
                                         1.0 / 255.0, ALU.max,
                                         ALU.mult).then_inc(r2D, 1)
                    nc.vector.wait_ge(r2D, 2)
                    vector.wait_ge(r2S, 64)
                    vbar(vector.tensor_scalar(scale2row_sb[:, :], s2wrow_sb[:, :],
                                              s2v_sb[:, :], 1.0 / 127.0, ALU.mult,
                                              ALU.mult))
                    vbar(vector.reciprocal(rc2row_sb[:, :], scale2row_sb[:, :]))
                    vbar(vector.tensor_mul(tmp1r_sb[:, :], b2row_sb[:, :],
                                           rc2row_sb[:, :]))
                    vbar(vector.tensor_copy(tmpi1r_sb[:, :], tmp1r_sb[:, :]))
                    vbar(vector.tensor_copy(tmp1r_sb[:, :], tmpi1r_sb[:, :]))
                    vector.tensor_mul(b2qrow_sb[:, :], tmp1r_sb[:, :],
                                      scale2row_sb[:, :]).then_inc(r2D, 1)
                    vector.wait_ge(r2P, 2)
                    vector.tensor_copy(SCALE2B[:, 0:512], psR[0][:, :])
                    vector.tensor_copy(SCALE2B[:, 512:1024],
                                       psR[1][:, :]).then_inc(r2D, 1)
                    vector.wait_ge(r2P, 4)
                    vector.tensor_copy(B2QB[:, 0:512], psR[0][:, :])
                    vector.tensor_copy(B2QB[:, 512:1024],
                                       psR[1][:, :]).then_inc(r2D, 1)
                    vector.wait_ge(r2P, 5)
                    vector.tensor_copy(s2b_sb[:, :],
                                       psR[0][:, 0:1]).then_inc(r2D, 1)
                    nc.vector.wait_ge(r2D, 6)
                    vector.reciprocal(r2b_sb[:, :], s2b_sb[:, :])

                @ar2blk.gpsimd
                def _(gpsimd):
                    gpsimd.wait_ge(r2S, 80)
                    gpsimd.collective_compute(
                        "AllReduce", ALU.max,
                        replica_groups=[list(range(NCORES))],
                        ins=[cc2i_h.ap().opt()],
                        outs=[cc2o_h.ap().opt()]).then_inc(r2Cc, 1)

                @ar2blk.tensor
                def _(tensor):
                    tensor.wait_ge(r2D, 3)
                    tensor.matmul(psR[0][:, :], ones_sb[:, :],
                                  scale2row_sb[0:1, 0:512], start=True,
                                  stop=True).then_inc(r2P, 1)
                    tensor.matmul(psR[1][:, :], ones_sb[:, :],
                                  scale2row_sb[0:1, 512:1024], start=True,
                                  stop=True).then_inc(r2P, 1)
                    tensor.wait_ge(r2D, 4)
                    tensor.matmul(psR[0][:, :], ones_sb[:, :],
                                  b2qrow_sb[0:1, 0:512], start=True,
                                  stop=True).then_inc(r2P, 1)
                    tensor.matmul(psR[1][:, :], ones_sb[:, :],
                                  b2qrow_sb[0:1, 512:1024], start=True,
                                  stop=True).then_inc(r2P, 1)
                    tensor.wait_ge(r2D, 5)
                    tensor.matmul(psR[0][:, 0:1], ones_sb[:, :], s2v_sb[:, :],
                                  start=True, stop=True).then_inc(r2P, 1)

                ar2blk.__exit__(None, None, None)
                arstack.close()
                prstack.close()
                if BUILD_PHASES < 6:
                    nc.finalize()
                    return nc
                ps_c = [cstack.enter_context(
                    nc.psum_tensor(f"psOC{i}", [128, 512], F32)) for i in range(8)]
                h_ld = [c2sb(f"h_ld{i}", [128, 32 * 512], F16) for i in range(2)]
                o1_ld = [c2sb(f"o1_ld{i}", [128, 4 * D], F16) for i in range(2)]
                fout_sb = c2sb("fout_sb", [128, 4 * D], F32)
                hu8_sb = [c2sb(f"hu8{i}", [128, 512], U8) for i in range(2)]
                eb1_sb = c2sb("eb1_sb", [128, 512], F32)
                eb2_sb = c2sb("eb2_sb", [128, 512], F32)

                cblk = cstack.enter_context(nc.Block())

                @cblk.sync
                def _(sync):
                    for g in range(16):
                        if g >= 2:
                            sync.wait_ge(cMM, 4 * (g - 1))
                            sync.wait_ge(cFO, g - 1)
                        for fc in range(32):
                            sync.dma_start(
                                out=h_ld[g % 2][:, fc * 512:(fc + 1) * 512],
                                in_=h16_h[fc, g]).then_inc(cHL, 16)
                        for tt in range(4):
                            r0 = g * 512 + tt * 128
                            sync.dma_start(out=o1_ld[g % 2][:, tt * D:(tt + 1) * D],
                                           in_=o16_h[r0:r0 + 128, :]
                                           ).then_inc(cOL, 16)
                        if g >= 1:
                            sync.wait_ge(cFO, g)
                            for tt in range(4):
                                r0 = (g - 1) * 512 + tt * 128
                                sync.dma_start(out=out_h[r0:r0 + 128, :],
                                               in_=fout_sb[:, tt * D:(tt + 1) * D]
                                               ).then_inc(cOS, 16)
                    sync.wait_ge(cFO, 16)
                    for tt in range(4):
                        r0 = 15 * 512 + tt * 128
                        sync.dma_start(out=out_h[r0:r0 + 128, :],
                                       in_=fout_sb[:, tt * D:(tt + 1) * D]
                                       ).then_inc(cOS, 16)
                    sync.wait_ge(cOS, 16 * 64)
                    if DEBUG_TAPS:
                        sync.dma_start(out=dbg_h[:, :], in_=dbg_sb[:, :]).then_inc(cOS, 16)
                        sync.wait_ge(cOS, 16 * 65)

                @cblk.vector
                def _(vector):
                    def epilogue_group(g):
                        if g >= 1:
                            nc.vector.wait_ge(cOS, 64 * g)
                        nc.vector.wait_ge(cOL, 64 * min(g + 2, 16))
                        for tt in range(4):
                            nc.vector.wait_ge(cMM, 4 * g + tt + 1)
                            for half in range(2):
                                bank = ps_c[2 * tt + half]
                                sl_ = slice(half * 512, half * 512 + 512)
                                osl = slice(tt * D + half * 512,
                                            tt * D + half * 512 + 512)
                                vbar(nc.vector.tensor_mul(eb1_sb[:, :],
                                                          bank[:, :],
                                                          SCALE2B[:, sl_]))
                                vbar(nc.vector.tensor_add(eb2_sb[:, :],
                                                          eb1_sb[:, :],
                                                          B2QB[:, sl_]))
                                nc.vector.tensor_add(fout_sb[:, osl],
                                                     eb2_sb[:, :],
                                                     o1_ld[g % 2][:, osl])
                            nc.vector.engine_nop().then_inc(cOE, 1)
                        nc.vector.engine_nop().then_inc(cFO, 1)

                    for g in range(16):
                        vector.wait_ge(cHL, 512 * (g + 1))
                        for fc in range(32):
                            u = 32 * g + fc
                            if u >= 2:
                                vector.wait_ge(cQ2, u - 1)
                            vector.tensor_scalar(
                                hu8_sb[fc % 2][:, :],
                                h_ld[g % 2][:, fc * 512:(fc + 1) * 512],
                                r2b_sb[:, :], None, ALU.mult).then_inc(cQ1, 1)
                        if g >= 1:
                            epilogue_group(g - 1)
                    epilogue_group(15)

                @cblk.scalar
                def _(scalar):
                    for g in range(16):
                        for fc in range(32):
                            u = 32 * g + fc
                            scalar.wait_ge(cQ1, u + 1)
                            scalar.copy(h_ld[g % 2][:, fc * 512:(fc + 1) * 512],
                                        hu8_sb[fc % 2][:, :]).then_inc(cQ2, 1)

                @cblk.tensor
                def _(tensor):
                    for g in range(16):
                        for tt in range(4):
                            tu = 4 * g + tt
                            if tu >= 4:
                                tensor.wait_ge(cOE, tu - 3)
                            b0 = ps_c[2 * tt]
                            b1 = ps_c[2 * tt + 1]
                            for fc in range(32):
                                if tt == 0:
                                    tensor.wait_ge(cQ2, 32 * g + fc + 1)
                                lhs = h_ld[g % 2][:, fc * 512 + tt * 128:
                                                  fc * 512 + (tt + 1) * 128]
                                tensor.matmul(b0[:, :], lhs,
                                              w2qT_sb[:, fc * D: fc * D + 512],
                                              start=(fc == 0), stop=(fc == 31))
                                ins = tensor.matmul(b1[:, :], lhs,
                                                    w2qT_sb[:, fc * D + 512:
                                                            fc * D + 1024],
                                                    start=(fc == 0),
                                                    stop=(fc == 31))
                            ins.then_inc(cMM, 1)

    nc.finalize()
    return nc


def kernel(L0_attn_X, L0_attn_Wq, L0_attn_bq, L0_attn_Wk, L0_attn_bk,
           L0_attn_Wv, L0_attn_bv, L0_ffn_w1, L0_ffn_b1, L0_ffn_w2, L0_ffn_b2):
    X = np.ascontiguousarray(np.asarray(L0_attn_X, dtype=np.float32))
    wq = np.ascontiguousarray(np.asarray(L0_attn_Wq, dtype=np.float32))
    wk = np.ascontiguousarray(np.asarray(L0_attn_Wk, dtype=np.float32))
    wv = np.ascontiguousarray(np.asarray(L0_attn_Wv, dtype=np.float32))
    bq = np.asarray(L0_attn_bq, dtype=np.float32)
    bk = np.asarray(L0_attn_bk, dtype=np.float32)
    bv = np.asarray(L0_attn_bv, dtype=np.float32)
    w1 = np.ascontiguousarray(np.asarray(L0_ffn_w1, dtype=np.float32))
    b1 = np.asarray(L0_ffn_b1, dtype=np.float32)
    w2 = np.ascontiguousarray(np.asarray(L0_ffn_w2, dtype=np.float32))
    b2 = np.asarray(L0_ffn_b2, dtype=np.float32)

    bqr = np.ascontiguousarray(bq.reshape(8, 128).T)
    bkr = np.ascontiguousarray(bk.reshape(8, 128).T)
    bvr = np.ascontiguousarray(bv.reshape(1, D))
    b1r = np.ascontiguousarray(b1.reshape(32, 128).T)
    b2r = np.ascontiguousarray(b2.reshape(1, D))
    ii = np.arange(128)
    mask = np.where(ii[None, :] <= ii[:, None], 0.0, NEG).astype(np.float16)
    qts = np.arange(32)
    cnt = (S - 1 - (qts[None, :] * 128 + ii[:, None])).astype(np.float32)
    onesf = np.ones((1, 128), np.float32)

    nc = build_nc()

    in_maps = []
    for c in range(NCORES):
        xs = np.ascontiguousarray(X[2 * c:2 * c + 2].reshape(T, D))
        in_maps.append(dict(x=xs, wq=wq, wk=wk, wv=wv, bqr=bqr, bkr=bkr,
                            bvr=bvr, w1=w1, b1r=b1r, w2=w2, b2r=b2r,
                            mask=mask, cnt=cnt, onesf=onesf))
    res = run_bass_kernel_spmd(nc, in_maps, core_ids=list(range(NCORES)))
    out = np.concatenate([np.asarray(res.results[c]["out"], np.float32)
                          for c in range(NCORES)], axis=0)
    return out.reshape(16, S, D)


# version 13
# speedup vs baseline: 1.1562x; 1.0169x over previous
import numpy as np
from contextlib import ExitStack

import concourse.bass as bass
import concourse.mybir as mybir
from concourse.bass_utils import run_bass_kernel_spmd

F32 = mybir.dt.float32
F16 = mybir.dt.float16
U8 = mybir.dt.uint8
I8 = mybir.dt.int8
I32 = mybir.dt.int32
ALU = mybir.AluOpType
ACT = mybir.ActivationFunctionType
AX = mybir.AxisListType.X

NCORES = 8
S = 4096
D = 1024
FF = 4096
T = 2 * S
NST = T // 512
SPB = S // 512
EPS_LN = 1e-5
EPS_Q = 1e-8
NEG = -60000.0


def _sgroups():
    out = []
    sgidx = 0
    for st in range(NST):
        sl = st % SPB
        rows = []
        for tt in range(4):
            qt = sl * 4 + tt
            gf = (qt * 128 + 511) // 512
            for g in range(gf):
                ng = min(512, qt * 128 - g * 512)
                rows.append((tt, g, ng, False, sgidx % 8, sgidx))
                sgidx += 1
            rows.append((tt, gf, 128, True, sgidx % 8, sgidx))
            sgidx += 1
        out.append(rows)
    return out


SGROUPS = _sgroups()
LASTUSE = [{b: max(r[5] for r in rows if r[4] == b)
            for b in set(r[4] for r in rows)} for rows in SGROUPS]
BUILD_PHASES = 6
B_STAGES = 4
B_HMAX = True
B_SIMPLE_EVAC = False
SKIP_A = False
SKIP_AR1 = False
SKIP_P = False
DEBUG_TAPS = False


def build_nc():
    nc = bass.Bass("TRN2", target_bir_lowering=False,
                   detect_race_conditions=False)

    x_h = nc.dram_tensor("x", [T, D], F32, kind="ExternalInput")
    wq_h = nc.dram_tensor("wq", [D, D], F32, kind="ExternalInput")
    wk_h = nc.dram_tensor("wk", [D, D], F32, kind="ExternalInput")
    wv_h = nc.dram_tensor("wv", [D, D], F32, kind="ExternalInput")
    bq_h = nc.dram_tensor("bqr", [128, 8], F32, kind="ExternalInput")
    bk_h = nc.dram_tensor("bkr", [128, 8], F32, kind="ExternalInput")
    bv_h = nc.dram_tensor("bvr", [1, D], F32, kind="ExternalInput")
    w1_h = nc.dram_tensor("w1", [FF, D], F32, kind="ExternalInput")
    b1_h = nc.dram_tensor("b1r", [128, 32], F32, kind="ExternalInput")
    w2_h = nc.dram_tensor("w2", [D, FF], F32, kind="ExternalInput")
    b2_h = nc.dram_tensor("b2r", [1, D], F32, kind="ExternalInput")
    mask_h = nc.dram_tensor("mask", [128, 128], F16, kind="ExternalInput")
    cnt_h = nc.dram_tensor("cnt", [128, 32], F32, kind="ExternalInput")
    ones_h = nc.dram_tensor("onesf", [1, 128], F32, kind="ExternalInput")
    out_h = nc.dram_tensor("out", [T, D], F32, kind="ExternalOutput")

    KI = "ExternalOutput" if DEBUG_TAPS else "Internal"
    dbg_h = nc.dram_tensor("dbg", [128, 64], F32, kind=KI)
    y16_h = nc.dram_tensor("y16", [T, D], F16, kind=KI)
    o16_h = nc.dram_tensor("o16", [T, D], F16, kind=KI)
    w1q_h = nc.dram_tensor("w1qs", [FF, D], F16, kind=KI)
    w2q_h = nc.dram_tensor("w2qs", [D, FF], F16, kind=KI)
    h16_h = nc.dram_tensor("h16", [32, 16, 128, 512], F16, kind=KI)
    red1_h = nc.dram_tensor("red1", [1, 128], F32, kind="Internal")
    red2_h = nc.dram_tensor("red2", [1, 128], F32, kind="Internal")
    sw2d_h = nc.dram_tensor("sw2d", [1, D], F32, kind=KI)
    cc1i_h = nc.dram_tensor("cc1i", [128, 4], F32, kind="Internal")
    cc1o_h = nc.dram_tensor("cc1o", [128, 4], F32, kind="Internal")
    cc2i_h = nc.dram_tensor("cc2i", [128, 4], F32, kind="Internal")
    cc2o_h = nc.dram_tensor("cc2o", [128, 4], F32, kind="Internal")

    with ExitStack() as outer:
        def osb(*a):
            return outer.enter_context(nc.sbuf_tensor(*a))

        def sem(name):
            return outer.enter_context(nc.semaphore(name))

        b1_sb = osb("b1_sb", [128, 32], F32)
        sw1_sb = osb("sw1_sb", [128, 32], F32)
        scale1_sb = osb("scale1_sb", [128, 32], F32)
        b1q_sb = osb("b1q_sb", [128, 32], F32)
        sw2_sb = osb("sw2_sb", [128, 8], F32)
        ones_sb = osb("ones_sb", [1, 128], F32)
        ymax_sb = osb("ymax_sb", [128, 2], F32)
        hmax_sb = osb("hmax_sb", [128, 2], F32)
        s1b_sb = osb("s1b_sb", [128, 1], F32)
        r1b_sb = osb("r1b_sb", [128, 1], F32)
        s2b_sb = osb("s2b_sb", [128, 1], F32)
        r2b_sb = osb("r2b_sb", [128, 1], F32)
        g1_sb = osb("g1_sb", [1, 1], F32)
        g2_sb = osb("g2_sb", [1, 1], F32)
        s1v_sb = osb("s1v_sb", [1, 1], F32)
        s2v_sb = osb("s2v_sb", [1, 1], F32)
        redt_sb = osb("redt_sb", [1, 128], F32)
        dbg_sb = osb("dbg_sb", [128, 64], F32)
        cc_sb = osb("cc_sb", [128, 4], F32)

        pLD = sem("pLD"); pW = sem("pW"); pWF = sem("pWF"); pQS = sem("pQS")
        pST = sem("pST"); pSLF = sem("pSLF"); pMM = sem("pMM")
        pLDg = sem("pLDg"); r1T = sem("r1T")
        pW0 = sem("pW0"); pW1 = sem("pW1")
        bHS0 = sem("bHS0"); bHS1 = sem("bHS1"); bHS2 = sem("bHS2"); bHS3 = sem("bHS3")
        aXL0 = sem("aXL0"); aXL1 = sem("aXL1"); aX20 = sem("aX20")
        aX21 = sem("aX21"); aLN = sem("aLN"); aTR0 = sem("aTR0")
        aTR1 = sem("aTR1"); aPJ = sem("aPJ"); aQK = sem("aQK")
        aVE = sem("aVE"); aSG = sem("aSG"); aMN = sem("aMN")
        aES = sem("aES"); aEM = sem("aEM"); aO1 = sem("aO1")
        aOS0 = sem("aOS0"); aOS1 = sem("aOS1"); aY = sem("aY")
        aYS0 = sem("aYS0"); aYS1 = sem("aYS1"); aSQ1 = sem("aSQ1")
        aSQ2 = sem("aSQ2"); aRSTD1 = sem("aRSTD1"); aRSTD2 = sem("aRSTD2")
        aVAR1 = sem("aVAR1"); aVAR2 = sem("aVAR2"); aXLF = sem("aXLF")
        aX2F = sem("aX2F"); aSLF = sem("aSLF")
        r1S = sem("r1S"); r1C = sem("r1C"); r1D = sem("r1D"); r1P = sem("r1P")
        r1Cc = sem("r1Cc"); r2Cc = sem("r2Cc")
        bYL = sem("bYL"); bYQ = sem("bYQ"); bXQ = sem("bXQ"); bXF = sem("bXF")
        bHM = sem("bHM"); bHE = sem("bHE"); bHS = sem("bHS"); bHR = sem("bHR")
        r2S = sem("r2S"); r2C = sem("r2C"); r2D = sem("r2D"); r2P = sem("r2P")
        cHL = sem("cHL"); cHF = sem("cHF"); cOL = sem("cOL"); cQ1 = sem("cQ1")
        cQ2 = sem("cQ2"); cMM = sem("cMM"); cOE = sem("cOE"); cFO = sem("cFO")
        cOS = sem("cOS")
        vBAR = sem("vBAR"); sBAR = sem("sBAR")

        _vb = [0]
        _sb_ = [0]

        def vbar(ins=None):
            if ins is None:
                ins = nc.vector.engine_nop()
            _vb[0] += 1
            ins.then_inc(vBAR, 1)
            nc.vector.wait_ge(vBAR, _vb[0])

        def sbar(ins):
            _sb_[0] += 1
            ins.then_inc(sBAR, 1)
            nc.scalar.wait_ge(sBAR, _sb_[0])

        pa = outer.enter_context(ExitStack())

        def pasb(*a):
            return pa.enter_context(nc.sbuf_tensor(*a))
        wq_sb = pasb("wq_sb", [128, 8 * D], F16)
        wk_sb = pasb("wk_sb", [128, 8 * D], F16)
        wv_sb = pasb("wv_sb", [128, 8 * D], F16)
        bq_sb = pasb("bq_sb", [128, 8], F32)
        bk_sb = pasb("bk_sb", [128, 8], F32)
        mask_sb = pasb("mask_sb", [128, 128], F16)
        cnt_sb = pasb("cnt_sb", [128, 32], F32)
        eps_sb = pasb("eps_sb", [128, 1], F32)
        bv_bc = pasb("bv_bc", [128, D], F16)

        with ExitStack() as pstack:
            def psb(*a):
                return pstack.enter_context(nc.sbuf_tensor(*a))
            wbuf = [psb(f"wbuf{i}", [128, D], F32) for i in range(2)]
            w2buf = [psb(f"w2buf{i}", [128, FF], F32) for i in range(2)]
            i8b = psb("i8b", [128, D], I8)
            i8b2 = psb("i8b2", [128, FF], I8)
            w1qf = [psb(f"w1qf{i}", [128, D], F16) for i in range(2)]
            w2qf = [psb(f"w2qf{i}", [128, FF], F16) for i in range(2)]
            rw_sb = psb("rw_sb", [128, 1], F32)
            rw_in = psb("rw_in", [128, 1], F32)
            bvrow_sb = psb("bvrow_sb", [1, D], F32)
            psB = [pstack.enter_context(nc.psum_tensor(f"psB{i}", [128, 512], F32))
                   for i in range(2)]

            pblk = pstack.enter_context(nc.Block())

            @pblk.sync
            def _(sync):
                if SKIP_P:
                    return
                sync.dma_start(out=bq_sb[:, :], in_=bq_h[:, :]).then_inc(pLD, 16)
                sync.dma_start(out=bk_sb[:, :], in_=bk_h[:, :]).then_inc(pLD, 16)
                sync.dma_start(out=b1_sb[:, :], in_=b1_h[:, :]).then_inc(pLD, 16)
                sync.dma_start(out=mask_sb[:, :], in_=mask_h[:, :]).then_inc(pLD, 16)
                sync.dma_start(out=cnt_sb[:, :], in_=cnt_h[:, :]).then_inc(pLD, 16)
                sync.dma_start(out=ones_sb[:, :], in_=ones_h[:, :]).then_inc(pLD, 16)
                sync.dma_start(out=bvrow_sb[:, :], in_=bv_h[:, :]).then_inc(pLD, 16)
                for it in range(42):
                    if it < 40:
                        if it >= 2:
                            sync.wait_ge(pWF, it - 1)
                        if it < 32:
                            sync.dma_start(out=wbuf[it % 2][:, :],
                                           in_=w1_h[it * 128:(it + 1) * 128, :]
                                           ).then_inc(pW1 if it % 2 else pW0, 16)
                        else:
                            dc = it - 32
                            sync.dma_start(out=w2buf[dc % 2][:, :],
                                           in_=w2_h[dc * 128:(dc + 1) * 128, :]
                                           ).then_inc(pW1 if it % 2 else pW0, 16)
                    j = it - 2
                    if j >= 0:
                        sync.wait_ge(pQS, j + 1)
                        if j < 32:
                            sync.dma_start(out=w1q_h[j * 128:(j + 1) * 128, :],
                                           in_=w1qf[j % 2][:, :]).then_inc(pST, 16)
                        else:
                            dc = j - 32
                            sync.dma_start(out=w2q_h[dc * 128:(dc + 1) * 128, :],
                                           in_=w2qf[dc % 2][:, :]).then_inc(pST, 16)
                            sync.dma_start(out=sw2d_h[0:1, dc * 128:(dc + 1) * 128],
                                           in_=sw2_sb[:, dc:dc + 1]).then_inc(pST, 16)

            @pblk.gpsimd
            def _(gpsimd):
                if SKIP_P:
                    return
                for wh, wsb in ((wq_h, wq_sb), (wk_h, wk_sb), (wv_h, wv_sb)):
                    for c in range(8):
                        gpsimd.dma_start(
                            out=wsb[:, c * D:(c + 1) * D],
                            in_=wh[c * 128:(c + 1) * 128, :]).then_inc(pLDg, 16)
                gpsimd.wait_ge(pLDg, 16 * 24)

            @pblk.tensor
            def _(tensor):
                if SKIP_P:
                    return
                tensor.wait_ge(pLD, 16 * 7)
                tensor.matmul(psB[0][:, :], ones_sb[:, :], bvrow_sb[0:1, 0:512],
                              start=True, stop=True).then_inc(pMM, 1)
                tensor.matmul(psB[1][:, :], ones_sb[:, :], bvrow_sb[0:1, 512:1024],
                              start=True, stop=True).then_inc(pMM, 1)

            @pblk.vector
            def _(vector):
                if SKIP_P:
                    return
                vector.memset(eps_sb[:, :], EPS_LN)
                vector.memset(ymax_sb[:, :], 0.0)
                vector.memset(hmax_sb[:, :], 0.0)
                vector.wait_ge(pMM, 2)
                vector.tensor_copy(bv_bc[:, 0:512], psB[0][:, :])
                vector.tensor_copy(bv_bc[:, 512:1024], psB[1][:, :])
                selfc = 0
                for i in range(40):
                    is1 = i < 32
                    buf = wbuf[i % 2] if is1 else w2buf[(i - 32) % 2]
                    i8t = i8b if is1 else i8b2
                    qf = w1qf[i % 2] if is1 else w2qf[(i - 32) % 2]
                    scol = (sw1_sb[:, i:i + 1] if is1
                            else sw2_sb[:, i - 32:i - 31])
                    vector.wait_ge(pW1 if i % 2 else pW0, 16 * (i // 2 + 1))
                    vector.reduce_max(scol, buf[:, :], AX, apply_absolute_value=True)
                    vbar()
                    selfc += 1
                    vector.reciprocal(rw_sb[:, :], scol).then_inc(pSLF, 1)
                    nc.vector.wait_ge(pSLF, selfc)
                    vector.tensor_scalar(i8t[:, :], buf[:, :], rw_sb[:, :],
                                         127.0, ALU.mult,
                                         ALU.mult).then_inc(pWF, 1)
                    nc.vector.wait_ge(pWF, i + 1)
                    if DEBUG_TAPS:
                        vector.tensor_copy(dbg_sb[:, i:i + 1], rw_sb[:, :])
                        if i < 8:
                            vbar(vector.tensor_copy(dbg_sb[:, 40 + i:41 + i],
                                                    scol))
                            vbar(vector.tensor_copy(dbg_sb[:, 48 + i:49 + i],
                                                    buf[:, 0:1]))
                    if i >= 2:
                        j = i - 1
                        pst_need = 16 * (j + 1) if j < 32 else 512 + 32 * (j - 31)
                        vector.wait_ge(pST, pst_need)
                    vector.tensor_copy(qf[:, :], i8t[:, :]).then_inc(pQS, 1)

        if BUILD_PHASES < 2:
            nc.finalize()
            return nc
        with ExitStack() as astack:
            if SKIP_A:
                pass
            def asb(*a):
                return astack.enter_context(nc.sbuf_tensor(*a))
            K_sb = asb("K_sb", [128, 8 * S], F16)
            xa = [asb(f"xa{i}", [128, D], F32) for i in range(2)]
            x2b = [asb(f"x2b{i}", [128, D], F32) for i in range(2)]
            xn_tm = asb("xn_tm", [128, 4 * D], F16)
            xn_fm = [asb(f"xn_fm{i}", [128, 8 * 512], F16) for i in range(2)]
            q_fm = asb("q_fm", [128, 8 * 512], F16)
            v_sb = asb("v_sb", [128, 4 * D], F16)
            out1b = [asb(f"out1b{i}", [128, D], F16) for i in range(2)]
            ybuf = [asb(f"ybuf{i}", [128, D], F16) for i in range(2)]
            e_dump = asb("e_dump", [128, D], F16)
            masked_sb = asb("masked_sb", [128, 128], F32)
            msum_sb = asb("msum_sb", [128, 2], F32)
            m1_sb = asb("m1_sb", [128, 2], F32)
            mm_sb = asb("mm_sb", [128, 2], F32)
            v1_sb = asb("v1_sb", [128, 2], F32)
            sqacc_sb = asb("sqacc_sb", [128, 2], F32)
            lnt1_sb = asb("lnt1_sb", [128, 2], F32)
            rstd_sb = asb("rstd_sb", [128, 2], F32)
            msum2_sb = asb("msum2_sb", [128, 2], F32)
            m2_sb = asb("m2_sb", [128, 2], F32)
            mm2_sb = asb("mm2_sb", [128, 2], F32)
            v2_sb = asb("v2_sb", [128, 2], F32)
            sqacc2_sb = asb("sqacc2_sb", [128, 2], F32)
            lnt2_sb = asb("lnt2_sb", [128, 2], F32)
            rstd2_sb = asb("rstd2_sb", [128, 2], F32)
            mrun_sb = [asb(f"mrun{i}", [128, 1], F32) for i in range(2)]
            mneg_sb = [asb(f"mneg{i}", [128, 1], F32) for i in range(2)]
            delta_sb = [asb(f"delta{i}", [128, 1], F32) for i in range(2)]
            srun_sb = [asb(f"srun{i}", [128, 1], F32) for i in range(2)]
            esum_sb = [asb(f"esum{i}", [128, 1], F32) for i in range(2)]
            resc_sb = [asb(f"resc{i}", [128, 1], F32) for i in range(2)]
            gm_sb = asb("gm_sb", [128, 2], F32)
            den_sb = asb("den_sb", [128, 1], F32)
            rden_sb = asb("rden_sb", [128, 1], F32)
            rs_sb = asb("rs_sb", [128, 1], F32)
            ymt_sb = asb("ymt_sb", [128, 1], F32)
            ps = [astack.enter_context(
                nc.psum_tensor(f"psA{i}", [128, 512], F32)) for i in range(8)]

            ablk = astack.enter_context(nc.Block())

            NSUB = 4 * NST

            def xl_sem(k):
                return aXL0 if k % 2 == 0 else aXL1

            def x2_sem(k):
                return aX20 if k % 2 == 0 else aX21

            def os_sem(k):
                return aOS0 if k % 2 == 0 else aOS1

            def ys_sem(k):
                return aYS0 if k % 2 == 0 else aYS1

            def tr_sem(stx):
                return aTR0 if stx % 2 == 0 else aTR1

            @ablk.sync
            def _(sync):
                if SKIP_A:
                    return
                for j in range(2):
                    sync.dma_start(out=xa[j][:, :],
                                   in_=x_h[j * 128:(j + 1) * 128, :]
                                   ).then_inc(xl_sem(j), 16)
                    sync.dma_start(out=x2b[j][:, :],
                                   in_=x_h[j * 128:(j + 1) * 128, :]
                                   ).then_inc(x2_sem(j), 16)
                for j in range(2, 8):
                    sync.wait_ge(aXLF, j - 1)
                    sync.wait_ge(aSQ1, j - 1)
                    sync.dma_start(out=xa[j % 2][:, :],
                                   in_=x_h[j * 128:(j + 1) * 128, :]
                                   ).then_inc(xl_sem(j), 16)
                for p in range(NSUB):
                    j2 = p + 2
                    if j2 < NSUB:
                        sync.wait_ge(aX2F, p + 1)
                        sync.dma_start(out=x2b[j2 % 2][:, :],
                                       in_=x_h[j2 * 128:(j2 + 1) * 128, :]
                                       ).then_inc(x2_sem(j2), 16)
                    ja = p + 8
                    if ja < NSUB:
                        sync.wait_ge(aXLF, ja - 1)
                        sync.wait_ge(aSQ1, ja - 1)
                        sync.dma_start(out=xa[ja % 2][:, :],
                                       in_=x_h[ja * 128:(ja + 1) * 128, :]
                                       ).then_inc(xl_sem(ja), 16)

            @ablk.gpsimd
            def _(gpsimd):
                if SKIP_A:
                    return

            @ablk.scalar
            def _(scalar):
                if SKIP_A:
                    return

                def fe_sc(kN):
                    p = kN % 2
                    stN = kN // 4
                    ttn = kN % 4
                    scalar.wait_ge(xl_sem(kN), 16 * (kN // 2 + 1))
                    scalar.activation(e_dump[:, 0:D], xa[p][:, :], ACT.Square,
                                      accum_out=sqacc_sb[:, p:p + 1]
                                      ).then_inc(aSQ1, 1)
                    scalar.wait_ge(aVAR1, kN + 1)
                    sbar(scalar.activation(lnt1_sb[:, p:p + 1],
                                           v1_sb[:, p:p + 1], ACT.Ln,
                                           bias=eps_sb[:, :], scale=1.0))
                    scalar.activation(rstd_sb[:, p:p + 1], lnt1_sb[:, p:p + 1],
                                      ACT.Exp, scale=-0.5).then_inc(aRSTD1, 1)
                    scalar.wait_ge(aLN, kN + 1)
                    if stN >= 2:
                        scalar.wait_ge(aPJ, 24 * (stN - 1))
                    buf = xn_fm[stN % 2]
                    for c in range(8):
                        scalar.dma_start_transpose(
                            buf[:, c * 512 + ttn * 128: c * 512 + (ttn + 1) * 128],
                            xn_tm[:, ttn * D + c * 128: ttn * D + (c + 1) * 128]
                        ).then_inc(tr_sem(stN), 16)

                def tail_sc(j):
                    jp = j % 2
                    scalar.wait_ge(aO1, j + 1)
                    scalar.activation(e_dump[:, 0:D], out1b[jp][:, :],
                                      ACT.Square,
                                      accum_out=sqacc2_sb[:, jp:jp + 1]
                                      ).then_inc(aSQ2, 1)
                    scalar.dma_start(out=o16_h[j * 128:(j + 1) * 128, :],
                                     in_=out1b[jp][:, :]
                                     ).then_inc(os_sem(j), 16)
                    scalar.wait_ge(aVAR2, j + 1)
                    sbar(scalar.activation(lnt2_sb[:, jp:jp + 1],
                                           v2_sb[:, jp:jp + 1], ACT.Ln,
                                           bias=eps_sb[:, :], scale=1.0))
                    scalar.activation(rstd2_sb[:, jp:jp + 1],
                                      lnt2_sb[:, jp:jp + 1], ACT.Exp,
                                      scale=-0.5).then_inc(aRSTD2, 1)
                    if j >= 1:
                        scalar.wait_ge(aY, j)
                        scalar.dma_start(
                            out=y16_h[(j - 1) * 128:j * 128, :],
                            in_=ybuf[(j - 1) % 2][:, :]
                        ).then_inc(ys_sem(j - 1), 16)
                    if j + 7 < NSUB:
                        fe_sc(j + 7)

                for j in range(7):
                    fe_sc(j)
                for st in range(NST):
                    sl = st % SPB
                    grp_by_tt = {}
                    for rec in SGROUPS[st]:
                        grp_by_tt.setdefault(rec[0], []).append(rec)
                    for tt in range(4):
                        k = 4 * st + tt
                        kp = k % 2
                        for (_tt, g, ng, is_diag, bank, sgidx) in grp_by_tt[tt]:
                            scalar.wait_ge(aMN, sgidx + 1)
                            par = g % 2
                            scalar.activation(resc_sb[par][:, :],
                                              delta_sb[par][:, :],
                                              ACT.Exp, scale=1.0 / 32.0)
                            src = (masked_sb[:, 0:128] if is_diag
                                   else ps[bank][:, 0:ng])
                            scalar.activation(e_dump[:, 0:ng], src, ACT.Exp,
                                              scale=1.0 / 32.0,
                                              bias=mneg_sb[par][:, :],
                                              accum_out=esum_sb[par][:, :]
                                              ).then_inc(aES, 1)
                            if is_diag:
                                scalar.activation(resc_sb[(g + 1) % 2][:, :],
                                                  mneg_sb[par][:, :], ACT.Exp,
                                                  scale=1.0).then_inc(aEM, 1)
                        if k >= 1:
                            tail_sc(k - 1)
                tail_sc(NSUB - 1)
                scalar.wait_ge(aY, NSUB)
                scalar.dma_start(out=y16_h[(NSUB - 1) * 128:NSUB * 128, :],
                                 in_=ybuf[(NSUB - 1) % 2][:, :]
                                 ).then_inc(ys_sem(NSUB - 1), 16)
                scalar.wait_ge(aOS0, 16 * (NSUB // 2))
                scalar.wait_ge(aOS1, 16 * (NSUB // 2))
                scalar.wait_ge(aYS0, 16 * (NSUB // 2))
                scalar.wait_ge(aYS1, 16 * (NSUB // 2))

            @ablk.tensor
            def _(tensor):
                if SKIP_A:
                    return
                for st in range(NST):
                    sl = st % SPB
                    buf = xn_fm[st % 2]
                    tensor.wait_ge(tr_sem(st), 512 * (st // 2 + 1))
                    for u in range(24):
                        pu = 24 * st + u
                        if u < 4:
                            if st >= 1 and u in LASTUSE[st - 1]:
                                tensor.wait_ge(aES, LASTUSE[st - 1][u] + 1)
                        elif pu >= 4:
                            prev = pu - 4
                            pm = prev % 24
                            if pm < 16:
                                tensor.wait_ge(aQK, 16 * (prev // 24) + pm + 1)
                            else:
                                tensor.wait_ge(aVE, 8 * (prev // 24) + pm - 16 + 1)
                        bank = ps[u % 4]
                        if u < 8:
                            for c in range(8):
                                ins = tensor.matmul(
                                    bank[:, :],
                                    wq_sb[:, c * D + u * 128: c * D + (u + 1) * 128],
                                    buf[:, c * 512:(c + 1) * 512],
                                    start=(c == 0), stop=(c == 7))
                        elif u < 16:
                            co = u - 8
                            for c in range(8):
                                ins = tensor.matmul(
                                    bank[:, :],
                                    wk_sb[:, c * D + co * 128: c * D + (co + 1) * 128],
                                    buf[:, c * 512:(c + 1) * 512],
                                    start=(c == 0), stop=(c == 7))
                        else:
                            tt = (u - 16) // 2
                            half = (u - 16) % 2
                            for c in range(8):
                                ins = tensor.matmul(
                                    bank[:, :],
                                    buf[:, c * 512 + tt * 128: c * 512 + (tt + 1) * 128],
                                    wv_sb[:, c * D + half * 512: c * D + half * 512 + 512],
                                    start=(c == 0), stop=(c == 7))
                        ins.then_inc(aPJ, 1)
                    tensor.wait_ge(aQK, 16 * (st + 1))
                    seen = set()
                    for (tt, g, ng, is_diag, bank, sgidx) in SGROUPS[st]:
                        qt = sl * 4 + tt
                        if bank < 4 and bank not in seen:
                            seen.add(bank)
                            tensor.wait_ge(aVE, 8 * st + bank + 5)
                        if sgidx >= 8:
                            tensor.wait_ge(aES, sgidx - 7)
                        k0 = qt * 128 if is_diag else g * 512
                        for c in range(8):
                            ins = tensor.matmul(
                                ps[bank][:, 0:ng],
                                q_fm[:, c * 512 + tt * 128: c * 512 + (tt + 1) * 128],
                                K_sb[:, c * S + k0: c * S + k0 + ng],
                                start=(c == 0), stop=(c == 7))
                        ins.then_inc(aSG, 1)

            @ablk.vector
            def _(vector):
                if SKIP_A:
                    return
                slf = [0]

                def self_wait(ins):
                    slf[0] += 1
                    ins.then_inc(aSLF, 1)
                    nc.vector.wait_ge(aSLF, slf[0])

                def fe_dve(kN):
                    p = kN % 2
                    vector.wait_ge(xl_sem(kN), 16 * (kN // 2 + 1))
                    vector.reduce_sum(msum_sb[:, p:p + 1], xa[p][:, :], AX)
                    vbar()
                    vbar(vector.tensor_scalar(m1_sb[:, p:p + 1],
                                              msum_sb[:, p:p + 1],
                                              1.0 / D, None, ALU.mult))
                    vector.wait_ge(aSQ1, kN + 1)
                    vbar(vector.tensor_mul(mm_sb[:, p:p + 1], m1_sb[:, p:p + 1],
                                           m1_sb[:, p:p + 1]))
                    vector.scalar_tensor_tensor(
                        v1_sb[:, p:p + 1], sqacc_sb[:, p:p + 1], 1.0 / D,
                        mm_sb[:, p:p + 1], ALU.mult,
                        ALU.subtract).then_inc(aVAR1, 1)
                    nc.vector.wait_ge(aVAR1, kN + 1)
                    vector.wait_ge(aRSTD1, kN + 1)
                    if kN >= 4:
                        stp = (kN - 4) // 4
                        vector.wait_ge(tr_sem(stp), 512 * (stp // 2 + 1))
                    ttn = kN % 4
                    vector.tensor_scalar(xn_tm[:, ttn * D:(ttn + 1) * D],
                                         xa[p][:, :], m1_sb[:, p:p + 1],
                                         rstd_sb[:, p:p + 1],
                                         ALU.subtract, ALU.mult)
                    nc.vector.engine_nop().then_inc(aLN, 1)
                    nc.vector.engine_nop().then_inc(aXLF, 1)

                def tail_dve(j):
                    jp = j % 2
                    vector.reduce_sum(msum2_sb[:, jp:jp + 1],
                                      out1b[jp][:, :], AX)
                    vbar()
                    vbar(vector.tensor_scalar(m2_sb[:, jp:jp + 1],
                                              msum2_sb[:, jp:jp + 1],
                                              1.0 / D, None, ALU.mult))
                    vector.wait_ge(aSQ2, j + 1)
                    vbar(vector.tensor_mul(mm2_sb[:, jp:jp + 1],
                                           m2_sb[:, jp:jp + 1],
                                           m2_sb[:, jp:jp + 1]))
                    vector.scalar_tensor_tensor(
                        v2_sb[:, jp:jp + 1], sqacc2_sb[:, jp:jp + 1],
                        1.0 / D, mm2_sb[:, jp:jp + 1], ALU.mult,
                        ALU.subtract).then_inc(aVAR2, 1)
                    nc.vector.wait_ge(aVAR2, j + 1)
                    if j + 7 < NSUB:
                        fe_dve(j + 7)
                    vector.wait_ge(aRSTD2, j + 1)
                    if j >= 2:
                        vector.wait_ge(ys_sem(j), 16 * (j // 2))
                    vector.tensor_scalar(ybuf[jp][:, :], out1b[jp][:, :],
                                         m2_sb[:, jp:jp + 1],
                                         rstd2_sb[:, jp:jp + 1],
                                         ALU.subtract, ALU.mult)
                    nc.vector.engine_nop().then_inc(aY, 1)
                    vbar()
                    vector.reduce_max(ymt_sb[:, :], ybuf[jp][:, :], AX)
                    vbar()
                    vbar(vector.tensor_max(
                        ymax_sb[:, (j + 1) % 2:(j + 1) % 2 + 1],
                        ymax_sb[:, j % 2:j % 2 + 1], ymt_sb[:, :]))

                def qk_evacs(stq):
                    slq = stq % SPB
                    for u in range(16):
                        vector.wait_ge(aPJ, 24 * stq + u + 1)
                        bank = ps[u % 4]
                        if u < 8:
                            dest = q_fm[:, u * 512:(u + 1) * 512]
                            bias = bq_sb[:, u:u + 1]
                        else:
                            c = u - 8
                            dest = K_sb[:, c * S + slq * 512:
                                        c * S + (slq + 1) * 512]
                            bias = bk_sb[:, c:c + 1]
                        vector.tensor_scalar(dest, bank[:, :], bias, None,
                                             ALU.add).then_inc(aQK, 1)

                for j in range(7):
                    fe_dve(j)
                qk_evacs(0)
                for st in range(NST):
                    sl = st % SPB
                    for u in range(8):
                        vector.wait_ge(aPJ, 24 * st + 16 + u + 1)
                        bank = ps[(16 + u) % 4]
                        tt = u // 2
                        half = u % 2
                        dst = v_sb[:, tt * D + half * 512: tt * D + half * 512 + 512]
                        vector.tensor_add(dst, bank[:, :],
                                          bv_bc[:, half * 512: half * 512 + 512]
                                          ).then_inc(aVE, 1)
                    grp_by_tt = {}
                    for rec in SGROUPS[st]:
                        grp_by_tt.setdefault(rec[0], []).append(rec)
                    for tt in range(4):
                        k = 4 * st + tt
                        kp = k % 2
                        qt = sl * 4 + tt
                        recs = grp_by_tt[tt]
                        vector.memset(mrun_sb[0][:, :], 0.0)
                        vector.memset(srun_sb[0][:, :], 0.0)

                        def red(i):
                            (_t, _g, ngi, dia, bki, sgi) = recs[i][0], recs[i][1], recs[i][2], recs[i][3], recs[i][4], recs[i][5]
                            vector.wait_ge(aSG, sgi + 1)
                            if dia:
                                vbar(vector.tensor_add(masked_sb[:, :],
                                                       ps[bki][:, 0:128],
                                                       mask_sb[:, :]))
                                srci = masked_sb[:, :]
                            else:
                                srci = ps[bki][:, 0:ngi]
                            vector.reduce_max(gm_sb[:, i % 2:i % 2 + 1], srci, AX)

                        red(0)
                        for i, (_tt, g, ng, is_diag, bank, sgidx) in enumerate(recs):
                            par = g % 2
                            nxt = (g + 1) % 2
                            if i + 1 < len(recs):
                                red(i + 1)
                            vbar()
                            vbar(vector.tensor_max(mrun_sb[nxt][:, :],
                                                   mrun_sb[par][:, :],
                                                   gm_sb[:, i % 2:i % 2 + 1]))
                            vector.tensor_sub(delta_sb[par][:, :],
                                              mrun_sb[par][:, :],
                                              mrun_sb[nxt][:, :])
                            vector.tensor_scalar(mneg_sb[par][:, :],
                                                 mrun_sb[nxt][:, :], -1.0 / 32.0,
                                                 None, ALU.mult).then_inc(aMN, 1)
                            vector.wait_ge(aES, sgidx + 1)
                            vector.scalar_tensor_tensor(
                                srun_sb[nxt][:, :], srun_sb[par][:, :],
                                resc_sb[par][:, :], esum_sb[par][:, :],
                                ALU.mult, ALU.add)
                        if tt == 3 and st + 1 < NST:
                            qk_evacs(st + 1)
                        lastpar = len(recs) % 2
                        vector.wait_ge(aEM, 4 * st + tt + 1)
                        vbar(vector.scalar_tensor_tensor(
                            den_sb[:, :], resc_sb[lastpar][:, :],
                            cnt_sb[:, qt:qt + 1], srun_sb[lastpar][:, :],
                            ALU.mult, ALU.add))
                        vbar(vector.reciprocal(rden_sb[:, :], den_sb[:, :]))
                        self_wait(vector.tensor_mul(rs_sb[:, :], den_sb[:, :],
                                                    rden_sb[:, :]))
                        vector.wait_ge(x2_sem(k), 16 * (k // 2 + 1))
                        if k >= 2:
                            vector.wait_ge(os_sem(k), 16 * (k // 2))
                        vector.scalar_tensor_tensor(
                            out1b[kp][:, :],
                            v_sb[:, tt * D:(tt + 1) * D], rs_sb[:, :],
                            x2b[kp][:, :], ALU.mult, ALU.add)
                        nc.vector.engine_nop().then_inc(aO1, 1)
                        nc.vector.engine_nop().then_inc(aX2F, 1)
                        vbar()
                        if k >= 1:
                            tail_dve(k - 1)
                tail_dve(NSUB - 1)

        pa.close()
        if BUILD_PHASES < 3:
            nc.finalize()
            return nc

        with ExitStack() as ccom:
            def csb(*a):
                return ccom.enter_context(nc.sbuf_tensor(*a))
            w2qT_sb = csb("w2qT_sb", [128, 32 * D], F16)
            prstack = ExitStack()
            psR = [prstack.enter_context(nc.psum_tensor(f"psR{i}", [128, 512], F32))
                   for i in range(2)]

            with ExitStack() as bstack:
                def bsb(*a):
                    return bstack.enter_context(nc.sbuf_tensor(*a))
                w1qT_sb = bsb("w1qT_sb", [128, 8 * FF], F16)
                y_fm = [bsb(f"y_fm{i}", [128, 8 * 512], F16) for i in range(2)]
                xq_sb = [bsb(f"xq{i}", [128, 8 * 512], F16) for i in range(2)]
                yq_u8 = bsb("yq_u8", [128, 512], U8)
                h_tile = [bsb(f"h_tile{i}", [128, 512], F16) for i in range(4)]
                hmc_sb = bsb("hmc_sb", [128, 32], F32)
                hmt_sb = bsb("hmt_sb", [128, 1], F32)
                b1i_sb = bsb("b1i_sb", [128, 32], I32)
                b1f_sb = bsb("b1f_sb", [128, 32], F32)
                rcs1_sb = bsb("rcs1_sb", [128, 32], F32)
                ps_b = [bstack.enter_context(
                    nc.psum_tensor(f"psHB{i}", [128, 512], F32)) for i in range(4)]

                ar1blk = nc.Block()
                ar1blk.__enter__()

                @ar1blk.sync
                def _(sync):
                    if SKIP_AR1:
                        return
                    ymfin = ymax_sb[:, NST % 2:NST % 2 + 1]
                    sync.dma_start(out=red1_h[:, :], in_=ymfin).then_inc(r1S, 16)
                    sync.wait_ge(r1S, 16)
                    sync.dma_start(out=redt_sb[:, :],
                                   in_=red1_h[:, :]).then_inc(r1S, 16)
                    sync.wait_ge(r1D, 1)
                    sync.dma_start(out=cc1i_h[:, :],
                                   in_=cc_sb[:, :]).then_inc(r1S, 16)
                    sync.wait_ge(r1Cc, 1)
                    sync.dma_start(out=g1_sb[:, :],
                                   in_=cc1o_h[0:1, 0:1]).then_inc(r1C, 16)

                @ar1blk.vector
                def _(vector):
                    if SKIP_AR1:
                        return
                    vector.wait_ge(r1S, 32)
                    vector.reduce_max(g1_sb[:, :], redt_sb[:, :], AX)
                    vbar()
                    vbar(vector.memset(cc_sb[:, :], 0.0))
                    vector.tensor_copy(cc_sb[0:1, 0:1], g1_sb[:, :]).then_inc(r1D, 1)
                    vector.wait_ge(r1C, 16)
                    vector.tensor_scalar(s1v_sb[:, :], g1_sb[:, :], EPS_Q,
                                         1.0 / 255.0, ALU.max,
                                         ALU.mult).then_inc(r1D, 1)
                    vector.wait_ge(r1P, 1)
                    vector.tensor_copy(s1b_sb[:, :],
                                       psR[0][:, 0:1]).then_inc(r1D, 1)
                    nc.vector.wait_ge(r1D, 3)
                    vector.reciprocal(r1b_sb[:, :], s1b_sb[:, :])
                    vector.tensor_scalar(scale1_sb[:, :], sw1_sb[:, :],
                                         s1b_sb[:, :], 1.0 / 127.0, ALU.mult,
                                         ALU.mult).then_inc(r1D, 1)
                    nc.vector.wait_ge(r1D, 4)
                    vbar(vector.reciprocal(rcs1_sb[:, :], scale1_sb[:, :]))
                    vbar(vector.tensor_mul(b1f_sb[:, :], b1_sb[:, :], rcs1_sb[:, :]))
                    vbar(vector.tensor_copy(b1i_sb[:, :], b1f_sb[:, :]))
                    vbar(vector.tensor_copy(b1f_sb[:, :], b1i_sb[:, :]))
                    vbar(vector.tensor_mul(b1q_sb[:, :], b1f_sb[:, :],
                                           scale1_sb[:, :]))

                @ar1blk.gpsimd
                def _(gpsimd):
                    if SKIP_AR1:
                        return
                    gpsimd.wait_ge(r1S, 48)
                    gpsimd.collective_compute(
                        "AllReduce", ALU.max,
                        replica_groups=[list(range(NCORES))],
                        ins=[cc1i_h.ap().opt()],
                        outs=[cc1o_h.ap().opt()]).then_inc(r1Cc, 1)

                @ar1blk.tensor
                def _(tensor):
                    if SKIP_AR1:
                        return
                    tensor.wait_ge(r1D, 2)
                    tensor.matmul(psR[0][:, 0:1], ones_sb[:, :], s1v_sb[:, :],
                                  start=True, stop=True).then_inc(r1P, 1)

                @ar1blk.scalar
                def _(scalar):
                    if SKIP_AR1:
                        return
                    for dc in range(8):
                        scalar.dma_start_transpose(
                            w1qT_sb[:, dc * FF:(dc + 1) * FF],
                            w1q_h[0:FF, dc * 128:(dc + 1) * 128]).then_inc(r1T, 16)
                    for fc in range(32):
                        scalar.dma_start_transpose(
                            w2qT_sb[:, fc * D:(fc + 1) * D],
                            w2q_h[0:D, fc * 128:(fc + 1) * 128]).then_inc(r1T, 16)
                    for g in range(2):
                        for c in range(8):
                            scalar.dma_start_transpose(
                                y_fm[g][:, c * 512:(c + 1) * 512],
                                y16_h[g * 512:(g + 1) * 512,
                                      c * 128:(c + 1) * 128]).then_inc(bYL, 16)
                    scalar.wait_ge(r1T, 16 * 40)

                ar1blk.__exit__(None, None, None)
                if BUILD_PHASES < 4:
                    nc.finalize()
                    return nc
                bblk = bstack.enter_context(nc.Block())

                @bblk.sync
                def _(sync):
                    for g in range(2, 16):
                        sync.wait_ge(bYQ, g - 1)
                        for c in range(8):
                            sync.dma_start_transpose(
                                y_fm[g % 2][:, c * 512:(c + 1) * 512],
                                y16_h[g * 512:(g + 1) * 512,
                                      c * 128:(c + 1) * 128]).then_inc(bYL, 16)

                @bblk.vector
                def _(vector):
                    def hmax_group(g):
                        for fc in range(32):
                            u = 32 * g + fc
                            nc.vector.wait_ge(bHE, u + 1)
                            nc.vector.reduce_max(hmc_sb[:, fc:fc + 1],
                                                 h_tile[fc % 4][:, :], AX)
                            nc.vector.engine_nop().then_inc(bHR, 1)
                        nc.vector.wait_ge(bHR, 32 * (g + 1))
                        nc.vector.reduce_max(hmt_sb[:, :], hmc_sb[:, :], AX)
                        vbar()
                        nc.vector.tensor_max(
                            hmax_sb[:, (g + 1) % 2:(g + 1) % 2 + 1],
                            hmax_sb[:, g % 2:g % 2 + 1], hmt_sb[:, :])

                    for g in range(16):
                        vector.wait_ge(bYL, 128 * (g + 1))
                        if g >= 2 and B_STAGES >= 2:
                            vector.wait_ge(bHM, 32 * (g - 1))
                        for c in range(8):
                            sl_ = slice(c * 512, (c + 1) * 512)
                            vbar(vector.tensor_scalar(yq_u8[:, :],
                                                      y_fm[g % 2][:, sl_],
                                                      r1b_sb[:, :], None,
                                                      ALU.mult))
                            vector.tensor_copy(xq_sb[g % 2][:, sl_], yq_u8[:, :])
                        nc.vector.engine_nop().then_inc(bYQ, 1)
                        nc.vector.engine_nop().then_inc(bXQ, 1)
                        if g >= 1 and B_STAGES >= 3 and B_HMAX:
                            hmax_group(g - 1)
                    if B_STAGES >= 3 and B_HMAX:
                        hmax_group(15)

                @bblk.tensor
                def _(tensor):
                    if B_STAGES < 2:
                        return
                    for g in range(16):
                        tensor.wait_ge(bXQ, g + 1)
                        for fc in range(32):
                            u = 32 * g + fc
                            if u >= 4 and B_STAGES >= 3:
                                tensor.wait_ge(bHE, u - 3)
                            for dc in range(8):
                                ins = tensor.matmul(
                                    ps_b[fc % 4][:, :],
                                    w1qT_sb[:, dc * FF + fc * 128:
                                            dc * FF + (fc + 1) * 128],
                                    xq_sb[g % 2][:, dc * 512:(dc + 1) * 512],
                                    start=(dc == 0), stop=(dc == 7))
                            ins.then_inc(bHM, 1)

                @bblk.scalar
                def _(scalar):
                    def evac_group(g):
                        for fc in range(32):
                            u = 32 * g + fc
                            scalar.wait_ge(bHM, u + 1)
                            if u >= 4:
                                if B_STAGES >= 4:
                                    scalar.wait_ge([bHS0, bHS1, bHS2, bHS3][u % 4],
                                                   16 * (u // 4))
                                if B_HMAX:
                                    scalar.wait_ge(bHR, u - 3)
                            if B_SIMPLE_EVAC:
                                scalar.copy(h_tile[fc % 4][:, :],
                                            ps_b[fc % 4][:, :]).then_inc(bHE, 1)
                            else:
                                scalar.activation(h_tile[fc % 4][:, :],
                                                  ps_b[fc % 4][:, :], ACT.Relu,
                                                  bias=b1q_sb[:, fc:fc + 1],
                                                  scale=scale1_sb[:, fc:fc + 1]
                                                  ).then_inc(bHE, 1)
                            if B_STAGES >= 4:
                                scalar.wait_ge(bHE, u + 1)
                                scalar.dma_start(out=h16_h[fc, g],
                                                 in_=h_tile[fc % 4][:, :]
                                                 ).then_inc([bHS0, bHS1, bHS2, bHS3][u % 4], 16)

                    for g in range(16):
                        if B_STAGES >= 3 and g >= 1:
                            evac_group(g - 1)
                    if B_STAGES >= 3:
                        evac_group(15)
                    if B_STAGES >= 4:
                        for bb in range(4):
                            scalar.wait_ge([bHS0, bHS1, bHS2, bHS3][bb], 16 * 8 * 16)

            if BUILD_PHASES < 5:
                nc.finalize()
                return nc
            with ExitStack() as cstack:
                def c2sb(*a):
                    return cstack.enter_context(nc.sbuf_tensor(*a))
                SCALE2B = c2sb("SCALE2B", [128, D], F32)
                B2QB = c2sb("B2QB", [128, D], F32)

                arstack = cstack.enter_context(ExitStack())

                def arsb(*a):
                    return arstack.enter_context(nc.sbuf_tensor(*a))
                b2row_sb = arsb("b2row_sb", [1, D], F32)
                s2wrow_sb = arsb("s2wrow_sb", [1, D], F32)
                scale2row_sb = arsb("scale2row_sb", [1, D], F32)
                b2qrow_sb = arsb("b2qrow_sb", [1, D], F32)
                rc2row_sb = arsb("rc2row_sb", [1, D], F32)
                tmp1r_sb = arsb("tmp1r_sb", [1, D], F32)
                tmpi1r_sb = arsb("tmpi1r_sb", [1, D], I32)

                ar2blk = nc.Block()
                ar2blk.__enter__()

                @ar2blk.sync
                def _(sync):
                    hmfin = hmax_sb[:, 16 % 2:16 % 2 + 1]
                    sync.dma_start(out=red2_h[:, :], in_=hmfin).then_inc(r2S, 16)
                    sync.wait_ge(r2S, 16)
                    sync.dma_start(out=redt_sb[:, :],
                                   in_=red2_h[:, :]).then_inc(r2S, 16)
                    sync.dma_start(out=s2wrow_sb[:, :],
                                   in_=sw2d_h[:, :]).then_inc(r2S, 16)
                    sync.dma_start(out=b2row_sb[:, :],
                                   in_=b2_h[:, :]).then_inc(r2S, 16)
                    sync.wait_ge(r2D, 1)
                    sync.dma_start(out=cc2i_h[:, :],
                                   in_=cc_sb[:, :]).then_inc(r2S, 16)
                    sync.wait_ge(r2Cc, 1)
                    sync.dma_start(out=g2_sb[:, :],
                                   in_=cc2o_h[0:1, 0:1]).then_inc(r2C, 16)

                @ar2blk.vector
                def _(vector):
                    vector.wait_ge(r2S, 32)
                    vector.reduce_max(g2_sb[:, :], redt_sb[:, :], AX)
                    vbar()
                    vbar(vector.memset(cc_sb[:, :], 0.0))
                    vector.tensor_copy(cc_sb[0:1, 0:1], g2_sb[:, :]).then_inc(r2D, 1)
                    vector.wait_ge(r2C, 16)
                    vector.tensor_scalar(s2v_sb[:, :], g2_sb[:, :], EPS_Q,
                                         1.0 / 255.0, ALU.max,
                                         ALU.mult).then_inc(r2D, 1)
                    nc.vector.wait_ge(r2D, 2)
                    vector.wait_ge(r2S, 64)
                    vbar(vector.tensor_scalar(scale2row_sb[:, :], s2wrow_sb[:, :],
                                              s2v_sb[:, :], 1.0 / 127.0, ALU.mult,
                                              ALU.mult))
                    vbar(vector.reciprocal(rc2row_sb[:, :], scale2row_sb[:, :]))
                    vbar(vector.tensor_mul(tmp1r_sb[:, :], b2row_sb[:, :],
                                           rc2row_sb[:, :]))
                    vbar(vector.tensor_copy(tmpi1r_sb[:, :], tmp1r_sb[:, :]))
                    vbar(vector.tensor_copy(tmp1r_sb[:, :], tmpi1r_sb[:, :]))
                    vector.tensor_mul(b2qrow_sb[:, :], tmp1r_sb[:, :],
                                      scale2row_sb[:, :]).then_inc(r2D, 1)
                    vector.wait_ge(r2P, 2)
                    vector.tensor_copy(SCALE2B[:, 0:512], psR[0][:, :])
                    vector.tensor_copy(SCALE2B[:, 512:1024],
                                       psR[1][:, :]).then_inc(r2D, 1)
                    vector.wait_ge(r2P, 4)
                    vector.tensor_copy(B2QB[:, 0:512], psR[0][:, :])
                    vector.tensor_copy(B2QB[:, 512:1024],
                                       psR[1][:, :]).then_inc(r2D, 1)
                    vector.wait_ge(r2P, 5)
                    vector.tensor_copy(s2b_sb[:, :],
                                       psR[0][:, 0:1]).then_inc(r2D, 1)
                    nc.vector.wait_ge(r2D, 6)
                    vector.reciprocal(r2b_sb[:, :], s2b_sb[:, :])

                @ar2blk.gpsimd
                def _(gpsimd):
                    gpsimd.wait_ge(r2S, 80)
                    gpsimd.collective_compute(
                        "AllReduce", ALU.max,
                        replica_groups=[list(range(NCORES))],
                        ins=[cc2i_h.ap().opt()],
                        outs=[cc2o_h.ap().opt()]).then_inc(r2Cc, 1)

                @ar2blk.tensor
                def _(tensor):
                    tensor.wait_ge(r2D, 3)
                    tensor.matmul(psR[0][:, :], ones_sb[:, :],
                                  scale2row_sb[0:1, 0:512], start=True,
                                  stop=True).then_inc(r2P, 1)
                    tensor.matmul(psR[1][:, :], ones_sb[:, :],
                                  scale2row_sb[0:1, 512:1024], start=True,
                                  stop=True).then_inc(r2P, 1)
                    tensor.wait_ge(r2D, 4)
                    tensor.matmul(psR[0][:, :], ones_sb[:, :],
                                  b2qrow_sb[0:1, 0:512], start=True,
                                  stop=True).then_inc(r2P, 1)
                    tensor.matmul(psR[1][:, :], ones_sb[:, :],
                                  b2qrow_sb[0:1, 512:1024], start=True,
                                  stop=True).then_inc(r2P, 1)
                    tensor.wait_ge(r2D, 5)
                    tensor.matmul(psR[0][:, 0:1], ones_sb[:, :], s2v_sb[:, :],
                                  start=True, stop=True).then_inc(r2P, 1)

                ar2blk.__exit__(None, None, None)
                arstack.close()
                prstack.close()
                if BUILD_PHASES < 6:
                    nc.finalize()
                    return nc
                ps_c = [cstack.enter_context(
                    nc.psum_tensor(f"psOC{i}", [128, 512], F32)) for i in range(8)]
                h_ld = [c2sb(f"h_ld{i}", [128, 32 * 512], F16) for i in range(2)]
                o1_ld = [c2sb(f"o1_ld{i}", [128, 4 * D], F16) for i in range(2)]
                fout_sb = c2sb("fout_sb", [128, 4 * D], F32)
                hu8_sb = [c2sb(f"hu8{i}", [128, 512], U8) for i in range(2)]
                eb1_sb = c2sb("eb1_sb", [128, 512], F32)
                eb2_sb = c2sb("eb2_sb", [128, 512], F32)

                cblk = cstack.enter_context(nc.Block())

                @cblk.sync
                def _(sync):
                    for g in range(16):
                        if g >= 2:
                            sync.wait_ge(cMM, 4 * (g - 1))
                            sync.wait_ge(cFO, g - 1)
                        for fc in range(32):
                            sync.dma_start(
                                out=h_ld[g % 2][:, fc * 512:(fc + 1) * 512],
                                in_=h16_h[fc, g]).then_inc(cHL, 16)
                        for tt in range(4):
                            r0 = g * 512 + tt * 128
                            sync.dma_start(out=o1_ld[g % 2][:, tt * D:(tt + 1) * D],
                                           in_=o16_h[r0:r0 + 128, :]
                                           ).then_inc(cOL, 16)
                        if g >= 1:
                            sync.wait_ge(cFO, g)
                            for tt in range(4):
                                r0 = (g - 1) * 512 + tt * 128
                                sync.dma_start(out=out_h[r0:r0 + 128, :],
                                               in_=fout_sb[:, tt * D:(tt + 1) * D]
                                               ).then_inc(cOS, 16)
                    sync.wait_ge(cFO, 16)
                    for tt in range(4):
                        r0 = 15 * 512 + tt * 128
                        sync.dma_start(out=out_h[r0:r0 + 128, :],
                                       in_=fout_sb[:, tt * D:(tt + 1) * D]
                                       ).then_inc(cOS, 16)
                    sync.wait_ge(cOS, 16 * 64)
                    if DEBUG_TAPS:
                        sync.dma_start(out=dbg_h[:, :], in_=dbg_sb[:, :]).then_inc(cOS, 16)
                        sync.wait_ge(cOS, 16 * 65)

                @cblk.vector
                def _(vector):
                    def epilogue_group(g):
                        if g >= 1:
                            nc.vector.wait_ge(cOS, 64 * g)
                        nc.vector.wait_ge(cOL, 64 * min(g + 2, 16))
                        for tt in range(4):
                            nc.vector.wait_ge(cMM, 4 * g + tt + 1)
                            for half in range(2):
                                bank = ps_c[2 * tt + half]
                                sl_ = slice(half * 512, half * 512 + 512)
                                osl = slice(tt * D + half * 512,
                                            tt * D + half * 512 + 512)
                                vbar(nc.vector.tensor_mul(eb1_sb[:, :],
                                                          bank[:, :],
                                                          SCALE2B[:, sl_]))
                                vbar(nc.vector.tensor_add(eb2_sb[:, :],
                                                          eb1_sb[:, :],
                                                          B2QB[:, sl_]))
                                nc.vector.tensor_add(fout_sb[:, osl],
                                                     eb2_sb[:, :],
                                                     o1_ld[g % 2][:, osl])
                            nc.vector.engine_nop().then_inc(cOE, 1)
                        nc.vector.engine_nop().then_inc(cFO, 1)

                    for g in range(16):
                        vector.wait_ge(cHL, 512 * (g + 1))
                        for fc in range(32):
                            u = 32 * g + fc
                            if u >= 2:
                                vector.wait_ge(cQ2, u - 1)
                            vector.tensor_scalar(
                                hu8_sb[fc % 2][:, :],
                                h_ld[g % 2][:, fc * 512:(fc + 1) * 512],
                                r2b_sb[:, :], None, ALU.mult).then_inc(cQ1, 1)
                        if g >= 1:
                            epilogue_group(g - 1)
                    epilogue_group(15)

                @cblk.scalar
                def _(scalar):
                    for g in range(16):
                        for fc in range(32):
                            u = 32 * g + fc
                            scalar.wait_ge(cQ1, u + 1)
                            scalar.copy(h_ld[g % 2][:, fc * 512:(fc + 1) * 512],
                                        hu8_sb[fc % 2][:, :]).then_inc(cQ2, 1)

                @cblk.tensor
                def _(tensor):
                    for g in range(16):
                        for tt in range(4):
                            tu = 4 * g + tt
                            if tu >= 4:
                                tensor.wait_ge(cOE, tu - 3)
                            b0 = ps_c[2 * tt]
                            b1 = ps_c[2 * tt + 1]
                            for fc in range(32):
                                if tt == 0:
                                    tensor.wait_ge(cQ2, 32 * g + fc + 1)
                                lhs = h_ld[g % 2][:, fc * 512 + tt * 128:
                                                  fc * 512 + (tt + 1) * 128]
                                tensor.matmul(b0[:, :], lhs,
                                              w2qT_sb[:, fc * D: fc * D + 512],
                                              start=(fc == 0), stop=(fc == 31))
                                ins = tensor.matmul(b1[:, :], lhs,
                                                    w2qT_sb[:, fc * D + 512:
                                                            fc * D + 1024],
                                                    start=(fc == 0),
                                                    stop=(fc == 31))
                            ins.then_inc(cMM, 1)

    nc.finalize()
    return nc


def kernel(L0_attn_X, L0_attn_Wq, L0_attn_bq, L0_attn_Wk, L0_attn_bk,
           L0_attn_Wv, L0_attn_bv, L0_ffn_w1, L0_ffn_b1, L0_ffn_w2, L0_ffn_b2):
    X = np.ascontiguousarray(np.asarray(L0_attn_X, dtype=np.float32))
    wq = np.ascontiguousarray(np.asarray(L0_attn_Wq, dtype=np.float32))
    wk = np.ascontiguousarray(np.asarray(L0_attn_Wk, dtype=np.float32))
    wv = np.ascontiguousarray(np.asarray(L0_attn_Wv, dtype=np.float32))
    bq = np.asarray(L0_attn_bq, dtype=np.float32)
    bk = np.asarray(L0_attn_bk, dtype=np.float32)
    bv = np.asarray(L0_attn_bv, dtype=np.float32)
    w1 = np.ascontiguousarray(np.asarray(L0_ffn_w1, dtype=np.float32))
    b1 = np.asarray(L0_ffn_b1, dtype=np.float32)
    w2 = np.ascontiguousarray(np.asarray(L0_ffn_w2, dtype=np.float32))
    b2 = np.asarray(L0_ffn_b2, dtype=np.float32)

    bqr = np.ascontiguousarray(bq.reshape(8, 128).T)
    bkr = np.ascontiguousarray(bk.reshape(8, 128).T)
    bvr = np.ascontiguousarray(bv.reshape(1, D))
    b1r = np.ascontiguousarray(b1.reshape(32, 128).T)
    b2r = np.ascontiguousarray(b2.reshape(1, D))
    ii = np.arange(128)
    mask = np.where(ii[None, :] <= ii[:, None], 0.0, NEG).astype(np.float16)
    qts = np.arange(32)
    cnt = (S - 1 - (qts[None, :] * 128 + ii[:, None])).astype(np.float32)
    onesf = np.ones((1, 128), np.float32)

    nc = build_nc()

    in_maps = []
    for c in range(NCORES):
        xs = np.ascontiguousarray(X[2 * c:2 * c + 2].reshape(T, D))
        in_maps.append(dict(x=xs, wq=wq, wk=wk, wv=wv, bqr=bqr, bkr=bkr,
                            bvr=bvr, w1=w1, b1r=b1r, w2=w2, b2r=b2r,
                            mask=mask, cnt=cnt, onesf=onesf))
    res = run_bass_kernel_spmd(nc, in_maps, core_ids=list(range(NCORES)))
    out = np.concatenate([np.asarray(res.results[c]["out"], np.float32)
                          for c in range(NCORES)], axis=0)
    return out.reshape(16, S, D)
